# version 09
# baseline (speedup 1.0000x reference)
import contextlib
import numpy as np
import concourse.bass as bass
import concourse.mybir as mybir
from concourse.bass_utils import run_bass_kernel_spmd

F32 = mybir.dt.float32
BF16 = mybir.dt.bfloat16
AF = mybir.ActivationFunctionType
ALU = mybir.AluOpType
AX = mybir.AxisListType

T = 4096
D = 1024
MACRO = 512
NMAC = T // MACRO
EPS = 1e-5

ENG = ("pe", "dve", "act", "sp", "pool")
DEBUG = False
BYPASS_LN = False


class Prog:

    def __init__(self, nc, stack):
        self.nc = nc
        self.base = stack
        self.scope = stack
        self.eng = dict(pe=nc.tensor, dve=nc.vector, act=nc.scalar,
                        sp=nc.sync, pool=nc.gpsimd)
        self.sem = {e: stack.enter_context(nc.semaphore(f"sem_{e}"))
                    for e in ("pe", "dve", "act")}
        self.instrs = {e: [] for e in ENG}
        self.marked = {e: [] for e in ENG}
        self.last_write = {}
        self.readers = {}
        self.waited = {e: {} for e in ENG}
        self.streams = {}

    def _resolve(self, tok):
        if tok[0] == "d":
            s = self.streams[tok[1]]
            return s[0], 16 * tok[2]
        _, e, rec = tok
        if rec[2] is None:
            for m in self.marked[e]:
                if m[1] >= rec[1]:
                    return self.sem[e], m[2]
            rec[2] = len(self.marked[e]) + 1
            self.marked[e].append(rec)
            rec[0].then_inc(self.sem[e], 1)
        return self.sem[e], rec[2]

    def _wait(self, e, sem_obj, val):
        key = id(sem_obj)
        if self.waited[e].get(key, 0) >= val:
            return
        self.eng[e].wait_ge(sem_obj, val)
        self.waited[e][key] = val

    def _deps(self, e, reads, writes):
        def skip(tok):
            if tok[0] != "e" or tok[1] != e:
                return False
            return e in ("pe", "sp", "pool")
        toks = []
        for k in reads:
            lw = self.last_write.get(k)
            if lw is not None and not skip(lw):
                toks.append(lw)
        for k in writes:
            lw = self.last_write.get(k)
            if lw is not None and not skip(lw):
                toks.append(lw)
            for rt in self.readers.get(k, ()):
                if not skip(rt):
                    toks.append(rt)
        return toks

    def _update(self, tok, reads, writes):
        for k in reads:
            self.readers.setdefault(k, []).append(tok)
        for k in writes:
            self.last_write[k] = tok
            self.readers[k] = []

    def emit(self, e, fn, reads=(), writes=()):
        for tok in self._deps(e, reads, writes):
            sem_obj, val = self._resolve(tok)
            self._wait(e, sem_obj, val)
        instr = fn(self.eng[e])
        rec = [instr, len(self.instrs[e]), None]
        self.instrs[e].append(rec)
        self._update(("e", e, rec), reads, writes)
        return rec

    def stream(self, name, q="sp", serial=False):
        if name not in self.streams:
            sem = self.base.enter_context(self.nc.semaphore(f"dma_{name}"))
            self.streams[name] = [sem, 0, q, serial]
        return name

    def dma(self, sname, dst, src, reads=(), writes=(), **kw):
        s = self.streams[sname]
        q = s[2]
        if s[3] and s[1] > 0:
            self._wait(q, s[0], 16 * s[1])
        for tok in self._deps(q, reads, writes):
            sem_obj, val = self._resolve(tok)
            self._wait(q, sem_obj, val)
        instr = self.eng[q].dma_start(dst, src, **kw)
        s[1] += 1
        instr.then_inc(s[0], 16)
        rec = [instr, len(self.instrs[q]), None]
        self.instrs[q].append(rec)
        self._update(("d", sname, s[1]), reads, writes)
        return rec

    def barrier(self):
        toks = []
        for e in ("pe", "dve", "act"):
            if self.instrs[e]:
                rec = self.instrs[e][-1]
                toks.append(self._resolve(("e", e, rec)))
        for name, s in self.streams.items():
            if s[1] > 0:
                toks.append((s[0], 16 * s[1]))
        for e in ENG:
            for sem_obj, val in toks:
                self._wait(e, sem_obj, val)

    def finish(self, out_streams):
        for name in set(out_streams):
            s = self.streams[name]
            self._wait("pool", s[0], 16 * s[1])


class Tens:
    def __init__(self, h, name):
        self.h = h
        self.name = name

    def __getitem__(self, idx):
        return self.h[idx]

    def k(self, tag=None):
        return (self.name, tag)


_uid = [0]


def sb(P, shape, dtype, name=None, base=False):
    _uid[0] += 1
    nm = f"{name or 'sb'}_{_uid[0]}"
    st = P.base if base else P.scope
    h = st.enter_context(P.nc.sbuf_tensor(nm, list(shape), dtype))
    return Tens(h, nm)


def psbank(P, name=None):
    _uid[0] += 1
    nm = f"{name or 'ps'}_{_uid[0]}"
    h = P.base.enter_context(P.nc.psum_tensor(nm, [128, 512], F32))
    return Tens(h, nm)


def prelude(P, io):
    P.banks = [psbank(P, f"B{i}") for i in range(8)]
    P.identf = sb(P, [128, 128], F32, "identf", base=True)
    P.identb = sb(P, [128, 128], BF16, "identb", base=True)
    ld = P.stream("init_const", "pool", serial=True)
    P.dma(ld, P.identf[:], io["ident"], writes=[P.identf.k()])
    P.dma(ld, P.identb[:], io["ident"], writes=[P.identb.k()])


def layernorm_tile(P, out_ap, x_ap, tmp, st, xkeys, okey, n=D,
                   gamma=None, beta=None):
    xkeys = list(xkeys)
    P.emit("act", lambda a: a.activation(out_ap, x_ap, AF.Square,
                                         accum_out=st[:, 0:1]),
           reads=xkeys, writes=[(okey, "trash"), st.k("ssq")])
    P.emit("dve", lambda v: v.reduce_sum(st[:, 1:2], x_ap, axis=AX.X),
           reads=xkeys, writes=[st.k("s")])
    P.emit("dve", lambda v: v.tensor_scalar_mul(st[:, 2:3], st[:, 1:2], 1.0 / n),
           reads=[st.k("s")], writes=[st.k("mu")])
    P.emit("dve", lambda v: v.tensor_mul(st[:, 3:4], st[:, 2:3], st[:, 2:3]),
           reads=[st.k("mu")], writes=[st.k("mu2")])
    P.emit("dve", lambda v: v.tensor_scalar(st[:, 4:5], st[:, 0:1], 1.0 / n,
                                            None, ALU.mult),
           reads=[st.k("ssq")], writes=[st.k("var")])
    P.emit("dve", lambda v: v.tensor_sub(st[:, 4:5], st[:, 4:5], st[:, 3:4]),
           reads=[st.k("var"), st.k("mu2")], writes=[st.k("var")])
    P.emit("dve", lambda v: v.tensor_scalar_add(st[:, 4:5], st[:, 4:5], EPS),
           reads=[st.k("var")], writes=[st.k("var")])
    P.emit("act", lambda a: a.activation(st[:, 5:6], st[:, 4:5], AF.Sqrt),
           reads=[st.k("var")], writes=[st.k("std")])
    P.emit("dve", lambda v: v.reciprocal(st[:, 6:7], st[:, 5:6]),
           reads=[st.k("std")], writes=[st.k("rstd")])
    P.emit("dve", lambda v: v.tensor_mul(st[:, 7:8], st[:, 2:3], st[:, 6:7]),
           reads=[st.k("mu"), st.k("rstd")], writes=[st.k("nb")])
    P.emit("dve", lambda v: v.tensor_scalar_mul(st[:, 7:8], st[:, 7:8], -1.0),
           reads=[st.k("nb")], writes=[st.k("nb")])
    P.emit("act", lambda a: a.activation(out_ap, x_ap, AF.Identity,
                                         bias=st[:, 7:8], scale=st[:, 6:7]),
           reads=xkeys + [st.k("rstd"), st.k("nb")], writes=[okey])
    if gamma is not None:
        P.emit("dve", lambda v: v.tensor_mul(out_ap, out_ap, gamma[:, 0:n]),
               reads=[okey, gamma.k()], writes=[okey])
        P.emit("dve", lambda v: v.tensor_add(out_ap, out_ap, beta[:, 0:n]),
               reads=[okey, beta.k()], writes=[okey])


def _pm(tiles):
    t = np.asarray(tiles)
    P_ = t.shape[-2]
    lead = int(np.prod(t.shape[:-2], dtype=np.int64))
    t = t.reshape(lead, P_, t.shape[-1])
    return np.ascontiguousarray(t.transpose(1, 0, 2).reshape(P_, -1))


def _lhs_tiles(A):
    K, M = A.shape
    return A.reshape(K // 128, 128, M // 128, 128).transpose(2, 0, 1, 3)


def _pad_rows(A, to):
    out = np.zeros((to,) + A.shape[1:], dtype=A.dtype)
    out[: A.shape[0]] = A
    return out


def _col128(v):
    return np.ascontiguousarray(v.reshape(-1, 128).T)


def prep(inputs):
    g = {}
    f32 = np.float32

    g["ident"] = np.eye(128, dtype=f32)
    hd = 64
    freqs = (10000.0 ** (-np.arange(0, hd, 2, dtype=np.float64) / hd))
    ang = np.arange(T, dtype=np.float64)[:, None] * freqs[None, :]
    g["l1a_cos"] = np.cos(ang).astype(f32)
    g["l1a_sin"] = np.sin(ang).astype(f32)
    half = np.arange(0, hd, 2, dtype=np.float64)
    inv_freq = 1.0 / 10000.0 ** (half / hd)
    t_ = np.arange(T, dtype=np.float64)
    fr = t_[:, None] * inv_freq[None, :]
    fr64 = np.concatenate([fr, fr], 1)
    base = (half + 0.4 * hd) / (1.4 * hd)
    power = (t_ - T // 2) / 256.0
    sc = base[None, :] ** power[:, None]
    sc64 = np.concatenate([sc, sc], 1)
    cos64, sin64 = np.cos(fr64), np.sin(fr64)
    sgn = np.concatenate([-np.ones(32), np.ones(32)])[None, :]
    g["xpos_cq"] = (cos64 * sc64).astype(f32)
    g["xpos_sq"] = (sin64 * sc64 * sgn).astype(f32)
    g["xpos_ck"] = (cos64 / sc64).astype(f32)
    g["xpos_sk"] = (sin64 / sc64 * sgn).astype(f32)
    g["mask_up"] = np.triu(np.full((128, 128), -1e9, dtype=f32), 1)
    g["mask_low"] = np.tril(np.full((128, 128), -1e9, dtype=f32), -1)

    g["inw_pm"] = _pm(_lhs_tiles(inputs["L0_attn_in_w"]))
    g["outw_pm"] = _pm(inputs["L0_attn_out_w"].reshape(16, 128, 1024))
    g["inb_col"] = _col128(inputs["L0_attn_in_b"])
    g["dwb_col"] = _col128(inputs["L0_attn_dw_b"])
    g["mixb_col"] = _col128(inputs["L0_attn_mix_b"])
    g["ld_col"] = _col128(inputs["L0_attn_log_decay"])
    g["outb_b"] = np.broadcast_to(inputs["L0_attn_out_b"], (128, D)).copy()
    g["l0g_b"] = np.broadcast_to(inputs["L0_attn_norm_gamma"], (128, D)).copy()
    g["l0b_b"] = np.broadcast_to(inputs["L0_attn_norm_beta"], (128, D)).copy()
    dw = inputs["L0_attn_dw_w"]
    dwdiag = np.zeros((16, 7, 128, 128), dtype=f32)
    idx = np.arange(128)
    for gi in range(16):
        for k in range(7):
            dwdiag[gi, k, idx, idx] = dw[gi * 128 + idx, k]
    mixt = _lhs_tiles(inputs["L0_attn_mix_w"])
    cmw = np.concatenate([dwdiag, mixt], axis=1)
    g["cmw_pm"] = _pm(cmw)

    w1 = np.concatenate([inputs["L0_ffn_w1"], inputs["L0_ffn_new_w1"]], 0)
    w2 = np.concatenate([inputs["L0_ffn_w2"], inputs["L0_ffn_new_w2"]], 0)
    w3 = np.concatenate([inputs["L0_ffn_w3"], inputs["L0_ffn_new_w3"]], 0)
    g["moe_w1"] = np.stack([_pm(_lhs_tiles(np.ascontiguousarray(w1[e].T)))
                            for e in range(9)])
    g["moe_w3"] = np.stack([_pm(_lhs_tiles(np.ascontiguousarray(w3[e].T)))
                            for e in range(9)])
    g["moe_w2"] = np.stack([_pm(np.ascontiguousarray(w2[e].T).reshape(22, 128, 1024))
                            for e in range(9)])
    gate = np.concatenate([inputs["L0_ffn_gate_weight"],
                           inputs["L0_ffn_new_gate_weight"]], 0)
    g["gateT_pm"] = _pm(np.ascontiguousarray(gate.T).reshape(8, 128, 9))
    gb = np.zeros((128, 9), dtype=f32)
    gb[:, 8] = np.asarray(inputs["L0_ffn_gate_bias"])[0]
    g["gateb_b"] = gb

    for nm, W in (("wq", "L1_attn_Wq"), ("wk", "L1_attn_Wk"),
                  ("wv", "L1_attn_Wv"), ("wo", "L1_attn_Wo")):
        g[nm + "T_pm"] = _pm(np.ascontiguousarray(inputs[W].T).reshape(8, 128, 1024))
    for nm, B in (("bq", "L1_attn_bq"), ("bk", "L1_attn_bk"),
                  ("bv", "L1_attn_bv"), ("bo", "L1_attn_bo")):
        g[nm + "_b"] = np.broadcast_to(inputs[B], (128, D)).copy()
    g["l1scale"] = np.broadcast_to(np.asarray(inputs["L1_attn_scale"]), (128, 1)).copy()

    g["f_wqkv"] = np.stack([_pm(np.ascontiguousarray(inputs["L1_ffn_wqkv"][l])
                                .reshape(8, 128, 1536)) for l in range(2)])
    g["f_wo"] = np.stack([_pm(np.ascontiguousarray(inputs["L1_ffn_wo"][l])
                              .reshape(4, 128, 1024)) for l in range(2)])
    w1f = inputs["L1_ffn_w1"]
    g["f_w1a"] = np.stack([_pm(_lhs_tiles(_pad_rows(
        np.ascontiguousarray(w1f[l][:, :2730].T), 22 * 128).T.copy()))
        for l in range(2)])
    g["f_w1g"] = np.stack([_pm(_lhs_tiles(_pad_rows(
        np.ascontiguousarray(w1f[l][:, 2730:].T), 22 * 128).T.copy()))
        for l in range(2)])
    g["f_w2"] = np.stack([_pm(_pad_rows(inputs["L1_ffn_w2"][l], 22 * 128)
                              .reshape(22, 128, 1024)) for l in range(2)])
    b1 = inputs["L1_ffn_b1"]
    g["f_b1a"] = np.stack([_col128(_pad_rows(b1[l][:2730], 22 * 128))
                           for l in range(2)])
    g["f_b1g"] = np.stack([_col128(_pad_rows(b1[l][2730:], 22 * 128))
                           for l in range(2)])
    g["f_b2_b"] = np.stack([np.broadcast_to(inputs["L1_ffn_b2"][l], (128, D)).copy()
                            for l in range(2)])
    g["f_lng_b"] = np.stack([np.broadcast_to(inputs["L1_ffn_ln_g"][l], (128, D)).copy()
                             for l in range(2)])
    g["f_lnb_b"] = np.stack([np.broadcast_to(inputs["L1_ffn_ln_b"][l], (128, D)).copy()
                             for l in range(2)])
    g["f_qs_b"] = np.stack([np.broadcast_to(np.tile(np.asarray(inputs["L1_ffn_q_scale"][l]), 8),
                                            (128, 512)).copy() for l in range(2)])
    g["f_ks_b"] = np.stack([np.broadcast_to(np.tile(np.asarray(inputs["L1_ffn_k_scale"][l]), 8),
                                            (128, 512)).copy() for l in range(2)])
    for k in g:
        g[k] = np.ascontiguousarray(g[k], dtype=f32)
    return g


HOST_SPECS = {
    "ident": [128, 128],
    "l1a_cos": [T, 32], "l1a_sin": [T, 32],
    "xpos_cq": [T, 64], "xpos_sq": [T, 64],
    "xpos_ck": [T, 64], "xpos_sk": [T, 64],
    "mask_up": [128, 128], "mask_low": [128, 128],
    "inw_pm": [128, 32 * 8 * 128],
    "cmw_pm": [128, 16 * 23 * 128],
    "outw_pm": [128, 16 * 1024],
    "inb_col": [128, 32], "dwb_col": [128, 16],
    "mixb_col": [128, 16], "ld_col": [128, 16],
    "outb_b": [128, D], "l0g_b": [128, D], "l0b_b": [128, D],
    "moe_w1": [9, 128, 22 * 8 * 128],
    "moe_w3": [9, 128, 22 * 8 * 128],
    "moe_w2": [9, 128, 22 * 1024],
    "gateT_pm": [128, 8 * 9], "gateb_b": [128, 9],
    "wqT_pm": [128, 8 * 1024], "wkT_pm": [128, 8 * 1024],
    "wvT_pm": [128, 8 * 1024], "woT_pm": [128, 8 * 1024],
    "bq_b": [128, D], "bk_b": [128, D], "bv_b": [128, D], "bo_b": [128, D],
    "l1scale": [128, 1],
    "f_wqkv": [2, 128, 8 * 1536],
    "f_wo": [2, 128, 4 * 1024],
    "f_w1a": [2, 128, 22 * 8 * 128],
    "f_w1g": [2, 128, 22 * 8 * 128],
    "f_w2": [2, 128, 22 * 1024],
    "f_b1a": [2, 128, 22], "f_b1g": [2, 128, 22],
    "f_b2_b": [2, 128, D], "f_lng_b": [2, 128, D], "f_lnb_b": [2, 128, D],
    "f_qs_b": [2, 128, 512], "f_ks_b": [2, 128, 512],
}

SCRATCH_SPECS = {
    "dbg_xnT": ([NMAC, 128, 8 * MACRO], "bf16"),
    "dbg_g": ([NMAC, 128, 16 * MACRO], "bf16"),
    "dbg_v": ([NMAC, 128, 16 * (MACRO + 6)], "bf16"),
    "dbg_y": ([NMAC, 128, 16 * MACRO], "bf16"),
    "dbg_h": ([NMAC, 128, 16 * MACRO], "bf16"),
    "dbg_ln": ([NMAC, 128, 2 * D], "bf16"),
    "dbg_att": ([128, 1232], "f32"),
    "dbg_st": ([NMAC, 128, 32], "f32"),
    "res1": ([T, D], "f32"),
    "res2": ([T, D], "f32"),
    "res3": ([T, D], "f32"),
    "xlnT": ([NMAC, 128, 8 * MACRO], "bf16"),
    "qT": ([8, 128, T], "bf16"),
    "kT": ([8, 128, T], "bf16"),
    "vT1": ([T, D], "bf16"),
    "attM": ([16, 1], "f32"),
    "oT2": ([8, 128, T], "bf16"),
    "za": ([T, D], "f32"),
    "zb": ([T, D], "f32"),
    "zc": ([T, D], "f32"),
    "q2T": ([4, 128, T], "bf16"),
    "k2T": ([4, 128, T], "bf16"),
    "v2T": ([T, 512], "bf16"),
    "o2T": ([4, 128, T], "bf16"),
    "h2T": ([NMAC, 128, 8 * MACRO], "bf16"),
    "out": ([T, D], "f32"),
}

DT = {"f32": F32, "bf16": BF16}


def emit_l0a(P, io):
    identf = P.identf
    outw = sb(P, [128, 16 * 1024], BF16, "outw")
    inb = sb(P, [128, 32], F32, "inb")
    bcomb = sb(P, [128, 16], F32, "bcomb")
    dec = sb(P, [128, 16], F32, "dec")
    outb = sb(P, [128, D], F32, "outb")
    gam = sb(P, [128, D], F32, "gam")
    bet = sb(P, [128, D], F32, "bet")
    tmpc = sb(P, [128, 16], F32, "tmpc")
    carry = sb(P, [128, 16], F32, "carry")

    ld = P.stream("l0_const", "pool", serial=True)
    P.dma(ld, outw[:], io["outw_pm"], writes=[outw.k()])
    P.dma(ld, inb[:], io["inb_col"], writes=[inb.k()])
    P.dma(ld, tmpc[:], io["dwb_col"], writes=[tmpc.k()])
    P.dma(ld, bcomb[:], io["mixb_col"], writes=[bcomb.k()])
    P.dma(ld, dec[:], io["ld_col"], writes=[dec.k()])
    P.dma(ld, outb[:], io["outb_b"], writes=[outb.k()])
    P.dma(ld, gam[:], io["l0g_b"], writes=[gam.k()])
    P.dma(ld, bet[:], io["l0b_b"], writes=[bet.k()])
    P.emit("dve", lambda v: v.tensor_add(bcomb[:], bcomb[:], tmpc[:]),
           reads=[bcomb.k(), tmpc.k()], writes=[bcomb.k()])
    P.emit("act", lambda a: a.activation(dec[:], dec[:], AF.Sigmoid),
           reads=[dec.k()], writes=[dec.k()])

    CT = 1
    NCH = 32 // CT
    NSLOT = 3
    inw_sl = [sb(P, [128, CT * 8 * 128], BF16, f"inw{s}") for s in range(NSLOT)]
    inw_st = [P.stream(f"l0_inw{s}", "pool") for s in range(NSLOT)]

    inw_seq = [0]

    def load_inw_next():
        seq = inw_seq[0]
        if seq >= NMAC * NCH:
            return
        inw_seq[0] += 1
        j = seq % NCH
        sl = seq % NSLOT
        P.dma(inw_st[sl], inw_sl[sl][:],
              io["inw_pm"][:, j * CT * 1024:(j + 1) * CT * 1024],
              writes=[inw_sl[sl].k()])

    MT = 1
    NMH = 16 // MT
    MSLOT = 3
    mix_sl = [sb(P, [128, 23 * 128], BF16, f"mix{s}") for s in range(MSLOT)]
    mix_st = [P.stream(f"l0_mix{s}", "pool") for s in range(MSLOT)]

    mix_seq = [0]

    def load_mix_next():
        seq = mix_seq[0]
        if seq >= NMAC * NMH:
            return
        mix_seq[0] += 1
        j = seq % NMH
        sl = seq % MSLOT
        P.dma(mix_st[sl], mix_sl[sl][:],
              io["cmw_pm"][:, j * 23 * 128:(j + 1) * 23 * 128],
              writes=[mix_sl[sl].k()])

    xt = [sb(P, [128, 4 * D], F32, f"xt{b}") for b in range(2)]
    ln1t_ = [sb(P, [128, D], F32, f"ln1t{b}") for b in range(2)]
    xn_ = [sb(P, [128, D], F32, f"xn{b}") for b in range(2)]
    sqt_ = [None, None]
    stats_ = [sb(P, [128, 8], F32, f"stats{b}") for b in range(2)]
    xnT_ = [sb(P, [128, 8 * MACRO], BF16, f"xnTa"),
            sb(P, [128, 8 * MACRO], BF16, f"xnTb")]
    gmac = sb(P, [128, 16 * MACRO], BF16, "gmac")
    vmac = sb(P, [128, 16 * (MACRO + 6)], BF16, "vmac")
    tail = sb(P, [128, 16 * 6], BF16, "tail")
    ymac = sb(P, [128, 16 * MACRO], BF16, "ymac")
    hmac = sb(P, [128, 16 * MACRO], BF16, "hmac")
    vtmp = sb(P, [128, MACRO], F32, "vtmp")
    res_t = [sb(P, [128, D], F32, f"res_t{b}") for b in range(2)]
    if DEBUG:
        dbgln = sb(P, [128, 2 * D], BF16, "dbgln")
        dbgst = sb(P, [128, 32], F32, "dbgst")

    xst = [[P.stream(f"l0_x{b}_{tt}", "sp") for tt in range(4)] for b in range(2)]
    rst = [P.stream(f"l0_res{b}", "sp") for b in range(2)]
    B = P.banks

    P.emit("dve", lambda v: v.memset(tail[:], 0.0), writes=[tail.k()])
    for _ in range(NSLOT - 1):
        load_inw_next()
        load_mix_next()

    def ln_phase(m):
        XT = xt[m % 2]
        xnT = xnT_[m % 2]
        for tt in range(4):
            t0 = m * MACRO + tt * 128
            P.dma(xst[m % 2][tt], XT[:, tt * D:(tt + 1) * D],
                  io["xin"][t0:t0 + 128, :], writes=[XT.k(tt)])
        for tt in range(4):
            xa = XT[:, tt * D:(tt + 1) * D]
            ln1t, xn = ln1t_[tt % 2], xn_[tt % 2]
            sqt, stats = sqt_[tt % 2], stats_[tt % 2]
            if BYPASS_LN:
                P.emit("dve", lambda v, xa=xa, ln1t=ln1t: v.tensor_copy(ln1t[:], xa),
                       reads=[XT.k(tt)], writes=[ln1t.k()])
                P.emit("dve", lambda v, xn=xn, ln1t=ln1t: v.tensor_copy(xn[:], ln1t[:]),
                       reads=[ln1t.k()], writes=[xn.k()])
            else:
                layernorm_tile(P, ln1t[:], xa, sqt, stats, [XT.k(tt)], ln1t.k())
                if DEBUG:
                    if tt < 2:
                        P.emit("dve", lambda v, tt=tt: v.tensor_copy(
                            dbgln[:, tt * D:(tt + 1) * D], ln1t[:]),
                            reads=[ln1t.k()], writes=[dbgln.k(tt)])
                    P.emit("dve", lambda v, tt=tt: v.tensor_copy(
                        dbgst[:, tt * 8:(tt + 1) * 8], stats[:]),
                        reads=[stats.k(kk) for kk in
                               ("ssq", "s", "mu", "mu2", "var", "std", "rstd", "nb")],
                        writes=[dbgst.k(tt)])
                layernorm_tile(P, xn[:], ln1t[:], sqt, stats, [ln1t.k()], xn.k(),
                               gamma=gam, beta=bet)
            for dk in range(8):
                pb = B[6 + dk % 2]
                P.emit("pe", lambda t, dk=dk, pb=pb, xn=xn: t.transpose(
                    pb[:, 0:128], xn[:, dk * 128:(dk + 1) * 128], identf[:]),
                    reads=[xn.k(), identf.k()], writes=[pb.k()])
                P.emit("act", lambda a, tt=tt, dk=dk, pb=pb, xnT=xnT: a.copy(
                    xnT[:, dk * MACRO + tt * 128: dk * MACRO + (tt + 1) * 128],
                    pb[:, 0:128]),
                    reads=[pb.k()], writes=[xnT.k(dk)])
            P.emit("dve", lambda v, xa=xa, ln1t=ln1t: v.tensor_add(xa, xa, ln1t[:]),
                   reads=[XT.k(tt), ln1t.k()], writes=[XT.k(tt)])

    def compute_phase(m):
        XT = xt[m % 2]
        xnT = xnT_[m % 2]
        for c in range(32):
            j = c // CT
            buf = inw_sl[(m * NCH + j) % NSLOT]
            coff = (c % CT) * 8 * 128
            pb = B[c % 2]
            for k in range(8):
                P.emit("pe", lambda t, k=k, buf=buf, coff=coff, pb=pb:
                       t.matmul(pb[:], buf[:, coff + k * 128:coff + (k + 1) * 128],
                                xnT[:, k * MACRO:(k + 1) * MACRO],
                                start=(k == 0), stop=(k == 7)),
                       reads=[buf.k(), xnT.k(k)], writes=[pb.k()])
            if c < 16:
                gi = c
                P.emit("act", lambda a, gi=gi, pb=pb: a.activation(
                    gmac[:, gi * MACRO:(gi + 1) * MACRO], pb[:], AF.Sigmoid,
                    bias=inb[:, gi:gi + 1]),
                    reads=[pb.k(), inb.k()], writes=[gmac.k(gi)])
            else:
                gi = c - 16
                P.emit("act", lambda a, gi=gi, pb=pb: a.add(
                    vmac[:, gi * (MACRO + 6) + 6: gi * (MACRO + 6) + 6 + MACRO],
                    pb[:], inb[:, 16 + gi:16 + gi + 1]),
                    reads=[pb.k(), inb.k()], writes=[vmac.k(gi)])
                init = 0.0 if m == 0 else carry[:, gi:gi + 1]
                P.emit("dve", lambda v, gi=gi, init=init: v.tensor_tensor_scan(
                    ymac[:, gi * MACRO:(gi + 1) * MACRO],
                    dec[:, gi:gi + 1].broadcast_to([128, MACRO]),
                    vmac[:, gi * (MACRO + 6) + 6: gi * (MACRO + 6) + 6 + MACRO],
                    init, ALU.mult, ALU.add),
                    reads=[vmac.k(gi), dec.k()] + ([carry.k()] if m else []),
                    writes=[ymac.k(gi)])
            if c % CT == CT - 1:
                load_inw_next()
        P.emit("dve", lambda v: v.tensor_copy(
            vmac[:].rearrange("p (g c) -> p g c", g=16)[:, :, 0:6],
            tail[:].rearrange("p (g c) -> p g c", g=16)),
            reads=[tail.k()] + [vmac.k(gi) for gi in range(16)],
            writes=[vmac.k("halo")])
        P.emit("dve", lambda v: v.tensor_copy(
            carry[:],
            ymac[:].rearrange("p (g c) -> p g c", g=16)[:, :, MACRO - 1:MACRO]
            .rearrange("p g c -> p (g c)")),
            reads=[ymac.k(gi) for gi in range(16)], writes=[carry.k()])
        for gi in range(16):
            jm = gi // MT
            mbuf = mix_sl[(m * NMH + jm) % MSLOT]
            pc, pm_ = B[2 + gi % 2], B[4 + gi % 2]
            for k in range(7):
                P.emit("pe", lambda t, gi=gi, k=k, pc=pc, mbuf=mbuf: t.matmul(
                    pc[:], mbuf[:, k * 128:(k + 1) * 128],
                    vmac[:, gi * (MACRO + 6) + k: gi * (MACRO + 6) + k + MACRO],
                    start=(k == 0), stop=(k == 6)),
                    reads=[mbuf.k(), vmac.k(gi), vmac.k("halo")], writes=[pc.k()])
            for k in range(16):
                P.emit("pe", lambda t, k=k, mbuf=mbuf, pm_=pm_: t.matmul(
                    pm_[:], mbuf[:, (7 + k) * 128:(8 + k) * 128],
                    ymac[:, k * MACRO:(k + 1) * MACRO],
                    start=(k == 0), stop=(k == 15)),
                    reads=[mbuf.k(), ymac.k(k)], writes=[pm_.k()])
            P.emit("act", lambda a, gi=gi, pc=pc: a.add(
                vtmp[:], pc[:], bcomb[:, gi:gi + 1]),
                reads=[pc.k(), bcomb.k()], writes=[vtmp.k()])
            P.emit("dve", lambda v, pm_=pm_: v.tensor_add(
                vtmp[:], vtmp[:], pm_[:]),
                reads=[vtmp.k(), pm_.k()], writes=[vtmp.k()])
            P.emit("dve", lambda v, gi=gi: v.tensor_mul(
                hmac[:, gi * MACRO:(gi + 1) * MACRO], vtmp[:],
                gmac[:, gi * MACRO:(gi + 1) * MACRO]),
                reads=[vtmp.k(), gmac.k(gi)], writes=[hmac.k(gi)])
            if gi % MT == MT - 1:
                load_mix_next()
        P.emit("dve", lambda v: v.tensor_copy(
            tail[:].rearrange("p (g c) -> p g c", g=16),
            vmac[:].rearrange("p (g c) -> p g c", g=16)[:, :, MACRO:MACRO + 6]),
            reads=[vmac.k(gi) for gi in range(16)], writes=[tail.k()])
        if DEBUG:
            dst = P.stream("l0_dbg", "sp", serial=True)
            for nm_, buf_ in (("dbg_xnT", xnT), ("dbg_g", gmac), ("dbg_v", vmac),
                              ("dbg_y", ymac), ("dbg_h", hmac)):
                pass
            for nm_, buf_ in (("dbg_xnT", xnT), ("dbg_g", gmac), ("dbg_v", vmac),
                              ("dbg_y", ymac), ("dbg_h", hmac),
                              ("dbg_ln", dbgln), ("dbg_st", dbgst)):
                if nm_ not in io:
                    continue
                rd = [buf_.k(kk) for kk in range(16 if nm_ != "dbg_xnT" else 8)]
                if nm_ == "dbg_v":
                    rd.append(buf_.k("halo"))
                if nm_ in ("dbg_ln", "dbg_st"):
                    rd = [buf_.k(kk) for kk in range(4)]
                P.dma(dst, io[nm_][m], buf_[:], reads=rd)
        for tt in range(4):
            RT = res_t[tt % 2]
            for half in range(2):
                po = B[6 + half]
                for k in range(16):
                    P.emit("pe", lambda t, tt=tt, half=half, k=k, po=po: t.matmul(
                        po[:], hmac[:, k * MACRO + tt * 128: k * MACRO + (tt + 1) * 128],
                        outw[:, k * 1024 + half * 512: k * 1024 + (half + 1) * 512],
                        start=(k == 0), stop=(k == 15)),
                        reads=[hmac.k(k), outw.k()], writes=[po.k()])
                P.emit("dve", lambda v, half=half, po=po, RT=RT: v.tensor_add(
                    RT[:, half * 512:(half + 1) * 512], po[:],
                    outb[:, half * 512:(half + 1) * 512]),
                    reads=[po.k(), outb.k()], writes=[RT.k()])
            P.emit("dve", lambda v, tt=tt, RT=RT: v.tensor_add(
                RT[:], RT[:], XT[:, tt * D:(tt + 1) * D]),
                reads=[RT.k(), XT.k(tt)], writes=[RT.k()])
            t0 = m * MACRO + tt * 128
            P.dma(rst[tt % 2], io["res1"][t0:t0 + 128, :], RT[:], reads=[RT.k()])

    ln_phase(0)
    for m in range(NMAC):
        if m + 1 < NMAC:
            ln_phase(m + 1)
        compute_phase(m)
    return ["l0_res0", "l0_res1"]


MOE_CHUNKS = [(0, 6), (6, 6), (12, 6), (18, 4)]


def emit_moe(P, io):
    identf = P.identf
    B = P.banks
    gateT = sb(P, [128, 8 * 9], F32, "gateT")
    gateb = sb(P, [128, 9], F32, "gateb")
    cw = sb(P, [128, 32 * 9], F32, "cw")
    ld = P.stream("moe_const", "pool", serial=True)
    P.dma(ld, gateT[:], io["gateT_pm"], writes=[gateT.k()])
    P.dma(ld, gateb[:], io["gateb_b"], writes=[gateb.k()])

    xt = [sb(P, [128, D], F32, f"mx{b}") for b in range(2)]
    xn = sb(P, [128, D], F32, "mxn")
    sqt = sb(P, [128, D], F32, "msq")
    stats = sb(P, [128, 8], F32, "mst")
    xnTf = sb(P, [128, D], F32, "mxnTf")
    xnTb = sb(P, [128, 8 * MACRO], BF16, "mxnTb")
    sc9 = sb(P, [128, 32], F32, "msc9")
    xst = [P.stream(f"moe_x{b}", "sp") for b in range(2)]
    r2st = [P.stream(f"moe_r2i{b}", "sp") for b in range(2)]
    xlst = P.stream("moe_xlT", "sp")

    for m in range(NMAC):
        for tt in range(4):
            ti = m * 4 + tt
            XT = xt[ti % 2]
            t0 = ti * 128
            P.dma(xst[ti % 2], XT[:], io["res1"][t0:t0 + 128, :],
                  reads=[("hbm_res1", m)], writes=[XT.k()])
            layernorm_tile(P, xn[:], XT[:], sqt, stats, [XT.k()], xn.k())
            P.dma(r2st[ti % 2], io["res2"][t0:t0 + 128, :], XT[:],
                  reads=[XT.k()], writes=[("hbm_res2", m, "init", tt)])
            for dk in range(8):
                pb = B[dk % 2]
                P.emit("pe", lambda t, dk=dk, pb=pb: t.transpose(
                    pb[:, 0:128], xn[:, dk * 128:(dk + 1) * 128], identf[:]),
                    reads=[xn.k(), identf.k()], writes=[pb.k()])
                P.emit("act", lambda a, dk=dk, pb=pb: a.copy(
                    xnTf[:, dk * 128:(dk + 1) * 128], pb[:, 0:128]),
                    reads=[pb.k()], writes=[xnTf.k(dk)])
                P.emit("dve", lambda v, tt=tt, dk=dk: v.tensor_copy(
                    xnTb[:, dk * MACRO + tt * 128: dk * MACRO + (tt + 1) * 128],
                    xnTf[:, dk * 128:(dk + 1) * 128]),
                    reads=[xnTf.k(dk)], writes=[xnTb.k(dk)])
            pg = B[2]
            for dk in range(8):
                P.emit("pe", lambda t, dk=dk, pg=pg: t.matmul(
                    pg[:, 0:9], xnTf[:, dk * 128:(dk + 1) * 128],
                    gateT[:, dk * 9:(dk + 1) * 9],
                    start=(dk == 0), stop=(dk == 7)),
                    reads=[xnTf.k(dk), gateT.k()], writes=[pg.k()])
            P.emit("dve", lambda v, pg=pg: v.tensor_add(
                sc9[:, 0:9], pg[:, 0:9], gateb[:]),
                reads=[pg.k(), gateb.k()], writes=[sc9.k("s")])
            P.emit("act", lambda a: a.activation(sc9[:, 0:9], sc9[:, 0:9],
                                                 AF.Sigmoid),
                   reads=[sc9.k("s")], writes=[sc9.k("s")])
            P.emit("dve", lambda v: v.max(sc9[:, 16:24], sc9[:, 0:9]),
                   reads=[sc9.k("s")], writes=[sc9.k("m8")])
            P.emit("dve", lambda v: v.tensor_add(sc9[:, 24:25], sc9[:, 16:17],
                                                 sc9[:, 17:18]),
                   reads=[sc9.k("m8")], writes=[sc9.k("d")])
            P.emit("dve", lambda v: v.tensor_scalar_add(sc9[:, 24:25],
                                                        sc9[:, 24:25], 1e-20),
                   reads=[sc9.k("d")], writes=[sc9.k("d")])
            P.emit("dve", lambda v: v.reciprocal(sc9[:, 25:26], sc9[:, 24:25]),
                   reads=[sc9.k("d")], writes=[sc9.k("rd")])
            P.emit("dve", lambda v: v.tensor_mul(sc9[:, 26:27], sc9[:, 16:17],
                                                 sc9[:, 25:26]),
                   reads=[sc9.k("m8"), sc9.k("rd")], writes=[sc9.k("w1")])
            P.emit("dve", lambda v: v.tensor_mul(sc9[:, 27:28], sc9[:, 17:18],
                                                 sc9[:, 25:26]),
                   reads=[sc9.k("m8"), sc9.k("rd")], writes=[sc9.k("w2")])
            P.emit("dve", lambda v: v.tensor_scalar(
                sc9[:, 0:9] if False else cw[:, ti * 9:(ti + 1) * 9],
                sc9[:, 0:9], sc9[:, 16:17], sc9[:, 26:27],
                ALU.is_equal, ALU.mult),
                reads=[sc9.k("s"), sc9.k("m8"), sc9.k("w1")],
                writes=[cw.k(ti)])
            P.emit("dve", lambda v: v.tensor_scalar(
                sc9[:, 0:9], sc9[:, 0:9], sc9[:, 17:18], sc9[:, 27:28],
                ALU.is_equal, ALU.mult),
                reads=[sc9.k("s"), sc9.k("m8"), sc9.k("w2")],
                writes=[sc9.k("eq2")])
            P.emit("dve", lambda v, ti=ti: v.tensor_add(
                cw[:, ti * 9:(ti + 1) * 9], cw[:, ti * 9:(ti + 1) * 9],
                sc9[:, 0:9]),
                reads=[cw.k(ti), sc9.k("eq2")], writes=[cw.k(ti)])
        P.dma(xlst, io["xlnT"][m], xnTb[:],
              reads=[xnTb.k(dk) for dk in range(8)],
              writes=[("hbm_xlnT", m)])

    w1r = [sb(P, [128, 6 * 8 * 128], BF16, f"w1r{b}") for b in range(2)]
    w3r = [sb(P, [128, 6 * 8 * 128], BF16, f"w3r{b}") for b in range(2)]
    w2r = [sb(P, [128, 6 * 1024], BF16, f"w2r{b}") for b in range(2)]
    wst = [[P.stream(f"moe_w{w}{b}", "pool") for b in range(2)]
           for w in range(3)]
    xbuf = [sb(P, [128, 8 * MACRO], BF16, f"moexb{b}") for b in range(2)]
    xbst = [P.stream(f"moe_xb{b}", "sp") for b in range(2)]
    hT = [sb(P, [128, 6 * MACRO], BF16, f"moeh{b}") for b in range(2)]
    delta = [sb(P, [128, 4 * D], F32, f"moed{b}") for b in range(2)]
    dst_ = [P.stream(f"moe_acc{b}", "pool") for b in range(2)]
    silu_ = [sb(P, [128, MACRO], F32, f"moesilu{b}") for b in range(2)]

    chunk_seq = [0]
    schedule = [(e, ci) for e in range(9) for ci in range(4)]

    def load_chunk_next():
        seq = chunk_seq[0]
        if seq >= len(schedule):
            return
        chunk_seq[0] += 1
        e, ci = schedule[seq]
        c0, cn = MOE_CHUNKS[ci]
        sl = seq % 2
        P.dma(wst[0][sl], w1r[sl][:, 0:cn * 8 * 128],
              io["moe_w1"][e][:, c0 * 8 * 128:(c0 + cn) * 8 * 128],
              writes=[w1r[sl].k()])
        P.dma(wst[1][sl], w3r[sl][:, 0:cn * 8 * 128],
              io["moe_w3"][e][:, c0 * 8 * 128:(c0 + cn) * 8 * 128],
              writes=[w3r[sl].k()])
        P.dma(wst[2][sl], w2r[sl][:, 0:cn * 1024],
              io["moe_w2"][e][:, c0 * 1024:(c0 + cn) * 1024],
              writes=[w2r[sl].k()])

    load_chunk_next()
    load_chunk_next()

    iters = [(seq, e, ci, m) for seq, (e, ci) in enumerate(schedule)
             for m in range(NMAC)]

    def ab_h(i):
        seq, e, ci, m = iters[i]
        c0, cn = MOE_CHUNKS[ci]
        sl = seq % 2
        W1, W3 = w1r[sl], w3r[sl]
        xb = xbuf[i % 2]
        P.dma(xbst[i % 2], xb[:], io["xlnT"][m],
              reads=[("hbm_xlnT", m)], writes=[xb.k()])
        HT = hT[i % 2]
        for h in range(cn):
            pa, pb_ = B[h % 2], B[2 + h % 2]
            for k in range(8):
                P.emit("pe", lambda t, h=h, k=k, pa=pa, W1=W1, xb=xb: t.matmul(
                    pa[:], W1[:, (h * 8 + k) * 128:(h * 8 + k + 1) * 128],
                    xb[:, k * MACRO:(k + 1) * MACRO],
                    start=(k == 0), stop=(k == 7)),
                    reads=[W1.k(), xb.k()], writes=[pa.k()])
            for k in range(8):
                P.emit("pe", lambda t, h=h, k=k, pb_=pb_, W3=W3, xb=xb: t.matmul(
                    pb_[:], W3[:, (h * 8 + k) * 128:(h * 8 + k + 1) * 128],
                    xb[:, k * MACRO:(k + 1) * MACRO],
                    start=(k == 0), stop=(k == 7)),
                    reads=[W3.k(), xb.k()], writes=[pb_.k()])
            silu_t = silu_[h % 2]
            P.emit("act", lambda a, pa=pa, silu_t=silu_t: a.activation(
                silu_t[:], pa[:], AF.Silu),
                reads=[pa.k()], writes=[silu_t.k()])
            P.emit("dve", lambda v, h=h, pb_=pb_, HT=HT, silu_t=silu_t:
                   v.tensor_mul(
                HT[:, h * MACRO:(h + 1) * MACRO], silu_t[:], pb_[:]),
                reads=[silu_t.k(), pb_.k()], writes=[HT.k(h)])

    def w2d(i):
        seq, e, ci, m = iters[i]
        c0, cn = MOE_CHUNKS[ci]
        sl = seq % 2
        W2 = w2r[sl]
        HT = hT[i % 2]
        DL = delta[i % 2]
        for tt in range(4):
            for dh in range(2):
                po = B[4 + (tt * 2 + dh) % 2]
                for k in range(cn):
                    P.emit("pe", lambda t, tt=tt, dh=dh, k=k, po=po, W2=W2,
                           HT=HT: t.matmul(
                        po[:], HT[:, k * MACRO + tt * 128: k * MACRO + (tt + 1) * 128],
                        W2[:, k * 1024 + dh * 512:k * 1024 + (dh + 1) * 512],
                        start=(k == 0), stop=(k == cn - 1)),
                        reads=[HT.k(k), W2.k()], writes=[po.k()])
                col = (m * 4 + tt) * 9 + e
                if (tt * 2 + dh) % 2 == 0:
                    P.emit("dve", lambda v, tt=tt, dh=dh, po=po, DL=DL,
                           col=col: v.tensor_scalar_mul(
                        DL[:, tt * D + dh * 512: tt * D + (dh + 1) * 512],
                        po[:], cw[:, col:col + 1]),
                        reads=[po.k(), cw.k(m * 4 + tt)], writes=[DL.k()])
                else:
                    P.emit("act", lambda a, tt=tt, dh=dh, po=po, DL=DL,
                           col=col: a.activation(
                        DL[:, tt * D + dh * 512: tt * D + (dh + 1) * 512],
                        po[:], AF.Identity, scale=cw[:, col:col + 1]),
                        reads=[po.k(), cw.k(m * 4 + tt)], writes=[DL.k()])
        t0 = m * MACRO
        P.dma(dst_[i % 2], io["res2"][t0:t0 + MACRO, :]
              .rearrange("(tt p) f -> p tt f", p=128),
              DL[:].rearrange("p (tt f) -> p tt f", tt=4),
              reads=[DL.k()] + [("hbm_res2", m, "init", t) for t in range(4)],
              writes=[("hbm_res2", m)], accum_op=ALU.add)
        if m == NMAC - 1:
            load_chunk_next()

    ab_h(0)
    for i in range(len(iters)):
        if i + 1 < len(iters):
            ab_h(i + 1)
        w2d(i)
    return ["moe_acc0", "moe_acc1", "moe_r2i0", "moe_r2i1"]


def emit_l1a_pre(P, io):
    identf, B = P.identf, P.banks
    wq = sb(P, [128, 8 * 1024], BF16, "wq")
    wk = sb(P, [128, 8 * 1024], BF16, "wk")
    wv = sb(P, [128, 8 * 1024], BF16, "wv")
    bq = sb(P, [128, D], F32, "bqb")
    bk = sb(P, [128, D], F32, "bkb")
    bv = sb(P, [128, D], F32, "bvb")
    cosb = sb(P, [128, 32 * 32], F32, "cosb")
    sinb = sb(P, [128, 32 * 32], F32, "sinb")
    cosq = sb(P, [128, 32 * 32], F32, "cosq")
    sinq = sb(P, [128, 32 * 32], F32, "sinq")
    scl = sb(P, [128, 1], F32, "scl")
    ld = P.stream("l1p_const", "pool", serial=True)
    P.dma(ld, wq[:], io["wqT_pm"], writes=[wq.k()])
    P.dma(ld, wk[:], io["wkT_pm"], writes=[wk.k()])
    P.dma(ld, wv[:], io["wvT_pm"], writes=[wv.k()])
    P.dma(ld, bq[:], io["bq_b"], writes=[bq.k()])
    P.dma(ld, bk[:], io["bk_b"], writes=[bk.k()])
    P.dma(ld, bv[:], io["bv_b"], writes=[bv.k()])
    P.dma(ld, scl[:], io["l1scale"], writes=[scl.k()])
    P.dma(ld, cosb[:].rearrange("p (tt c) -> p tt c", tt=32),
          io["l1a_cos"].rearrange("(tt p) c -> p tt c", p=128), writes=[cosb.k()])
    P.dma(ld, sinb[:].rearrange("p (tt c) -> p tt c", tt=32),
          io["l1a_sin"].rearrange("(tt p) c -> p tt c", p=128), writes=[sinb.k()])
    P.emit("dve", lambda v: v.tensor_scalar_mul(cosq[:], cosb[:], scl[:, 0:1]),
           reads=[cosb.k(), scl.k()], writes=[cosq.k()])
    P.emit("dve", lambda v: v.tensor_scalar_mul(sinq[:], sinb[:], scl[:, 0:1]),
           reads=[sinb.k(), scl.k()], writes=[sinq.k()])

    xt = [sb(P, [128, D], F32, f"ax{b}") for b in range(2)]
    xn_ = [sb(P, [128, D], F32, f"axn{b}") for b in range(2)]
    sqt_ = [sb(P, [128, D], F32, f"asq{b}") for b in range(2)]
    stats_ = [sb(P, [128, 8], F32, f"ast{b}") for b in range(2)]
    xnT_ = [sb(P, [128, 8 * 128], BF16, f"axnT{b}") for b in range(2)]
    qf_ = [sb(P, [128, D], F32, f"aqf{b}") for b in range(2)]
    qr_ = [sb(P, [128, D], F32, f"aqr{b}") for b in range(2)]
    te_ = [sb(P, [128, D], F32, f"ate{b}") for b in range(2)]
    qTm = sb(P, [128, 8 * MACRO], BF16, "aqTm")
    kTm = sb(P, [128, 8 * MACRO], BF16, "akTm")
    sq2 = sb(P, [128, D], F32, "asq2")
    nrm = sb(P, [128, 16], F32, "anrm")
    qmx = sb(P, [128, 16], F32, "aqmx")
    kmx = sb(P, [128, 16], F32, "akmx")
    P.emit("dve", lambda v: v.memset(qmx[:], 0.0), writes=[qmx.k()])
    P.emit("dve", lambda v: v.memset(kmx[:], 0.0), writes=[kmx.k()])
    xst = [P.stream(f"l1p_x{b}", "sp") for b in range(2)]
    vst = [P.stream(f"l1p_v{b}", "pool") for b in range(2)]
    qst = P.stream("l1p_q", "sp", serial=True)
    kst = P.stream("l1p_k", "sp", serial=True)

    def rope(dst_bf, src_f32, cos_t, sin_t, tt, keys_in, key_out, te):
        cs = cos_t[:, (tt % 32) * 32:(tt % 32) * 32 + 32]
        sn = sin_t[:, (tt % 32) * 32:(tt % 32) * 32 + 32]
        cs3 = cs.rearrange("p (o c) -> p o c", o=1).broadcast_to([128, 16, 32])
        sn3 = sn.rearrange("p (o c) -> p o c", o=1).broadcast_to([128, 16, 32])
        sv = src_f32.rearrange("p (h c) -> p h c", h=16)
        se = sv[:, :, 0:64:2]
        so = sv[:, :, 1:64:2]
        dv = dst_bf.rearrange("p (h c) -> p h c", h=16)
        de = dv[:, :, 0:64:2]
        do = dv[:, :, 1:64:2]
        t3 = te[:].rearrange("p (h c) -> p h c", h=16)
        t_a = t3[:, :, 0:32]
        t_b = t3[:, :, 32:64]
        P.emit("dve", lambda v: v.tensor_mul(t_a, se, cs3),
               reads=keys_in + [cos_t.k()], writes=[te.k("a")])
        P.emit("dve", lambda v: v.tensor_mul(t_b, so, sn3),
               reads=keys_in + [sin_t.k()], writes=[te.k("b")])
        P.emit("dve", lambda v: v.tensor_sub(de, t_a, t_b),
               reads=[te.k("a"), te.k("b")], writes=[key_out])
        P.emit("dve", lambda v: v.tensor_mul(t_a, se, sn3),
               reads=keys_in + [sin_t.k()], writes=[te.k("a")])
        P.emit("dve", lambda v: v.tensor_mul(t_b, so, cs3),
               reads=keys_in + [cos_t.k()], writes=[te.k("b")])
        P.emit("dve", lambda v: v.tensor_add(do, t_a, t_b),
               reads=[te.k("a"), te.k("b")], writes=[key_out])

    for m in range(NMAC):
        for tt in range(4):
            ti = m * 4 + tt
            bb = ti % 2
            XT = xt[bb]
            xn, sqt, stats = xn_[bb], sqt_[bb], stats_[bb]
            xnT, qf, qr, te = xnT_[bb], qf_[bb], qr_[bb], te_[bb]
            t0 = ti * 128
            P.dma(xst[ti % 2], XT[:], io["res2"][t0:t0 + 128, :],
                  writes=[XT.k()])
            layernorm_tile(P, xn[:], XT[:], sqt, stats, [XT.k()], xn.k())
            for dk in range(8):
                pb = B[dk % 2]
                P.emit("pe", lambda t, dk=dk, pb=pb, xn=xn: t.transpose(
                    pb[:, 0:128], xn[:, dk * 128:(dk + 1) * 128], identf[:]),
                    reads=[xn.k(), identf.k()], writes=[pb.k()])
                P.emit("act", lambda a, dk=dk, pb=pb, xnT=xnT: a.copy(
                    xnT[:, dk * 128:(dk + 1) * 128], pb[:, 0:128]),
                    reads=[pb.k()], writes=[xnT.k(dk)])
            for proj, (W, bias, dstT) in enumerate(
                    ((wq, bq, qTm), (wk, bk, kTm), (wv, bv, None))):
                for half in range(2):
                    pp = B[2 + half]
                    for dk in range(8):
                        P.emit("pe", lambda t, dk=dk, half=half, pp=pp, W=W,
                               xnT=xnT:
                               t.matmul(pp[:], xnT[:, dk * 128:(dk + 1) * 128],
                                        W[:, dk * 1024 + half * 512:
                                          dk * 1024 + (half + 1) * 512],
                                        start=(dk == 0), stop=(dk == 7)),
                               reads=[xnT.k(dk), W.k()], writes=[pp.k()])
                    P.emit("dve", lambda v, half=half, pp=pp, bias=bias, qf=qf:
                           v.tensor_add(
                        qf[:, half * 512:(half + 1) * 512], pp[:],
                        bias[:, half * 512:(half + 1) * 512]),
                        reads=[pp.k(), bias.k()], writes=[qf.k()])
                if proj < 2:
                    if proj == 0:
                        rope(qr[:], qf[:], cosq, sinq, ti, [qf.k()], qr.k(), te)
                        mxacc = qmx
                    else:
                        rope(qr[:], qf[:], cosb, sinb, ti, [qf.k()], qr.k(), te)
                        mxacc = kmx
                    P.emit("dve", lambda v, qr=qr: v.tensor_mul(sq2[:], qr[:], qr[:]),
                           reads=[qr.k()], writes=[sq2.k()])
                    P.emit("dve", lambda v: v.tensor_reduce(
                        nrm[:, 0:16],
                        sq2[:].rearrange("p (g c) -> p g c", g=16),
                        AX.X, ALU.add),
                        reads=[sq2.k()], writes=[nrm.k(0)])
                    P.emit("dve", lambda v, mxacc=mxacc: v.tensor_max(
                        mxacc[:], mxacc[:], nrm[:, 0:16]),
                        reads=[mxacc.k(), nrm.k(0)], writes=[mxacc.k()])
                else:
                    P.dma(vst[ti % 2], io["vT1"][t0:t0 + 128, :], qf[:],
                          reads=[qf.k()])
                    continue
                for blk in range(8):
                    pb = B[4 + blk % 2]
                    P.emit("pe", lambda t, blk=blk, pb=pb, qr=qr: t.transpose(
                        pb[:, 0:128], qr[:, blk * 128:(blk + 1) * 128], identf[:]),
                        reads=[qr.k(), identf.k()], writes=[pb.k()])
                    P.emit("act", lambda a, blk=blk, tt=tt, pb=pb, dstT=dstT: a.copy(
                        dstT[:, blk * MACRO + tt * 128: blk * MACRO + (tt + 1) * 128],
                        pb[:, 0:128]),
                        reads=[pb.k()], writes=[dstT.k(blk)])
        for blk in range(8):
            P.dma(qst, io["qT"][blk][:, m * MACRO:(m + 1) * MACRO],
                  qTm[:, blk * MACRO:(blk + 1) * MACRO], reads=[qTm.k(blk)])
            P.dma(kst, io["kT"][blk][:, m * MACRO:(m + 1) * MACRO],
                  kTm[:, blk * MACRO:(blk + 1) * MACRO], reads=[kTm.k(blk)])
    B2 = P.banks
    for src, col in ((qmx, 0), (kmx, 1)):
        pb = B2[col]
        P.emit("pe", lambda t, src=src, pb=pb: t.transpose(
            pb[0:16, 0:128], src[:], identf[:]),
            reads=[src.k(), identf.k()], writes=[pb.k()])
        P.emit("dve", lambda v, col=col, pb=pb: v.reduce_max(
            nrm[0:16, 2 + col:3 + col], pb[0:16, 0:128], axis=AX.X),
            reads=[pb.k()], writes=[nrm.k(("gm", col))])
    P.emit("act", lambda a: a.activation(nrm[0:16, 4:6], nrm[0:16, 2:4], AF.Sqrt),
           reads=[nrm.k(("gm", 0)), nrm.k(("gm", 1))], writes=[nrm.k("sq")])
    P.emit("dve", lambda v: v.tensor_mul(nrm[0:16, 6:7], nrm[0:16, 4:5],
                                         nrm[0:16, 5:6]),
           reads=[nrm.k("sq")], writes=[nrm.k("M")])
    P.emit("dve", lambda v: v.tensor_scalar(nrm[0:16, 7:8], nrm[0:16, 6:7],
                                            -1.0, 40.0, ALU.mult, ALU.add),
           reads=[nrm.k("M")], writes=[nrm.k("bias")])
    mst = P.stream("l1p_M", "sp", serial=True)
    P.dma(mst, io["attM"][0:16, 0:1], nrm[0:16, 7:8], reads=[nrm.k("bias")])
    return []


DBGH = [False]


def emit_l1a_att(P, io):
    B = P.banks
    NH = 16
    identf = P.identf
    vaug = sb(P, [128, 32 * 1040], BF16, "avaug")
    vtmp = sb(P, [128, 1024], BF16, "avtmp")
    mskl = sb(P, [128, 128], F32, "amaskl")
    biasb = sb(P, [128, NH], F32, "abias")
    ld = P.stream("l1t_const", "sp", serial=True)
    P.dma(ld, mskl[:], io["mask_low"], writes=[mskl.k()])
    for h in range(NH):
        P.dma(ld, biasb[:, h:h + 1],
              io["attM"][h:h + 1, 0:1].broadcast_to([128, 1]),
              writes=[biasb.k()])
    P.emit("dve", lambda v: v.memset(vaug[:], 1.0), writes=[vaug.k("ones")])
    vst = P.stream("l1t_v", "sp", serial=True)
    for jt in range(32):
        P.dma(vst, vtmp[:], io["vT1"][jt * 128:(jt + 1) * 128, :],
              writes=[vtmp.k()])
        P.emit("dve", lambda v, jt=jt: v.tensor_copy(
            vaug[:, jt * 1040:(jt + 1) * 1040]
            .rearrange("p (h c) -> p h c", h=16)[:, :, 0:64],
            vtmp[:].rearrange("p (h c) -> p h c", h=16)),
            reads=[vtmp.k(), vaug.k("ones")], writes=[vaug.k(jt)])
    qh = [sb(P, [64, T], BF16, f"aqh{b}") for b in range(2)]
    kh = [sb(P, [64, T], BF16, f"akh{b}") for b in range(2)]
    qhst = [P.stream(f"l1t_q{b}", "sp") for b in range(2)]
    khst = [P.stream(f"l1t_k{b}", "sp") for b in range(2)]
    pst = [sb(P, [128, T], BF16, f"apst{b}") for b in range(2)]
    ot = [sb(P, [128, 64], F32, f"aot{b}") for b in range(2)]
    rcp = sb(P, [128, 4], F32, "arcp")
    oTf = sb(P, [128, 8 * T], BF16, "aoTf")
    osst = P.stream("l1t_o", "sp", serial=True)

    def load_head(h):
        b = h % 2
        P.dma(qhst[b], qh[b][:], io["qT"][h // 2][64 * (h % 2):64 * (h % 2) + 64, :],
              writes=[qh[b].k()])
        P.dma(khst[b], kh[b][:], io["kT"][h // 2][64 * (h % 2):64 * (h % 2) + 64, :],
              writes=[kh[b].k()])

    load_head(0)
    load_head(1)

    strips = [(h, qt) for h in range(NH) for qt in range(32)]

    def emit_ST(i):
        h, qt = strips[i]
        b = h % 2
        QH, KH = qh[b], kh[b]
        PS = pst[i % 2]
        jn = qt + 1
        for b0 in range(0, jn, 4):
            bn = min(4, jn - b0)
            pb = B[(b0 // 4) % 4]
            for r in range(bn):
                jt = b0 + r
                P.emit("pe", lambda t, jt=jt, r=r, qt=qt, pb=pb, QH=QH, KH=KH:
                       t.matmul(pb[:, r * 128:(r + 1) * 128],
                                KH[:, jt * 128:(jt + 1) * 128],
                                QH[:, qt * 128:(qt + 1) * 128],
                                start=True, stop=True),
                       reads=[QH.k(), KH.k()], writes=[pb.k()])
                if jt == qt:
                    P.emit("dve", lambda v, r=r, pb=pb: v.tensor_add(
                        pb[:, r * 128:(r + 1) * 128],
                        pb[:, r * 128:(r + 1) * 128], mskl[:]),
                        reads=[pb.k(), mskl.k()], writes=[pb.k()])
            P.emit("act", lambda a, b0=b0, bn=bn, h=h, pb=pb, PS=PS:
                   a.activation(PS[:, b0 * 128:(b0 + bn) * 128],
                                pb[:, 0:bn * 128], AF.Exp,
                                bias=biasb[:, h:h + 1]),
                   reads=[pb.k(), biasb.k()], writes=[PS.k(b0 // 4)])

    def emit_AV(i):
        h, qt = strips[i]
        PS = pst[i % 2]
        jn = qt + 1
        pO = B[4 + i % 2]
        for jt in range(jn):
            P.emit("pe", lambda t, jt=jt, h=h, pO=pO, PS=PS: t.matmul(
                pO[:, 0:65], PS[:, jt * 128:(jt + 1) * 128],
                vaug[:, jt * 1040 + h * 65: jt * 1040 + (h + 1) * 65],
                start=(jt == 0), stop=(jt == jn - 1)),
                reads=[PS.k(jt // 4), vaug.k(jt), vaug.k("ones")],
                writes=[pO.k()])
        OT = ot[i % 2]
        P.emit("dve", lambda v, pO=pO: v.reciprocal(rcp[:, 0:1], pO[:, 64:65]),
               reads=[pO.k()], writes=[rcp.k()])
        P.emit("dve", lambda v, pO=pO, OT=OT: v.tensor_scalar_mul(
            OT[:], pO[:, 0:64], rcp[:, 0:1]),
            reads=[pO.k(), rcp.k()], writes=[OT.k()])
        pt_ = B[6 + i % 2]
        P.emit("pe", lambda t, pt_=pt_, OT=OT: t.transpose(
            pt_[0:64, 0:128], OT[:], identf[:]),
            reads=[OT.k(), identf.k()], writes=[pt_.k()])
        po_part = 64 * (h % 2)
        P.emit("act", lambda a, h=h, qt=qt, pt_=pt_, po_part=po_part: a.copy(
            oTf[po_part:po_part + 64, (h // 2) * T + qt * 128:
                (h // 2) * T + (qt + 1) * 128],
            pt_[0:64, 0:128]),
            reads=[pt_.k()], writes=[oTf.k((h, qt))])
        if qt == 31:
            if h + 2 < NH:
                load_head(h + 2)
            if h % 2 == 1:
                blk = h // 2
                P.dma(osst, io["oT2"][blk], oTf[:, blk * T:(blk + 1) * T],
                      reads=[oTf.k((h, g2)) for g2 in range(32)]
                      + [oTf.k((h - 1, g2)) for g2 in range(32)])

    emit_ST(0)
    for i in range(len(strips)):
        if i + 1 < len(strips):
            emit_ST(i + 1)
        emit_AV(i)
    return ["l1t_o"]


def emit_l1a_wo(P, io):
    B = P.banks
    wo = sb(P, [128, 8 * 1024], BF16, "wo")
    bo = sb(P, [128, D], F32, "bob")
    ld = P.stream("l1w_const", "pool", serial=True)
    P.dma(ld, wo[:], io["woT_pm"], writes=[wo.k()])
    P.dma(ld, bo[:], io["bo_b"], writes=[bo.k()])
    oT = [sb(P, [128, 8 * 128], BF16, f"woTt{b}") for b in range(2)]
    xt = [sb(P, [128, D], F32, f"wxt{b}") for b in range(2)]
    rt = [sb(P, [128, D], F32, f"wrt{b}") for b in range(2)]
    ost = [P.stream(f"l1w_o{b}", "sp") for b in range(2)]
    xst = [P.stream(f"l1w_x{b}", "sp") for b in range(2)]
    rst = [P.stream(f"l1w_r{b}", "sp") for b in range(2)]
    for ti in range(32):
        t0 = ti * 128
        b = ti % 2
        OT, XT, RT = oT[b], xt[b], rt[b]
        P.dma(ost[b], OT[:].rearrange("p (blk t) -> p blk t", blk=8),
              io["oT2"][:, :, t0:t0 + 128].rearrange("blk p t -> p blk t"),
              writes=[OT.k()])
        P.dma(xst[b], XT[:], io["res2"][t0:t0 + 128, :], writes=[XT.k()])
        for half in range(2):
            pp = B[2 + half]
            for blk in range(8):
                P.emit("pe", lambda t, blk=blk, half=half, pp=pp, OT=OT: t.matmul(
                    pp[:], OT[:, blk * 128:(blk + 1) * 128],
                    wo[:, blk * 1024 + half * 512: blk * 1024 + (half + 1) * 512],
                    start=(blk == 0), stop=(blk == 7)),
                    reads=[OT.k(), wo.k()], writes=[pp.k()])
            P.emit("dve", lambda v, half=half, pp=pp, RT=RT: v.tensor_add(
                RT[:, half * 512:(half + 1) * 512], pp[:],
                bo[:, half * 512:(half + 1) * 512]),
                reads=[pp.k(), bo.k()], writes=[RT.k()])
        P.emit("dve", lambda v, RT=RT, XT=XT: v.tensor_add(RT[:], RT[:], XT[:]),
               reads=[RT.k(), XT.k()], writes=[RT.k()])
        P.dma(rst[b], io["res3"][t0:t0 + 128, :], RT[:], reads=[RT.k()])
    return ["l1w_r0", "l1w_r1"]


def emit_l1f_pre(P, io):
    xt = [sb(P, [128, D], F32, f"fpx{b}") for b in range(2)]
    zn = [sb(P, [128, D], F32, f"fpz{b}") for b in range(2)]
    sqt = sb(P, [128, D], F32, "fpsq")
    stats = sb(P, [128, 8], F32, "fpst")
    xst = [P.stream(f"f_x{b}", "sp") for b in range(2)]
    zst = [P.stream(f"f_z{b}", "sp") for b in range(2)]
    for ti in range(32):
        b = ti % 2
        XT, ZN = xt[b], zn[b]
        t0 = ti * 128
        P.dma(xst[b], XT[:], io["res3"][t0:t0 + 128, :], writes=[XT.k()])
        layernorm_tile(P, ZN[:], XT[:], sqt, stats, [XT.k()], ZN.k())
        P.dma(zst[b], io["za"][t0:t0 + 128, :], ZN[:], reads=[ZN.k()])
    return []


def shift_load(P, io, zname, dst, ti, streams, key):
    t0 = ti * 128
    P.dma(streams[0], dst[:, 0:512], io[zname][t0:t0 + 128, 0:512],
          writes=[key])
    if ti == 0:
        P.emit("dve", lambda v: v.memset(dst[0:1, 512:1024], 0.0),
               writes=[key])
        P.dma(streams[1], dst[1:128, 512:1024], io[zname][0:127, 512:1024],
              writes=[key])
    else:
        P.dma(streams[1], dst[:, 512:1024], io[zname][t0 - 1:t0 + 127, 512:1024],
              writes=[key])


def emit_l1f_qkv(P, io, l, zname):
    identf, B = P.identf, P.banks
    wqkv = sb(P, [128, 8 * 1536], BF16, "fw")
    qsb = sb(P, [128, 512], F32, "fqs")
    ksb = sb(P, [128, 512], F32, "fks")
    tcq = sb(P, [128, 32 * 64], F32, "ftcq")
    tsq = sb(P, [128, 32 * 64], F32, "ftsq")
    tck = sb(P, [128, 32 * 64], F32, "ftck")
    tsk = sb(P, [128, 32 * 64], F32, "ftsk")
    ld = P.stream("f_const", "pool", serial=True)
    P.dma(ld, wqkv[:], io["f_wqkv"][l], writes=[wqkv.k()])
    P.dma(ld, qsb[:], io["f_qs_b"][l], writes=[qsb.k()])
    P.dma(ld, ksb[:], io["f_ks_b"][l], writes=[ksb.k()])
    for nm_, t_ in (("xpos_cq", tcq), ("xpos_sq", tsq),
                    ("xpos_ck", tck), ("xpos_sk", tsk)):
        P.dma(ld, t_[:].rearrange("p (tt c) -> p tt c", tt=32),
              io[nm_].rearrange("(tt p) c -> p tt c", p=128), writes=[t_.k()])

    ht = [sb(P, [128, D], F32, f"fht{b}") for b in range(2)]
    hT_ = [sb(P, [128, 8 * 128], BF16, f"fhT{b}") for b in range(2)]
    qf_ = [sb(P, [128, 512], F32, f"fqf{b}") for b in range(2)]
    sq2_ = [sb(P, [128, 512], F32, f"fsq2{b}") for b in range(2)]
    red_ = [sb(P, [128, 24], F32, f"fred{b}") for b in range(2)]
    qx_ = [sb(P, [128, 512], F32, f"fqx{b}") for b in range(2)]
    te_ = [sb(P, [128, 512], F32, f"fte{b}") for b in range(2)]
    qTm = sb(P, [128, 4 * MACRO], BF16, "fqTm")
    kTm = sb(P, [128, 4 * MACRO], BF16, "fkTm")
    hst = [[P.stream(f"f_h{a}{b}", "sp") for b in range(2)] for a in range(2)]
    vst = [P.stream(f"f_v{b}", "pool") for b in range(2)]
    qst = P.stream("f_qT", "sp", serial=True)
    kst = P.stream("f_kT", "sp", serial=True)

    def l2n_xpos(src_key, ct, st_, sb_, dstT, tt, qf, sq2, red, qx, te):
        P.emit("dve", lambda v: v.tensor_mul(sq2[:], qf[:], qf[:]),
               reads=[src_key], writes=[sq2.k()])
        P.emit("dve", lambda v: v.tensor_reduce(
            red[:, 0:8], sq2[:].rearrange("p (g c) -> p g c", g=8),
            AX.X, ALU.add),
            reads=[sq2.k()], writes=[red.k(0)])
        P.emit("act", lambda a: a.activation(red[:, 8:16], red[:, 0:8], AF.Sqrt),
               reads=[red.k(0)], writes=[red.k("n")])
        P.emit("dve", lambda v: v.tensor_scalar_max(red[:, 8:16], red[:, 8:16],
                                                    1e-12),
               reads=[red.k("n")], writes=[red.k("n")])
        P.emit("dve", lambda v: v.reciprocal(red[:, 16:24], red[:, 8:16]),
               reads=[red.k("n")], writes=[red.k("rn")])
        rb = red[:, 16:24].rearrange("p (h o) -> p h o", h=8).broadcast_to([128, 8, 64])
        q3 = qf[:].rearrange("p (h c) -> p h c", h=8)
        P.emit("dve", lambda v: v.tensor_mul(q3, q3, rb),
               reads=[src_key, red.k("rn")], writes=[src_key])
        P.emit("dve", lambda v, sb_=sb_: v.tensor_mul(qf[:], qf[:], sb_[:]),
               reads=[src_key, sb_.k()], writes=[src_key])
        cs = ct[:, (tt % 32) * 64:(tt % 32) * 64 + 64]
        sn = st_[:, (tt % 32) * 64:(tt % 32) * 64 + 64]
        cs3 = cs.rearrange("p (o c) -> p o c", o=1).broadcast_to([128, 8, 64])
        qx3 = qx[:].rearrange("p (h c) -> p h c", h=8)
        P.emit("dve", lambda v: v.tensor_mul(qx3, q3, cs3),
               reads=[src_key, ct.k()], writes=[qx.k()])
        sn1 = sn[:, 0:32].rearrange("p (o c) -> p o c", o=1).broadcast_to([128, 8, 32])
        sn2 = sn[:, 32:64].rearrange("p (o c) -> p o c", o=1).broadcast_to([128, 8, 32])
        te3 = te[:].rearrange("p (h c) -> p h c", h=8)
        P.emit("dve", lambda v: v.tensor_mul(te3[:, :, 0:32], q3[:, :, 32:64], sn1),
               reads=[src_key, st_.k()], writes=[te.k()])
        P.emit("dve", lambda v: v.tensor_mul(te3[:, :, 32:64], q3[:, :, 0:32], sn2),
               reads=[src_key, st_.k()], writes=[te.k()])
        P.emit("dve", lambda v: v.tensor_add(qx[:], qx[:], te[:]),
               reads=[qx.k(), te.k()], writes=[qx.k()])
        for blk in range(4):
            pb = B[4 + blk % 2]
            P.emit("pe", lambda t, blk=blk, pb=pb: t.transpose(
                pb[:, 0:128], qx[:, blk * 128:(blk + 1) * 128], identf[:]),
                reads=[qx.k(), identf.k()], writes=[pb.k()])
            P.emit("act", lambda a, blk=blk, tt=tt, pb=pb, dstT=dstT: a.copy(
                dstT[:, blk * MACRO + (tt % 4) * 128: blk * MACRO + (tt % 4 + 1) * 128],
                pb[:, 0:128]),
                reads=[pb.k()], writes=[dstT.k(blk)])

    for m in range(NMAC):
        for tt in range(4):
            ti = m * 4 + tt
            bb = ti % 2
            HT = ht[bb]
            hT, qf, sq2, red = hT_[bb], qf_[bb], sq2_[bb], red_[bb]
            qx, te = qx_[bb], te_[bb]
            shift_load(P, io, zname, HT, ti, hst[ti % 2], HT.k())
            for dk in range(8):
                pb = B[dk % 2]
                P.emit("pe", lambda t, dk=dk, pb=pb, HT=HT: t.transpose(
                    pb[:, 0:128], HT[:, dk * 128:(dk + 1) * 128], identf[:]),
                    reads=[HT.k(), identf.k()], writes=[pb.k()])
                P.emit("act", lambda a, dk=dk, pb=pb, hT=hT: a.copy(
                    hT[:, dk * 128:(dk + 1) * 128], pb[:, 0:128]),
                    reads=[pb.k()], writes=[hT.k(dk)])
            for proj in range(3):
                pp = B[2 + proj % 2]
                for dk in range(8):
                    P.emit("pe", lambda t, dk=dk, proj=proj, pp=pp, hT=hT: t.matmul(
                        pp[:], hT[:, dk * 128:(dk + 1) * 128],
                        wqkv[:, dk * 1536 + proj * 512: dk * 1536 + (proj + 1) * 512],
                        start=(dk == 0), stop=(dk == 7)),
                        reads=[hT.k(dk), wqkv.k()], writes=[pp.k()])
                if proj == 2:
                    t0 = ti * 128
                    P.emit("act", lambda a, pp=pp, qf=qf: a.copy(qf[:], pp[:]),
                           reads=[pp.k()], writes=[qf.k()])
                    P.dma(vst[ti % 2], io["v2T"][t0:t0 + 128, :], qf[:],
                          reads=[qf.k()])
                else:
                    P.emit("act", lambda a, pp=pp, qf=qf: a.copy(qf[:], pp[:]),
                           reads=[pp.k()], writes=[qf.k()])
                    if proj == 0:
                        l2n_xpos(qf.k(), tcq, tsq, qsb, qTm, ti,
                                 qf, sq2, red, qx, te)
                    else:
                        l2n_xpos(qf.k(), tck, tsk, ksb, kTm, ti,
                                 qf, sq2, red, qx, te)
        for blk in range(4):
            P.dma(qst, io["q2T"][blk][:, m * MACRO:(m + 1) * MACRO],
                  qTm[:, blk * MACRO:(blk + 1) * MACRO], reads=[qTm.k(blk)])
            P.dma(kst, io["k2T"][blk][:, m * MACRO:(m + 1) * MACRO],
                  kTm[:, blk * MACRO:(blk + 1) * MACRO], reads=[kTm.k(blk)])
    return []


def emit_l1f_att(P, io, l):
    B = P.banks
    vsb = sb(P, [128, 32 * 512], BF16, "fvsb")
    msku = sb(P, [128, 128], F32, "fmsku")
    mskl = sb(P, [128, 128], F32, "fmskl")
    ld = P.stream("f_tconst", "sp", serial=True)
    P.dma(ld, msku[:], io["mask_up"], writes=[msku.k()])
    P.dma(ld, mskl[:], io["mask_low"], writes=[mskl.k()])
    for jt in range(32):
        P.dma(ld, vsb[:, jt * 512:(jt + 1) * 512],
              io["v2T"][jt * 128:(jt + 1) * 128, :], writes=[vsb.k()])
    qh = [sb(P, [64, T], BF16, f"fqh{b}") for b in range(2)]
    kh = [sb(P, [64, T], BF16, f"fkh{b}") for b in range(2)]
    qhst = [P.stream(f"f_tq{b}", "sp") for b in range(2)]
    khst = [P.stream(f"f_tk{b}", "sp") for b in range(2)]
    Sst = [sb(P, [128, 1024], F32, f"fS{b}") for b in range(2)]
    Pst = [sb(P, [128, 1024], F32, f"fP{b}") for b in range(2)]
    PT = [sb(P, [128, 8 * 128], BF16, f"fPT{b}") for b in range(2)]
    red = [sb(P, [128, 16], F32, f"fres{b}") for b in range(2)]
    ot = [sb(P, [128, 64], F32, f"fot{b}") for b in range(2)]
    oTf = sb(P, [128, 4 * T], BF16, "foTf")
    osst = P.stream("f_to", "sp", serial=True)

    def load_head(h):
        b = h % 2
        P.dma(qhst[b], qh[b][:], io["q2T"][h // 2][64 * (h % 2):64 * (h % 2) + 64, :],
              writes=[qh[b].k()])
        P.dma(khst[b], kh[b][:], io["k2T"][h // 2][64 * (h % 2):64 * (h % 2) + 64, :],
              writes=[kh[b].k()])

    load_head(0)
    load_head(1)
    strips = [(h, g) for h in range(8) for g in range(32)]

    def geom(g):
        w = g // 4
        j0 = max(0, (w - 1) * 4)
        jn = g + 1 - j0
        return w, j0, jn

    def p_S(i):
        h, g = strips[i]
        b = h % 2
        QH, KH = qh[b], kh[b]
        SS = Sst[i % 2]
        w, j0, jn = geom(g)
        ext = jn * 128
        nch = (ext + 511) // 512
        for jc in range(nch):
            wd = min(512, ext - jc * 512)
            pS = B[jc % 2]
            P.emit("pe", lambda t, g=g, j0=j0, jc=jc, wd=wd, pS=pS, QH=QH,
                   KH=KH: t.matmul(
                pS[:, 0:wd], QH[:, g * 128:(g + 1) * 128],
                KH[:, j0 * 128 + jc * 512: j0 * 128 + jc * 512 + wd],
                start=True, stop=True),
                reads=[QH.k(), KH.k()], writes=[pS.k()])
            P.emit("act", lambda a, jc=jc, wd=wd, pS=pS, SS=SS: a.copy(
                SS[:, jc * 512:jc * 512 + wd], pS[:, 0:wd]),
                reads=[pS.k()], writes=[SS.k(jc)])
        dg = (g - j0) * 128
        P.emit("dve", lambda v, dg=dg, SS=SS: v.tensor_add(
            SS[:, dg:dg + 128], SS[:, dg:dg + 128], msku[:]),
            reads=[SS.k(dg // 512), msku.k()], writes=[SS.k(dg // 512)])
        if w > 0:
            P.emit("dve", lambda v, SS=SS: v.tensor_add(
                SS[:, 0:128], SS[:, 0:128], mskl[:]),
                reads=[SS.k(0), mskl.k()], writes=[SS.k(0)])

    def p_soft(i):
        h, g = strips[i]
        SS, PS, RD = Sst[i % 2], Pst[i % 2], red[i % 2]
        w, j0, jn = geom(g)
        ext = jn * 128
        nch = (ext + 511) // 512
        for jc in range(nch):
            wd = min(512, ext - jc * 512)
            P.emit("dve", lambda v, jc=jc, wd=wd, SS=SS, RD=RD: v.reduce_max(
                RD[:, jc:jc + 1], SS[:, jc * 512:jc * 512 + wd], axis=AX.X),
                reads=[SS.k(jc)], writes=[RD.k(jc)])
        P.emit("dve", lambda v, nch=nch, RD=RD: v.tensor_reduce(
            RD[:, 8:9], RD[:, 0:nch], AX.X, ALU.max),
            reads=[RD.k(jc) for jc in range(nch)], writes=[RD.k("nm")])
        P.emit("dve", lambda v, RD=RD: v.tensor_scalar_mul(RD[:, 8:9],
                                                           RD[:, 8:9], -1.0),
               reads=[RD.k("nm")], writes=[RD.k("nm")])
        P.emit("act", lambda a, ext=ext, SS=SS, PS=PS, RD=RD: a.activation(
            PS[:, 0:ext], SS[:, 0:ext], AF.Exp, bias=RD[:, 8:9],
            accum_out=RD[:, 9:10]),
            reads=[RD.k("nm")] + [SS.k(jc) for jc in range(nch)],
            writes=[PS.k(), RD.k("rs")])
        P.emit("dve", lambda v, RD=RD: v.reciprocal(RD[:, 10:11], RD[:, 9:10]),
               reads=[RD.k("rs")], writes=[RD.k("rrs")])

    def p_AV(i):
        h, g = strips[i]
        PS, RD, OT = Pst[i % 2], red[i % 2], ot[i % 2]
        PTb = PT[i % 2]
        w, j0, jn = geom(g)
        for b0 in range(0, jn, 4):
            bn = min(4, jn - b0)
            pb = B[2 + (b0 // 4) % 2]
            for r in range(bn):
                jt = b0 + r
                P.emit("pe", lambda t, jt=jt, r=r, pb=pb, PS=PS: t.transpose(
                    pb[:, r * 128:(r + 1) * 128], PS[:, jt * 128:(jt + 1) * 128],
                    P.identf[:]),
                    reads=[PS.k(), P.identf.k()], writes=[pb.k()])
            P.emit("act", lambda a, b0=b0, bn=bn, pb=pb, PTb=PTb: a.copy(
                PTb[:, b0 * 128:(b0 + bn) * 128], pb[:, 0:bn * 128]),
                reads=[pb.k()], writes=[PTb.k(b0 // 4)])
        pO = B[4 + i % 2]
        for jt in range(jn):
            P.emit("pe", lambda t, jt=jt, j0=j0, h=h, pO=pO, PTb=PTb: t.matmul(
                pO[:, 0:64], PTb[:, jt * 128:(jt + 1) * 128],
                vsb[:, (j0 + jt) * 512 + h * 64: (j0 + jt) * 512 + (h + 1) * 64],
                start=(jt == 0), stop=(jt == jn - 1)),
                reads=[PTb.k(jt // 4), vsb.k()], writes=[pO.k()])
        P.emit("dve", lambda v, pO=pO, RD=RD, OT=OT: v.tensor_scalar_mul(
            OT[:], pO[:, 0:64], RD[:, 10:11]),
            reads=[pO.k(), RD.k("rrs")], writes=[OT.k()])
        pt_ = B[6 + i % 2]
        P.emit("pe", lambda t, pt_=pt_, OT=OT: t.transpose(
            pt_[0:64, 0:128], OT[:], P.identf[:]),
            reads=[OT.k(), P.identf.k()], writes=[pt_.k()])
        po_part = 64 * (h % 2)
        P.emit("act", lambda a, h=h, g=g, pt_=pt_, po_part=po_part: a.copy(
            oTf[po_part:po_part + 64, (h // 2) * T + g * 128:
                (h // 2) * T + (g + 1) * 128],
            pt_[0:64, 0:128]),
            reads=[pt_.k()], writes=[oTf.k((h, g))])
        if g == 31:
            if h + 2 < 8:
                load_head(h + 2)
            if h % 2 == 1:
                blk = h // 2
                P.dma(osst, io["o2T"][blk], oTf[:, blk * T:(blk + 1) * T],
                      reads=[oTf.k((h, g2)) for g2 in range(32)]
                      + [oTf.k((h - 1, g2)) for g2 in range(32)])

    p_S(0)
    for i in range(len(strips)):
        p_soft(i)
        if i + 1 < len(strips):
            p_S(i + 1)
        p_AV(i)
    return []


def emit_l1f_wo(P, io, l, zname, outname):
    B = P.banks
    wo = sb(P, [128, 4 * 1024], BF16, "fwo")
    ld = P.stream("f_const", "pool", serial=True)
    P.dma(ld, wo[:], io["f_wo"][l], writes=[wo.k()])
    oT = [sb(P, [128, 4 * 128], BF16, f"fwoT{b}") for b in range(2)]
    xt = [sb(P, [128, D], F32, f"fwx{b}") for b in range(2)]
    rt = [sb(P, [128, D], F32, f"fwr{b}") for b in range(2)]
    ost = [P.stream(f"f_wo{b}", "sp") for b in range(2)]
    xst = [P.stream(f"f_x{b}", "sp") for b in range(2)]
    rst = [P.stream(f"f_z{b}", "sp") for b in range(2)]
    for ti in range(32):
        t0 = ti * 128
        b = ti % 2
        OT, XT, RT = oT[b], xt[b], rt[b]
        P.dma(ost[b], OT[:].rearrange("p (blk t) -> p blk t", blk=4),
              io["o2T"][:, :, t0:t0 + 128].rearrange("blk p t -> p blk t"),
              writes=[OT.k()])
        P.dma(xst[b], XT[:], io[zname][t0:t0 + 128, :], writes=[XT.k()])
        for half in range(2):
            pp = B[2 + half]
            for blk in range(4):
                P.emit("pe", lambda t, blk=blk, half=half, pp=pp, OT=OT: t.matmul(
                    pp[:], OT[:, blk * 128:(blk + 1) * 128],
                    wo[:, blk * 1024 + half * 512: blk * 1024 + (half + 1) * 512],
                    start=(blk == 0), stop=(blk == 3)),
                    reads=[OT.k(), wo.k()], writes=[pp.k()])
            P.emit("dve", lambda v, half=half, pp=pp, RT=RT, XT=XT: v.tensor_add(
                RT[:, half * 512:(half + 1) * 512], pp[:],
                XT[:, half * 512:(half + 1) * 512]),
                reads=[pp.k(), XT.k()], writes=[RT.k()])
        P.dma(rst[b], io[outname][t0:t0 + 128, :], RT[:], reads=[RT.k()])
    return []


def emit_l1f_ffn(P, io, l, zname, outname, resname=None):
    identf, B = P.identf, P.banks
    lng = sb(P, [128, D], F32, "flng")
    lnb = sb(P, [128, D], F32, "flnb")
    b2b = sb(P, [128, D], F32, "fb2")
    b1a = sb(P, [128, 22], F32, "fb1a")
    b1g = sb(P, [128, 22], F32, "fb1g")
    ld = P.stream("f_const", "pool", serial=True)
    P.dma(ld, lng[:], io["f_lng_b"][l], writes=[lng.k()])
    P.dma(ld, lnb[:], io["f_lnb_b"][l], writes=[lnb.k()])
    P.dma(ld, b2b[:], io["f_b2_b"][l], writes=[b2b.k()])
    P.dma(ld, b1a[:], io["f_b1a"][l], writes=[b1a.k()])
    P.dma(ld, b1g[:], io["f_b1g"][l], writes=[b1g.k()])

    ht = [sb(P, [128, D], F32, f"ffh{b}") for b in range(2)]
    zt = [sb(P, [128, D], F32, f"ffz{b}") for b in range(2)]
    hn = sb(P, [128, D], F32, "ffhn")
    sqt = sb(P, [128, D], F32, "ffsq")
    stats = sb(P, [128, 8], F32, "ffst")
    hTm = sb(P, [128, 8 * MACRO], BF16, "ffhTm")
    hst = [[P.stream(f"f_h{a}{b}", "sp") for b in range(2)] for a in range(2)]
    zst = [P.stream(f"f_x{b}", "sp") for b in range(2)]
    r3st = [P.stream(f"f_r3{b}", "sp") for b in range(2)]
    ist = [P.stream(f"f_oi{b}", "sp") for b in range(2)]
    hTst = P.stream("f_hT", "sp", serial=True)
    for m in range(NMAC):
        for tt in range(4):
            ti = m * 4 + tt
            t0 = ti * 128
            b = ti % 2
            HT, ZT = ht[b], zt[b]
            shift_load(P, io, zname, HT, ti, hst[b], HT.k())
            P.dma(zst[b], ZT[:], io[zname][t0:t0 + 128, :], writes=[ZT.k()])
            layernorm_tile(P, hn[:], HT[:], sqt, stats, [HT.k()], hn.k(),
                           gamma=lng, beta=lnb)
            for dk in range(8):
                pb = B[dk % 2]
                P.emit("pe", lambda t, dk=dk, pb=pb: t.transpose(
                    pb[:, 0:128], hn[:, dk * 128:(dk + 1) * 128], identf[:]),
                    reads=[hn.k(), identf.k()], writes=[pb.k()])
                P.emit("act", lambda a, dk=dk, tt=tt, pb=pb: a.copy(
                    hTm[:, dk * MACRO + tt * 128: dk * MACRO + (tt + 1) * 128],
                    pb[:, 0:128]),
                    reads=[pb.k()], writes=[hTm.k(dk)])
            P.emit("dve", lambda v, ZT=ZT: v.tensor_add(ZT[:], ZT[:], b2b[:]),
                   reads=[ZT.k(), b2b.k()], writes=[ZT.k()])
            if resname is not None:
                RT = sqt
                P.dma(r3st[b], RT[:], io[resname][t0:t0 + 128, :],
                      writes=[RT.k()])
                P.emit("dve", lambda v, ZT=ZT, RT=RT: v.tensor_add(
                    ZT[:], ZT[:], RT[:]),
                    reads=[ZT.k(), RT.k()], writes=[ZT.k()])
            P.dma(ist[b], io[outname][t0:t0 + 128, :], ZT[:],
                  reads=[ZT.k()], writes=[("hbm_out", ti // 4, "init", tt)])
        P.dma(hTst, io["h2T"][m], hTm[:],
              reads=[hTm.k(dk) for dk in range(8)],
              writes=[("hbm_h2T", m)])

    w1r = [sb(P, [128, 6 * 8 * 128], BF16, f"fw1r{b}") for b in range(2)]
    w3r = [sb(P, [128, 6 * 8 * 128], BF16, f"fw3r{b}") for b in range(2)]
    w2r = [sb(P, [128, 6 * 1024], BF16, f"fw2r{b}") for b in range(2)]
    wst = [[P.stream(f"f_w{w}{b}", "pool") for b in range(2)] for w in range(3)]
    xbuf = [sb(P, [128, 8 * MACRO], BF16, f"ffxb{b}") for b in range(2)]
    xbst = [P.stream(f"f_xb{b}", "sp") for b in range(2)]
    hTc = [sb(P, [128, 6 * MACRO], BF16, f"ffhc{b}") for b in range(2)]
    delta = [sb(P, [128, 4 * D], F32, f"ffd{b}") for b in range(2)]
    dst_ = [P.stream(f"f_acc{b}", "pool") for b in range(2)]
    gel_ = [sb(P, [128, MACRO], F32, f"ffgel{b}") for b in range(2)]

    chunk_seq = [0]

    def load_chunk_next():
        seq = chunk_seq[0]
        if seq >= 4:
            return
        chunk_seq[0] += 1
        c0, cn = MOE_CHUNKS[seq]
        sl = seq % 2
        P.dma(wst[0][sl], w1r[sl][:, 0:cn * 8 * 128],
              io["f_w1a"][l][:, c0 * 8 * 128:(c0 + cn) * 8 * 128],
              writes=[w1r[sl].k()])
        P.dma(wst[1][sl], w3r[sl][:, 0:cn * 8 * 128],
              io["f_w1g"][l][:, c0 * 8 * 128:(c0 + cn) * 8 * 128],
              writes=[w3r[sl].k()])
        P.dma(wst[2][sl], w2r[sl][:, 0:cn * 1024],
              io["f_w2"][l][:, c0 * 1024:(c0 + cn) * 1024],
              writes=[w2r[sl].k()])

    load_chunk_next()
    load_chunk_next()
    iters = [(ci, m) for ci in range(4) for m in range(NMAC)]

    def ab_h(i):
        ci, m = iters[i]
        c0, cn = MOE_CHUNKS[ci]
        sl = ci % 2
        W1, W3 = w1r[sl], w3r[sl]
        xb = xbuf[i % 2]
        P.dma(xbst[i % 2], xb[:], io["h2T"][m],
              reads=[("hbm_h2T", m)], writes=[xb.k()])
        HT = hTc[i % 2]
        for hh in range(cn):
            pa, pg = B[hh % 2], B[2 + hh % 2]
            for k in range(8):
                P.emit("pe", lambda t, hh=hh, k=k, pa=pa, W1=W1, xb=xb: t.matmul(
                    pa[:], W1[:, (hh * 8 + k) * 128:(hh * 8 + k + 1) * 128],
                    xb[:, k * MACRO:(k + 1) * MACRO],
                    start=(k == 0), stop=(k == 7)),
                    reads=[W1.k(), xb.k()], writes=[pa.k()])
            for k in range(8):
                P.emit("pe", lambda t, hh=hh, k=k, pg=pg, W3=W3, xb=xb: t.matmul(
                    pg[:], W3[:, (hh * 8 + k) * 128:(hh * 8 + k + 1) * 128],
                    xb[:, k * MACRO:(k + 1) * MACRO],
                    start=(k == 0), stop=(k == 7)),
                    reads=[W3.k(), xb.k()], writes=[pg.k()])
            gel = gel_[hh % 2]
            P.emit("act", lambda a, hh=hh, ci=ci, c0=c0, pg=pg, gel=gel:
                   a.activation(
                gel[:], pg[:], AF.Gelu, bias=b1g[:, c0 + hh:c0 + hh + 1]),
                reads=[pg.k(), b1g.k()], writes=[gel.k()])
            P.emit("dve", lambda v, hh=hh, ci=ci, c0=c0, pa=pa, HT=HT, gel=gel:
                   v.scalar_tensor_tensor(
                HT[:, hh * MACRO:(hh + 1) * MACRO], pa[:],
                b1a[:, c0 + hh:c0 + hh + 1], gel[:], ALU.add, ALU.mult),
                reads=[pa.k(), b1a.k(), gel.k()], writes=[HT.k(hh)])

    def w2d(i):
        ci, m = iters[i]
        c0, cn = MOE_CHUNKS[ci]
        sl = ci % 2
        W2 = w2r[sl]
        HT = hTc[i % 2]
        DL = delta[i % 2]
        for tt in range(4):
            for dh in range(2):
                po = B[4 + (tt * 2 + dh) % 2]
                for k in range(cn):
                    P.emit("pe", lambda t, tt=tt, dh=dh, k=k, po=po, W2=W2,
                           HT=HT: t.matmul(
                        po[:], HT[:, k * MACRO + tt * 128: k * MACRO + (tt + 1) * 128],
                        W2[:, k * 1024 + dh * 512:k * 1024 + (dh + 1) * 512],
                        start=(k == 0), stop=(k == cn - 1)),
                        reads=[HT.k(k), W2.k()], writes=[po.k()])
                if (tt * 2 + dh) % 2 == 0:
                    P.emit("dve", lambda v, tt=tt, dh=dh, po=po, DL=DL:
                           v.tensor_copy(
                        DL[:, tt * D + dh * 512: tt * D + (dh + 1) * 512], po[:]),
                        reads=[po.k()], writes=[DL.k()])
                else:
                    P.emit("act", lambda a, tt=tt, dh=dh, po=po, DL=DL: a.copy(
                        DL[:, tt * D + dh * 512: tt * D + (dh + 1) * 512], po[:]),
                        reads=[po.k()], writes=[DL.k()])
        t0 = m * MACRO
        P.dma(dst_[i % 2], io[outname][t0:t0 + MACRO, :]
              .rearrange("(tt p) f -> p tt f", p=128),
              DL[:].rearrange("p (tt f) -> p tt f", tt=4),
              reads=[DL.k()] + [("hbm_out", m, "init", t) for t in range(4)],
              writes=[("hbm_out", m)], accum_op=ALU.add)
        if m == NMAC - 1:
            load_chunk_next()

    ab_h(0)
    for i in range(len(iters)):
        if i + 1 < len(iters):
            ab_h(i + 1)
        w2d(i)
    return ["f_acc0", "f_acc1", "f_oi0", "f_oi1"]


STAGES = [
    ("l0a", emit_l0a,
     ["xin", "ident", "inw_pm", "cmw_pm", "outw_pm", "inb_col", "dwb_col",
      "mixb_col", "ld_col", "outb_b", "l0g_b", "l0b_b"],
     ["res1"]),
    ("moe", emit_moe,
     ["res1", "ident", "gateT_pm", "gateb_b", "moe_w1", "moe_w3", "moe_w2"],
     ["res2", "xlnT"]),
    ("l1a_pre", emit_l1a_pre,
     ["res2", "ident", "wqT_pm", "wkT_pm", "wvT_pm", "bq_b", "bk_b", "bv_b",
      "l1scale", "l1a_cos", "l1a_sin"],
     ["qT", "kT", "vT1", "attM"]),
    ("l1a_att", emit_l1a_att,
     ["qT", "kT", "vT1", "attM", "ident", "mask_low"],
     ["oT2"]),
    ("l1a_wo", emit_l1a_wo,
     ["oT2", "res2", "ident", "woT_pm", "bo_b"],
     ["res3"]),
    ("l1f_pre", emit_l1f_pre, ["res3"], ["za"]),
    ("l1f_qkv0", lambda P, io: emit_l1f_qkv(P, io, 0, "za"),
     ["za", "ident", "f_wqkv", "f_qs_b", "f_ks_b",
      "xpos_cq", "xpos_sq", "xpos_ck", "xpos_sk"],
     ["q2T", "k2T", "v2T"]),
    ("l1f_att0", lambda P, io: emit_l1f_att(P, io, 0),
     ["q2T", "k2T", "v2T", "ident", "mask_up", "mask_low"], ["o2T"]),
    ("l1f_wo0", lambda P, io: emit_l1f_wo(P, io, 0, "za", "zb"),
     ["o2T", "za", "ident", "f_wo"], ["zb"]),
    ("l1f_ffn0", lambda P, io: emit_l1f_ffn(P, io, 0, "zb", "zc"),
     ["zb", "ident", "f_lng_b", "f_lnb_b", "f_b2_b", "f_b1a", "f_b1g",
      "f_w1a", "f_w1g", "f_w2"],
     ["zc", "h2T"]),
    ("l1f_qkv1", lambda P, io: emit_l1f_qkv(P, io, 1, "zc"),
     ["zc", "ident", "f_wqkv", "f_qs_b", "f_ks_b",
      "xpos_cq", "xpos_sq", "xpos_ck", "xpos_sk"],
     ["q2T", "k2T", "v2T"]),
    ("l1f_att1", lambda P, io: emit_l1f_att(P, io, 1),
     ["q2T", "k2T", "v2T", "ident", "mask_up", "mask_low"], ["o2T"]),
    ("l1f_wo1", lambda P, io: emit_l1f_wo(P, io, 1, "zc", "zb"),
     ["o2T", "zc", "ident", "f_wo"], ["zb"]),
    ("l1f_ffn1", lambda P, io: emit_l1f_ffn(P, io, 1, "zb", "out",
                                            resname="res3"),
     ["zb", "res3", "ident", "f_lng_b", "f_lnb_b", "f_b2_b", "f_b1a", "f_b1g",
      "f_w1a", "f_w1g", "f_w2"],
     ["out", "h2T"]),
]

FINAL_OUT = "out"


def build_group(stages, ext_in, ext_out):
    nc = bass.Bass("TRN2", target_bir_lowering=False)
    with contextlib.ExitStack() as stack:
        io = {}
        names = {"ident"}
        for (_, _, rd, wr) in stages:
            names |= set(rd) | set(wr)
        for n in sorted(names):
            if n == "xin":
                shape, dt = [T, D], F32
            elif n in HOST_SPECS:
                shape, dt = HOST_SPECS[n], F32
            else:
                shp, dts = SCRATCH_SPECS[n]
                shape, dt = shp, DT[dts]
            kind = "Internal"
            if n in ext_in:
                kind = "ExternalInput"
            elif n in ext_out:
                kind = "ExternalOutput"
            io[n] = nc.dram_tensor(n, list(shape), dt, kind=kind).ap()
        with nc.Block() as block:
            P = Prog(nc, stack)

            def body(_):
                nc.gpsimd.dma_reset(range(153, 256))
                nc.gpsimd.sem_clear(range(153, 256))
                nc.all_engine_barrier()
                prelude(P, io)
                out_streams = []
                for i, (nm, fn, rd, wr) in enumerate(stages):
                    if i:
                        P.barrier()
                    with contextlib.ExitStack() as sc:
                        P.scope = sc
                        out_streams += fn(P, io)
                    P.scope = stack
                P.finish(out_streams)

            block.sync(body)
    return nc


def run_groups(groups, inputs, n_cores=8):
    host = prep(inputs)
    xin = np.ascontiguousarray(np.asarray(inputs["L0_attn_x"], dtype=np.float32))
    state = [dict() for _ in range(n_cores)]
    final = None
    for gi, group in enumerate(groups):
        produced, consumed = set(), set()
        for (_, _, rd, wr) in group:
            consumed |= set(rd) - produced
            produced |= set(wr)
        later_needed = set()
        for g2 in groups[gi + 1:]:
            for (_, _, rd, _w) in g2:
                later_needed |= set(rd)
        is_last = gi == len(groups) - 1
        ext_out = produced & later_needed
        if is_last:
            ext_out |= {FINAL_OUT}
        ext_in = consumed - produced
        nc = build_group(group, ext_in, ext_out)
        in_maps = []
        for c in range(n_cores):
            m = {}
            for n in ext_in:
                if n == "xin":
                    m[n] = xin[c]
                elif n in HOST_SPECS:
                    m[n] = host[n]
                else:
                    m[n] = state[c][n]
            in_maps.append(m)
        res = run_bass_kernel_spmd(nc, in_maps, core_ids=list(range(n_cores)))
        for c in range(n_cores):
            for n, arr in res.results[c].items():
                state[c][n] = np.asarray(arr)
        if is_last:
            final = [state[c][FINAL_OUT] for c in range(n_cores)]
    return np.stack(final)


FUSED = True


def kernel(**inputs):
    groups = [list(STAGES)] if FUSED else [[s] for s in STAGES]
    return run_groups(groups, inputs)


# version 10
# speedup vs baseline: 1.1367x; 1.1367x over previous
import contextlib
import numpy as np
import concourse.bass as bass
import concourse.mybir as mybir
from concourse.bass_utils import run_bass_kernel_spmd

F32 = mybir.dt.float32
BF16 = mybir.dt.bfloat16
AF = mybir.ActivationFunctionType
ALU = mybir.AluOpType
AX = mybir.AxisListType

T = 4096
D = 1024
MACRO = 512
NMAC = T // MACRO
EPS = 1e-5

ENG = ("pe", "dve", "act", "sp", "pool")
DEBUG = False
BYPASS_LN = False


class Prog:

    def __init__(self, nc, stack):
        self.nc = nc
        self.base = stack
        self.scope = stack
        self.eng = dict(pe=nc.tensor, dve=nc.vector, act=nc.scalar,
                        sp=nc.sync, pool=nc.gpsimd)
        self.sem = {e: stack.enter_context(nc.semaphore(f"sem_{e}"))
                    for e in ("pe", "dve", "act")}
        self.instrs = {e: [] for e in ENG}
        self.marked = {e: [] for e in ENG}
        self.last_write = {}
        self.readers = {}
        self.waited = {e: {} for e in ENG}
        self.streams = {}

    def _resolve(self, tok):
        if tok[0] == "d":
            s = self.streams[tok[1]]
            return s[0], 16 * tok[2]
        _, e, rec = tok
        if rec[2] is None:
            for m in self.marked[e]:
                if m[1] >= rec[1]:
                    return self.sem[e], m[2]
            rec[2] = len(self.marked[e]) + 1
            self.marked[e].append(rec)
            rec[0].then_inc(self.sem[e], 1)
        return self.sem[e], rec[2]

    def _wait(self, e, sem_obj, val):
        key = id(sem_obj)
        if self.waited[e].get(key, 0) >= val:
            return
        self.eng[e].wait_ge(sem_obj, val)
        self.waited[e][key] = val

    def _deps(self, e, reads, writes):
        def skip(tok):
            if tok[0] != "e" or tok[1] != e:
                return False
            return e in ("pe", "sp", "pool")
        toks = []
        for k in reads:
            lw = self.last_write.get(k)
            if lw is not None and not skip(lw):
                toks.append(lw)
        for k in writes:
            lw = self.last_write.get(k)
            if lw is not None and not skip(lw):
                toks.append(lw)
            for rt in self.readers.get(k, ()):
                if not skip(rt):
                    toks.append(rt)
        return toks

    def _update(self, tok, reads, writes):
        for k in reads:
            self.readers.setdefault(k, []).append(tok)
        for k in writes:
            self.last_write[k] = tok
            self.readers[k] = []

    def emit(self, e, fn, reads=(), writes=()):
        for tok in self._deps(e, reads, writes):
            sem_obj, val = self._resolve(tok)
            self._wait(e, sem_obj, val)
        instr = fn(self.eng[e])
        rec = [instr, len(self.instrs[e]), None]
        self.instrs[e].append(rec)
        self._update(("e", e, rec), reads, writes)
        return rec

    def stream(self, name, q="sp", serial=False):
        if name not in self.streams:
            sem = self.base.enter_context(self.nc.semaphore(f"dma_{name}"))
            self.streams[name] = [sem, 0, q, serial]
        return name

    def dma(self, sname, dst, src, reads=(), writes=(), **kw):
        s = self.streams[sname]
        q = s[2]
        if s[3] and s[1] > 0:
            self._wait(q, s[0], 16 * s[1])
        for tok in self._deps(q, reads, writes):
            sem_obj, val = self._resolve(tok)
            self._wait(q, sem_obj, val)
        instr = self.eng[q].dma_start(dst, src, **kw)
        s[1] += 1
        instr.then_inc(s[0], 16)
        rec = [instr, len(self.instrs[q]), None]
        self.instrs[q].append(rec)
        self._update(("d", sname, s[1]), reads, writes)
        return rec

    def barrier(self):
        toks = []
        for e in ("pe", "dve", "act"):
            if self.instrs[e]:
                rec = self.instrs[e][-1]
                toks.append(self._resolve(("e", e, rec)))
        for name, s in self.streams.items():
            if s[1] > 0:
                toks.append((s[0], 16 * s[1]))
        for e in ENG:
            for sem_obj, val in toks:
                self._wait(e, sem_obj, val)

    def finish(self, out_streams):
        for name in set(out_streams):
            s = self.streams[name]
            self._wait("pool", s[0], 16 * s[1])


class Tens:
    def __init__(self, h, name):
        self.h = h
        self.name = name

    def __getitem__(self, idx):
        return self.h[idx]

    def k(self, tag=None):
        return (self.name, tag)


_uid = [0]


def sb(P, shape, dtype, name=None, base=False):
    _uid[0] += 1
    nm = f"{name or 'sb'}_{_uid[0]}"
    st = P.base if base else P.scope
    h = st.enter_context(P.nc.sbuf_tensor(nm, list(shape), dtype))
    return Tens(h, nm)


def psbank(P, name=None):
    _uid[0] += 1
    nm = f"{name or 'ps'}_{_uid[0]}"
    h = P.base.enter_context(P.nc.psum_tensor(nm, [128, 512], F32))
    return Tens(h, nm)


def prelude(P, io):
    P.banks = [psbank(P, f"B{i}") for i in range(8)]
    P.identf = sb(P, [128, 128], F32, "identf", base=True)
    P.identb = sb(P, [128, 128], BF16, "identb", base=True)
    ld = P.stream("init_const", "pool", serial=True)
    P.dma(ld, P.identf[:], io["ident"], writes=[P.identf.k()])
    P.dma(ld, P.identb[:], io["ident"], writes=[P.identb.k()])


def layernorm_tile(P, out_ap, x_ap, tmp, st, xkeys, okey, n=D,
                   gamma=None, beta=None):
    xkeys = list(xkeys)
    P.emit("act", lambda a: a.activation(out_ap, x_ap, AF.Square,
                                         accum_out=st[:, 0:1]),
           reads=xkeys, writes=[(okey, "trash"), st.k("ssq")])
    P.emit("dve", lambda v: v.reduce_sum(st[:, 1:2], x_ap, axis=AX.X),
           reads=xkeys, writes=[st.k("s")])
    P.emit("dve", lambda v: v.tensor_scalar_mul(st[:, 2:3], st[:, 1:2], 1.0 / n),
           reads=[st.k("s")], writes=[st.k("mu")])
    P.emit("dve", lambda v: v.tensor_mul(st[:, 3:4], st[:, 2:3], st[:, 2:3]),
           reads=[st.k("mu")], writes=[st.k("mu2")])
    P.emit("dve", lambda v: v.tensor_scalar(st[:, 4:5], st[:, 0:1], 1.0 / n,
                                            None, ALU.mult),
           reads=[st.k("ssq")], writes=[st.k("var")])
    P.emit("dve", lambda v: v.tensor_sub(st[:, 4:5], st[:, 4:5], st[:, 3:4]),
           reads=[st.k("var"), st.k("mu2")], writes=[st.k("var")])
    P.emit("dve", lambda v: v.tensor_scalar_add(st[:, 4:5], st[:, 4:5], EPS),
           reads=[st.k("var")], writes=[st.k("var")])
    P.emit("act", lambda a: a.activation(st[:, 5:6], st[:, 4:5], AF.Sqrt),
           reads=[st.k("var")], writes=[st.k("std")])
    P.emit("dve", lambda v: v.reciprocal(st[:, 6:7], st[:, 5:6]),
           reads=[st.k("std")], writes=[st.k("rstd")])
    P.emit("dve", lambda v: v.tensor_mul(st[:, 7:8], st[:, 2:3], st[:, 6:7]),
           reads=[st.k("mu"), st.k("rstd")], writes=[st.k("nb")])
    P.emit("dve", lambda v: v.tensor_scalar_mul(st[:, 7:8], st[:, 7:8], -1.0),
           reads=[st.k("nb")], writes=[st.k("nb")])
    P.emit("act", lambda a: a.activation(out_ap, x_ap, AF.Identity,
                                         bias=st[:, 7:8], scale=st[:, 6:7]),
           reads=xkeys + [st.k("rstd"), st.k("nb")], writes=[okey])
    if gamma is not None:
        P.emit("dve", lambda v: v.tensor_mul(out_ap, out_ap, gamma[:, 0:n]),
               reads=[okey, gamma.k()], writes=[okey])
        P.emit("dve", lambda v: v.tensor_add(out_ap, out_ap, beta[:, 0:n]),
               reads=[okey, beta.k()], writes=[okey])


def _pm(tiles):
    t = np.asarray(tiles)
    P_ = t.shape[-2]
    lead = int(np.prod(t.shape[:-2], dtype=np.int64))
    t = t.reshape(lead, P_, t.shape[-1])
    return np.ascontiguousarray(t.transpose(1, 0, 2).reshape(P_, -1))


def _lhs_tiles(A):
    K, M = A.shape
    return A.reshape(K // 128, 128, M // 128, 128).transpose(2, 0, 1, 3)


def _pad_rows(A, to):
    out = np.zeros((to,) + A.shape[1:], dtype=A.dtype)
    out[: A.shape[0]] = A
    return out


def _col128(v):
    return np.ascontiguousarray(v.reshape(-1, 128).T)


def prep(inputs):
    g = {}
    f32 = np.float32

    g["ident"] = np.eye(128, dtype=f32)
    hd = 64
    freqs = (10000.0 ** (-np.arange(0, hd, 2, dtype=np.float64) / hd))
    ang = np.arange(T, dtype=np.float64)[:, None] * freqs[None, :]
    g["l1a_cos"] = np.cos(ang).astype(f32)
    g["l1a_sin"] = np.sin(ang).astype(f32)
    half = np.arange(0, hd, 2, dtype=np.float64)
    inv_freq = 1.0 / 10000.0 ** (half / hd)
    t_ = np.arange(T, dtype=np.float64)
    fr = t_[:, None] * inv_freq[None, :]
    fr64 = np.concatenate([fr, fr], 1)
    base = (half + 0.4 * hd) / (1.4 * hd)
    power = (t_ - T // 2) / 256.0
    sc = base[None, :] ** power[:, None]
    sc64 = np.concatenate([sc, sc], 1)
    cos64, sin64 = np.cos(fr64), np.sin(fr64)
    sgn = np.concatenate([-np.ones(32), np.ones(32)])[None, :]
    g["xpos_cq"] = (cos64 * sc64).astype(f32)
    g["xpos_sq"] = (sin64 * sc64 * sgn).astype(f32)
    g["xpos_ck"] = (cos64 / sc64).astype(f32)
    g["xpos_sk"] = (sin64 / sc64 * sgn).astype(f32)
    g["mask_up"] = np.triu(np.full((128, 128), -1e9, dtype=f32), 1)
    g["mask_low"] = np.tril(np.full((128, 128), -1e9, dtype=f32), -1)

    g["inw_pm"] = _pm(_lhs_tiles(inputs["L0_attn_in_w"]))
    g["outw_pm"] = _pm(inputs["L0_attn_out_w"].reshape(16, 128, 1024))
    g["inb_col"] = _col128(inputs["L0_attn_in_b"])
    g["dwb_col"] = _col128(inputs["L0_attn_dw_b"])
    g["mixb_col"] = _col128(inputs["L0_attn_mix_b"])
    g["ld_col"] = _col128(inputs["L0_attn_log_decay"])
    g["outb_b"] = np.broadcast_to(inputs["L0_attn_out_b"], (128, D)).copy()
    g["l0g_b"] = np.broadcast_to(inputs["L0_attn_norm_gamma"], (128, D)).copy()
    g["l0b_b"] = np.broadcast_to(inputs["L0_attn_norm_beta"], (128, D)).copy()
    dw = inputs["L0_attn_dw_w"]
    dwdiag = np.zeros((16, 7, 128, 128), dtype=f32)
    idx = np.arange(128)
    for gi in range(16):
        for k in range(7):
            dwdiag[gi, k, idx, idx] = dw[gi * 128 + idx, k]
    mixt = _lhs_tiles(inputs["L0_attn_mix_w"])
    cmw = np.concatenate([dwdiag, mixt], axis=1)
    g["cmw_pm"] = _pm(cmw)

    w1 = np.concatenate([inputs["L0_ffn_w1"], inputs["L0_ffn_new_w1"]], 0)
    w2 = np.concatenate([inputs["L0_ffn_w2"], inputs["L0_ffn_new_w2"]], 0)
    w3 = np.concatenate([inputs["L0_ffn_w3"], inputs["L0_ffn_new_w3"]], 0)
    g["moe_w1"] = np.stack([_pm(_lhs_tiles(np.ascontiguousarray(w1[e].T)))
                            for e in range(9)])
    g["moe_w3"] = np.stack([_pm(_lhs_tiles(np.ascontiguousarray(w3[e].T)))
                            for e in range(9)])
    g["moe_w2"] = np.stack([_pm(np.ascontiguousarray(w2[e].T).reshape(22, 128, 1024))
                            for e in range(9)])
    gate = np.concatenate([inputs["L0_ffn_gate_weight"],
                           inputs["L0_ffn_new_gate_weight"]], 0)
    g["gateT_pm"] = _pm(np.ascontiguousarray(gate.T).reshape(8, 128, 9))
    gb = np.zeros((128, 9), dtype=f32)
    gb[:, 8] = np.asarray(inputs["L0_ffn_gate_bias"])[0]
    g["gateb_b"] = gb

    for nm, W in (("wq", "L1_attn_Wq"), ("wk", "L1_attn_Wk"),
                  ("wv", "L1_attn_Wv"), ("wo", "L1_attn_Wo")):
        g[nm + "T_pm"] = _pm(np.ascontiguousarray(inputs[W].T).reshape(8, 128, 1024))
    for nm, B in (("bq", "L1_attn_bq"), ("bk", "L1_attn_bk"),
                  ("bv", "L1_attn_bv"), ("bo", "L1_attn_bo")):
        g[nm + "_b"] = np.broadcast_to(inputs[B], (128, D)).copy()
    g["l1scale"] = np.broadcast_to(np.asarray(inputs["L1_attn_scale"]), (128, 1)).copy()

    g["f_wqkv"] = np.stack([_pm(np.ascontiguousarray(inputs["L1_ffn_wqkv"][l])
                                .reshape(8, 128, 1536)) for l in range(2)])
    g["f_wo"] = np.stack([_pm(np.ascontiguousarray(inputs["L1_ffn_wo"][l])
                              .reshape(4, 128, 1024)) for l in range(2)])
    w1f = inputs["L1_ffn_w1"]
    g["f_w1a"] = np.stack([_pm(_lhs_tiles(_pad_rows(
        np.ascontiguousarray(w1f[l][:, :2730].T), 22 * 128).T.copy()))
        for l in range(2)])
    g["f_w1g"] = np.stack([_pm(_lhs_tiles(_pad_rows(
        np.ascontiguousarray(w1f[l][:, 2730:].T), 22 * 128).T.copy()))
        for l in range(2)])
    g["f_w2"] = np.stack([_pm(_pad_rows(inputs["L1_ffn_w2"][l], 22 * 128)
                              .reshape(22, 128, 1024)) for l in range(2)])
    b1 = inputs["L1_ffn_b1"]
    g["f_b1a"] = np.stack([_col128(_pad_rows(b1[l][:2730], 22 * 128))
                           for l in range(2)])
    g["f_b1g"] = np.stack([_col128(_pad_rows(b1[l][2730:], 22 * 128))
                           for l in range(2)])
    g["f_b2_b"] = np.stack([np.broadcast_to(inputs["L1_ffn_b2"][l], (128, D)).copy()
                            for l in range(2)])
    g["f_lng_b"] = np.stack([np.broadcast_to(inputs["L1_ffn_ln_g"][l], (128, D)).copy()
                             for l in range(2)])
    g["f_lnb_b"] = np.stack([np.broadcast_to(inputs["L1_ffn_ln_b"][l], (128, D)).copy()
                             for l in range(2)])
    g["f_qs_b"] = np.stack([np.broadcast_to(np.tile(np.asarray(inputs["L1_ffn_q_scale"][l]), 8),
                                            (128, 512)).copy() for l in range(2)])
    g["f_ks_b"] = np.stack([np.broadcast_to(np.tile(np.asarray(inputs["L1_ffn_k_scale"][l]), 8),
                                            (128, 512)).copy() for l in range(2)])
    for k in g:
        g[k] = np.ascontiguousarray(g[k], dtype=f32)
    return g


HOST_SPECS = {
    "ident": [128, 128],
    "l1a_cos": [T, 32], "l1a_sin": [T, 32],
    "xpos_cq": [T, 64], "xpos_sq": [T, 64],
    "xpos_ck": [T, 64], "xpos_sk": [T, 64],
    "mask_up": [128, 128], "mask_low": [128, 128],
    "inw_pm": [128, 32 * 8 * 128],
    "cmw_pm": [128, 16 * 23 * 128],
    "outw_pm": [128, 16 * 1024],
    "inb_col": [128, 32], "dwb_col": [128, 16],
    "mixb_col": [128, 16], "ld_col": [128, 16],
    "outb_b": [128, D], "l0g_b": [128, D], "l0b_b": [128, D],
    "moe_w1": [9, 128, 22 * 8 * 128],
    "moe_w3": [9, 128, 22 * 8 * 128],
    "moe_w2": [9, 128, 22 * 1024],
    "gateT_pm": [128, 8 * 9], "gateb_b": [128, 9],
    "wqT_pm": [128, 8 * 1024], "wkT_pm": [128, 8 * 1024],
    "wvT_pm": [128, 8 * 1024], "woT_pm": [128, 8 * 1024],
    "bq_b": [128, D], "bk_b": [128, D], "bv_b": [128, D], "bo_b": [128, D],
    "l1scale": [128, 1],
    "f_wqkv": [2, 128, 8 * 1536],
    "f_wo": [2, 128, 4 * 1024],
    "f_w1a": [2, 128, 22 * 8 * 128],
    "f_w1g": [2, 128, 22 * 8 * 128],
    "f_w2": [2, 128, 22 * 1024],
    "f_b1a": [2, 128, 22], "f_b1g": [2, 128, 22],
    "f_b2_b": [2, 128, D], "f_lng_b": [2, 128, D], "f_lnb_b": [2, 128, D],
    "f_qs_b": [2, 128, 512], "f_ks_b": [2, 128, 512],
}

SCRATCH_SPECS = {
    "dbg_xnT": ([NMAC, 128, 8 * MACRO], "bf16"),
    "dbg_g": ([NMAC, 128, 16 * MACRO], "bf16"),
    "dbg_v": ([NMAC, 128, 16 * (MACRO + 6)], "bf16"),
    "dbg_y": ([NMAC, 128, 16 * MACRO], "bf16"),
    "dbg_h": ([NMAC, 128, 16 * MACRO], "bf16"),
    "dbg_ln": ([NMAC, 128, 2 * D], "bf16"),
    "dbg_att": ([128, 1232], "f32"),
    "dbg_st": ([NMAC, 128, 32], "f32"),
    "res1": ([T, D], "f32"),
    "res2": ([T, D], "f32"),
    "res3": ([T, D], "f32"),
    "xlnT": ([NMAC, 128, 8 * MACRO], "bf16"),
    "qT": ([8, 128, T], "bf16"),
    "kT": ([8, 128, T], "bf16"),
    "vT1": ([T, D], "bf16"),
    "attM": ([16, 1], "f32"),
    "oT2": ([8, 128, T], "bf16"),
    "za": ([T, D], "f32"),
    "zb": ([T, D], "f32"),
    "zc": ([T, D], "f32"),
    "q2T": ([4, 128, T], "bf16"),
    "k2T": ([4, 128, T], "bf16"),
    "v2T": ([T, 512], "bf16"),
    "o2T": ([4, 128, T], "bf16"),
    "h2T": ([NMAC, 128, 8 * MACRO], "bf16"),
    "out": ([T, D], "f32"),
}

DT = {"f32": F32, "bf16": BF16}


def emit_l0a(P, io):
    identf = P.identf
    outw = sb(P, [128, 16 * 1024], BF16, "outw")
    inb = sb(P, [128, 32], F32, "inb")
    bcomb = sb(P, [128, 16], F32, "bcomb")
    dec = sb(P, [128, 16], F32, "dec")
    outb = sb(P, [128, D], F32, "outb")
    gam = sb(P, [128, D], F32, "gam")
    bet = sb(P, [128, D], F32, "bet")
    tmpc = sb(P, [128, 16], F32, "tmpc")
    carry = sb(P, [128, 16], F32, "carry")

    ld = P.stream("l0_const", "pool", serial=True)
    P.dma(ld, outw[:], io["outw_pm"], writes=[outw.k()])
    P.dma(ld, inb[:], io["inb_col"], writes=[inb.k()])
    P.dma(ld, tmpc[:], io["dwb_col"], writes=[tmpc.k()])
    P.dma(ld, bcomb[:], io["mixb_col"], writes=[bcomb.k()])
    P.dma(ld, dec[:], io["ld_col"], writes=[dec.k()])
    P.dma(ld, outb[:], io["outb_b"], writes=[outb.k()])
    P.dma(ld, gam[:], io["l0g_b"], writes=[gam.k()])
    P.dma(ld, bet[:], io["l0b_b"], writes=[bet.k()])
    P.emit("dve", lambda v: v.tensor_add(bcomb[:], bcomb[:], tmpc[:]),
           reads=[bcomb.k(), tmpc.k()], writes=[bcomb.k()])
    P.emit("act", lambda a: a.activation(dec[:], dec[:], AF.Sigmoid),
           reads=[dec.k()], writes=[dec.k()])

    CT = 1
    NCH = 32 // CT
    NSLOT = 3
    inw_sl = [sb(P, [128, CT * 8 * 128], BF16, f"inw{s}") for s in range(NSLOT)]
    inw_st = [P.stream(f"l0_inw{s}", "pool") for s in range(NSLOT)]

    inw_seq = [0]

    def load_inw_next():
        seq = inw_seq[0]
        if seq >= NMAC * NCH:
            return
        inw_seq[0] += 1
        j = seq % NCH
        sl = seq % NSLOT
        P.dma(inw_st[sl], inw_sl[sl][:],
              io["inw_pm"][:, j * CT * 1024:(j + 1) * CT * 1024],
              writes=[inw_sl[sl].k()])

    MT = 1
    NMH = 16 // MT
    MSLOT = 3
    mix_sl = [sb(P, [128, 23 * 128], BF16, f"mix{s}") for s in range(MSLOT)]
    mix_st = [P.stream(f"l0_mix{s}", "pool") for s in range(MSLOT)]

    mix_seq = [0]

    def load_mix_next():
        seq = mix_seq[0]
        if seq >= NMAC * NMH:
            return
        mix_seq[0] += 1
        j = seq % NMH
        sl = seq % MSLOT
        P.dma(mix_st[sl], mix_sl[sl][:],
              io["cmw_pm"][:, j * 23 * 128:(j + 1) * 23 * 128],
              writes=[mix_sl[sl].k()])

    xt = [sb(P, [128, 4 * D], F32, f"xt{b}") for b in range(2)]
    ln1t_ = [sb(P, [128, D], F32, f"ln1t{b}") for b in range(2)]
    xn_ = [sb(P, [128, D], F32, f"xn{b}") for b in range(2)]
    sqt_ = [None, None]
    stats_ = [sb(P, [128, 8], F32, f"stats{b}") for b in range(2)]
    xnT_ = [sb(P, [128, 8 * MACRO], BF16, f"xnTa"),
            sb(P, [128, 8 * MACRO], BF16, f"xnTb")]
    gmac = sb(P, [128, 16 * MACRO], BF16, "gmac")
    vmac = sb(P, [128, 16 * (MACRO + 6)], BF16, "vmac")
    tail = sb(P, [128, 16 * 6], BF16, "tail")
    ymac = sb(P, [128, 16 * MACRO], BF16, "ymac")
    hmac = sb(P, [128, 16 * MACRO], BF16, "hmac")
    vtmp = sb(P, [128, MACRO], F32, "vtmp")
    res_t = [sb(P, [128, D], F32, f"res_t{b}") for b in range(2)]
    if DEBUG:
        dbgln = sb(P, [128, 2 * D], BF16, "dbgln")
        dbgst = sb(P, [128, 32], F32, "dbgst")

    xst = [[P.stream(f"l0_x{b}_{tt}", "sp") for tt in range(4)] for b in range(2)]
    rst = [P.stream(f"l0_res{b}", "sp") for b in range(2)]
    B = P.banks

    P.emit("dve", lambda v: v.memset(tail[:], 0.0), writes=[tail.k()])
    for _ in range(NSLOT - 1):
        load_inw_next()
        load_mix_next()

    def ln_phase(m):
        XT = xt[m % 2]
        xnT = xnT_[m % 2]
        for tt in range(4):
            t0 = m * MACRO + tt * 128
            P.dma(xst[m % 2][tt], XT[:, tt * D:(tt + 1) * D],
                  io["xin"][t0:t0 + 128, :], writes=[XT.k(tt)])
        for tt in range(4):
            xa = XT[:, tt * D:(tt + 1) * D]
            ln1t, xn = ln1t_[tt % 2], xn_[tt % 2]
            sqt, stats = sqt_[tt % 2], stats_[tt % 2]
            if BYPASS_LN:
                P.emit("dve", lambda v, xa=xa, ln1t=ln1t: v.tensor_copy(ln1t[:], xa),
                       reads=[XT.k(tt)], writes=[ln1t.k()])
                P.emit("dve", lambda v, xn=xn, ln1t=ln1t: v.tensor_copy(xn[:], ln1t[:]),
                       reads=[ln1t.k()], writes=[xn.k()])
            else:
                layernorm_tile(P, ln1t[:], xa, sqt, stats, [XT.k(tt)], ln1t.k())
                if DEBUG:
                    if tt < 2:
                        P.emit("dve", lambda v, tt=tt: v.tensor_copy(
                            dbgln[:, tt * D:(tt + 1) * D], ln1t[:]),
                            reads=[ln1t.k()], writes=[dbgln.k(tt)])
                    P.emit("dve", lambda v, tt=tt: v.tensor_copy(
                        dbgst[:, tt * 8:(tt + 1) * 8], stats[:]),
                        reads=[stats.k(kk) for kk in
                               ("ssq", "s", "mu", "mu2", "var", "std", "rstd", "nb")],
                        writes=[dbgst.k(tt)])
                layernorm_tile(P, xn[:], ln1t[:], sqt, stats, [ln1t.k()], xn.k(),
                               gamma=gam, beta=bet)
            for d0 in range(0, 8, 4):
                pb = B[6 + (d0 // 4) % 2]
                for r in range(4):
                    dk = d0 + r
                    P.emit("pe", lambda t, dk=dk, r=r, pb=pb, xn=xn: t.transpose(
                        pb[:, r * 128:(r + 1) * 128],
                        xn[:, dk * 128:(dk + 1) * 128], identf[:]),
                        reads=[xn.k(), identf.k()], writes=[pb.k()])
                P.emit("act", lambda a, tt=tt, d0=d0, pb=pb, xnT=xnT: a.copy(
                    xnT[:].rearrange("p (dk t) -> p dk t", dk=8)
                    [:, d0:d0 + 4, tt * 128:(tt + 1) * 128],
                    pb[:].rearrange("p (dk t) -> p dk t", dk=4)),
                    reads=[pb.k()],
                    writes=[xnT.k(d0), xnT.k(d0 + 1), xnT.k(d0 + 2), xnT.k(d0 + 3)])
            P.emit("dve", lambda v, xa=xa, ln1t=ln1t: v.tensor_add(xa, xa, ln1t[:]),
                   reads=[XT.k(tt), ln1t.k()], writes=[XT.k(tt)])

    def compute_phase(m):
        XT = xt[m % 2]
        xnT = xnT_[m % 2]
        for c in range(32):
            j = c // CT
            buf = inw_sl[(m * NCH + j) % NSLOT]
            coff = (c % CT) * 8 * 128
            pb = B[c % 2]
            for k in range(8):
                P.emit("pe", lambda t, k=k, buf=buf, coff=coff, pb=pb:
                       t.matmul(pb[:], buf[:, coff + k * 128:coff + (k + 1) * 128],
                                xnT[:, k * MACRO:(k + 1) * MACRO],
                                start=(k == 0), stop=(k == 7)),
                       reads=[buf.k(), xnT.k(k)], writes=[pb.k()])
            if c < 16:
                gi = c
                P.emit("act", lambda a, gi=gi, pb=pb: a.activation(
                    gmac[:, gi * MACRO:(gi + 1) * MACRO], pb[:], AF.Sigmoid,
                    bias=inb[:, gi:gi + 1]),
                    reads=[pb.k(), inb.k()], writes=[gmac.k(gi)])
            else:
                gi = c - 16
                P.emit("act", lambda a, gi=gi, pb=pb: a.add(
                    vmac[:, gi * (MACRO + 6) + 6: gi * (MACRO + 6) + 6 + MACRO],
                    pb[:], inb[:, 16 + gi:16 + gi + 1]),
                    reads=[pb.k(), inb.k()], writes=[vmac.k(gi)])
                init = 0.0 if m == 0 else carry[:, gi:gi + 1]
                P.emit("dve", lambda v, gi=gi, init=init: v.tensor_tensor_scan(
                    ymac[:, gi * MACRO:(gi + 1) * MACRO],
                    dec[:, gi:gi + 1].broadcast_to([128, MACRO]),
                    vmac[:, gi * (MACRO + 6) + 6: gi * (MACRO + 6) + 6 + MACRO],
                    init, ALU.mult, ALU.add),
                    reads=[vmac.k(gi), dec.k()] + ([carry.k()] if m else []),
                    writes=[ymac.k(gi)])
            if c % CT == CT - 1:
                load_inw_next()
        P.emit("dve", lambda v: v.tensor_copy(
            vmac[:].rearrange("p (g c) -> p g c", g=16)[:, :, 0:6],
            tail[:].rearrange("p (g c) -> p g c", g=16)),
            reads=[tail.k()] + [vmac.k(gi) for gi in range(16)],
            writes=[vmac.k("halo")])
        P.emit("dve", lambda v: v.tensor_copy(
            carry[:],
            ymac[:].rearrange("p (g c) -> p g c", g=16)[:, :, MACRO - 1:MACRO]
            .rearrange("p g c -> p (g c)")),
            reads=[ymac.k(gi) for gi in range(16)], writes=[carry.k()])
        for gi in range(16):
            jm = gi // MT
            mbuf = mix_sl[(m * NMH + jm) % MSLOT]
            pc, pm_ = B[2 + gi % 2], B[4 + gi % 2]
            for k in range(7):
                P.emit("pe", lambda t, gi=gi, k=k, pc=pc, mbuf=mbuf: t.matmul(
                    pc[:], mbuf[:, k * 128:(k + 1) * 128],
                    vmac[:, gi * (MACRO + 6) + k: gi * (MACRO + 6) + k + MACRO],
                    start=(k == 0), stop=(k == 6)),
                    reads=[mbuf.k(), vmac.k(gi), vmac.k("halo")], writes=[pc.k()])
            for k in range(16):
                P.emit("pe", lambda t, k=k, mbuf=mbuf, pm_=pm_: t.matmul(
                    pm_[:], mbuf[:, (7 + k) * 128:(8 + k) * 128],
                    ymac[:, k * MACRO:(k + 1) * MACRO],
                    start=(k == 0), stop=(k == 15)),
                    reads=[mbuf.k(), ymac.k(k)], writes=[pm_.k()])
            P.emit("act", lambda a, gi=gi, pc=pc: a.add(
                vtmp[:], pc[:], bcomb[:, gi:gi + 1]),
                reads=[pc.k(), bcomb.k()], writes=[vtmp.k()])
            P.emit("dve", lambda v, pm_=pm_: v.tensor_add(
                vtmp[:], vtmp[:], pm_[:]),
                reads=[vtmp.k(), pm_.k()], writes=[vtmp.k()])
            P.emit("dve", lambda v, gi=gi: v.tensor_mul(
                hmac[:, gi * MACRO:(gi + 1) * MACRO], vtmp[:],
                gmac[:, gi * MACRO:(gi + 1) * MACRO]),
                reads=[vtmp.k(), gmac.k(gi)], writes=[hmac.k(gi)])
            if gi % MT == MT - 1:
                load_mix_next()
        P.emit("dve", lambda v: v.tensor_copy(
            tail[:].rearrange("p (g c) -> p g c", g=16),
            vmac[:].rearrange("p (g c) -> p g c", g=16)[:, :, MACRO:MACRO + 6]),
            reads=[vmac.k(gi) for gi in range(16)], writes=[tail.k()])
        if DEBUG:
            dst = P.stream("l0_dbg", "sp", serial=True)
            for nm_, buf_ in (("dbg_xnT", xnT), ("dbg_g", gmac), ("dbg_v", vmac),
                              ("dbg_y", ymac), ("dbg_h", hmac)):
                pass
            for nm_, buf_ in (("dbg_xnT", xnT), ("dbg_g", gmac), ("dbg_v", vmac),
                              ("dbg_y", ymac), ("dbg_h", hmac),
                              ("dbg_ln", dbgln), ("dbg_st", dbgst)):
                if nm_ not in io:
                    continue
                rd = [buf_.k(kk) for kk in range(16 if nm_ != "dbg_xnT" else 8)]
                if nm_ == "dbg_v":
                    rd.append(buf_.k("halo"))
                if nm_ in ("dbg_ln", "dbg_st"):
                    rd = [buf_.k(kk) for kk in range(4)]
                P.dma(dst, io[nm_][m], buf_[:], reads=rd)
        for tt in range(4):
            RT = res_t[tt % 2]
            for half in range(2):
                po = B[6 + half]
                for k in range(16):
                    P.emit("pe", lambda t, tt=tt, half=half, k=k, po=po: t.matmul(
                        po[:], hmac[:, k * MACRO + tt * 128: k * MACRO + (tt + 1) * 128],
                        outw[:, k * 1024 + half * 512: k * 1024 + (half + 1) * 512],
                        start=(k == 0), stop=(k == 15)),
                        reads=[hmac.k(k), outw.k()], writes=[po.k()])
                P.emit("dve", lambda v, half=half, po=po, RT=RT: v.tensor_add(
                    RT[:, half * 512:(half + 1) * 512], po[:],
                    outb[:, half * 512:(half + 1) * 512]),
                    reads=[po.k(), outb.k()], writes=[RT.k()])
            P.emit("dve", lambda v, tt=tt, RT=RT: v.tensor_add(
                RT[:], RT[:], XT[:, tt * D:(tt + 1) * D]),
                reads=[RT.k(), XT.k(tt)], writes=[RT.k()])
            t0 = m * MACRO + tt * 128
            P.dma(rst[tt % 2], io["res1"][t0:t0 + 128, :], RT[:], reads=[RT.k()])

    ln_phase(0)
    for m in range(NMAC):
        if m + 1 < NMAC:
            ln_phase(m + 1)
        compute_phase(m)
    return ["l0_res0", "l0_res1"]


MOE_CHUNKS = [(0, 6), (6, 6), (12, 6), (18, 4)]


def emit_moe(P, io):
    identf = P.identf
    B = P.banks
    gateT = sb(P, [128, 8 * 9], F32, "gateT")
    gateb = sb(P, [128, 9], F32, "gateb")
    cw = sb(P, [128, 32 * 9], F32, "cw")
    ld = P.stream("moe_const", "pool", serial=True)
    P.dma(ld, gateT[:], io["gateT_pm"], writes=[gateT.k()])
    P.dma(ld, gateb[:], io["gateb_b"], writes=[gateb.k()])

    xt = [sb(P, [128, D], F32, f"mx{b}") for b in range(2)]
    xn = sb(P, [128, D], F32, "mxn")
    sqt = sb(P, [128, D], F32, "msq")
    stats = sb(P, [128, 8], F32, "mst")
    xnTf = sb(P, [128, D], F32, "mxnTf")
    xnTb = sb(P, [128, 8 * MACRO], BF16, "mxnTb")
    sc9 = sb(P, [128, 32], F32, "msc9")
    xst = [P.stream(f"moe_x{b}", "sp") for b in range(2)]
    r2st = [P.stream(f"moe_r2i{b}", "sp") for b in range(2)]
    xlst = P.stream("moe_xlT", "sp")

    for m in range(NMAC):
        for tt in range(4):
            ti = m * 4 + tt
            XT = xt[ti % 2]
            t0 = ti * 128
            P.dma(xst[ti % 2], XT[:], io["res1"][t0:t0 + 128, :],
                  reads=[("hbm_res1", m)], writes=[XT.k()])
            layernorm_tile(P, xn[:], XT[:], sqt, stats, [XT.k()], xn.k())
            P.dma(r2st[ti % 2], io["res2"][t0:t0 + 128, :], XT[:],
                  reads=[XT.k()], writes=[("hbm_res2", m, "init", tt)])
            for d0 in range(0, 8, 4):
                pb = B[(d0 // 4) % 2]
                for r in range(4):
                    dk = d0 + r
                    P.emit("pe", lambda t, dk=dk, r=r, pb=pb: t.transpose(
                        pb[:, r * 128:(r + 1) * 128],
                        xn[:, dk * 128:(dk + 1) * 128], identf[:]),
                        reads=[xn.k(), identf.k()], writes=[pb.k()])
                P.emit("act", lambda a, d0=d0, pb=pb: a.copy(
                    xnTf[:, d0 * 128:(d0 + 4) * 128], pb[:, 0:512]),
                    reads=[pb.k()],
                    writes=[xnTf.k(d0), xnTf.k(d0 + 1), xnTf.k(d0 + 2),
                            xnTf.k(d0 + 3)])
                P.emit("dve", lambda v, tt=tt, d0=d0: v.tensor_copy(
                    xnTb[:].rearrange("p (dk t) -> p dk t", dk=8)
                    [:, d0:d0 + 4, tt * 128:(tt + 1) * 128],
                    xnTf[:, d0 * 128:(d0 + 4) * 128]
                    .rearrange("p (dk t) -> p dk t", dk=4)),
                    reads=[xnTf.k(d0), xnTf.k(d0 + 1), xnTf.k(d0 + 2),
                           xnTf.k(d0 + 3)],
                    writes=[xnTb.k(d0), xnTb.k(d0 + 1), xnTb.k(d0 + 2),
                            xnTb.k(d0 + 3)])
            pg = B[2]
            for dk in range(8):
                P.emit("pe", lambda t, dk=dk, pg=pg: t.matmul(
                    pg[:, 0:9], xnTf[:, dk * 128:(dk + 1) * 128],
                    gateT[:, dk * 9:(dk + 1) * 9],
                    start=(dk == 0), stop=(dk == 7)),
                    reads=[xnTf.k(dk), gateT.k()], writes=[pg.k()])
            P.emit("dve", lambda v, pg=pg: v.tensor_add(
                sc9[:, 0:9], pg[:, 0:9], gateb[:]),
                reads=[pg.k(), gateb.k()], writes=[sc9.k("s")])
            P.emit("act", lambda a: a.activation(sc9[:, 0:9], sc9[:, 0:9],
                                                 AF.Sigmoid),
                   reads=[sc9.k("s")], writes=[sc9.k("s")])
            P.emit("dve", lambda v: v.max(sc9[:, 16:24], sc9[:, 0:9]),
                   reads=[sc9.k("s")], writes=[sc9.k("m8")])
            P.emit("dve", lambda v: v.tensor_add(sc9[:, 24:25], sc9[:, 16:17],
                                                 sc9[:, 17:18]),
                   reads=[sc9.k("m8")], writes=[sc9.k("d")])
            P.emit("dve", lambda v: v.tensor_scalar_add(sc9[:, 24:25],
                                                        sc9[:, 24:25], 1e-20),
                   reads=[sc9.k("d")], writes=[sc9.k("d")])
            P.emit("dve", lambda v: v.reciprocal(sc9[:, 25:26], sc9[:, 24:25]),
                   reads=[sc9.k("d")], writes=[sc9.k("rd")])
            P.emit("dve", lambda v: v.tensor_mul(sc9[:, 26:27], sc9[:, 16:17],
                                                 sc9[:, 25:26]),
                   reads=[sc9.k("m8"), sc9.k("rd")], writes=[sc9.k("w1")])
            P.emit("dve", lambda v: v.tensor_mul(sc9[:, 27:28], sc9[:, 17:18],
                                                 sc9[:, 25:26]),
                   reads=[sc9.k("m8"), sc9.k("rd")], writes=[sc9.k("w2")])
            P.emit("dve", lambda v: v.tensor_scalar(
                sc9[:, 0:9] if False else cw[:, ti * 9:(ti + 1) * 9],
                sc9[:, 0:9], sc9[:, 16:17], sc9[:, 26:27],
                ALU.is_equal, ALU.mult),
                reads=[sc9.k("s"), sc9.k("m8"), sc9.k("w1")],
                writes=[cw.k(ti)])
            P.emit("dve", lambda v: v.tensor_scalar(
                sc9[:, 0:9], sc9[:, 0:9], sc9[:, 17:18], sc9[:, 27:28],
                ALU.is_equal, ALU.mult),
                reads=[sc9.k("s"), sc9.k("m8"), sc9.k("w2")],
                writes=[sc9.k("eq2")])
            P.emit("dve", lambda v, ti=ti: v.tensor_add(
                cw[:, ti * 9:(ti + 1) * 9], cw[:, ti * 9:(ti + 1) * 9],
                sc9[:, 0:9]),
                reads=[cw.k(ti), sc9.k("eq2")], writes=[cw.k(ti)])
        P.dma(xlst, io["xlnT"][m], xnTb[:],
              reads=[xnTb.k(dk) for dk in range(8)],
              writes=[("hbm_xlnT", m)])

    w1r = [sb(P, [128, 6 * 8 * 128], BF16, f"w1r{b}") for b in range(2)]
    w3r = [sb(P, [128, 6 * 8 * 128], BF16, f"w3r{b}") for b in range(2)]
    w2r = [sb(P, [128, 6 * 1024], BF16, f"w2r{b}") for b in range(2)]
    wst = [[P.stream(f"moe_w{w}{b}", "pool") for b in range(2)]
           for w in range(3)]
    xbuf = [sb(P, [128, 8 * MACRO], BF16, f"moexb{b}") for b in range(2)]
    xbst = [P.stream(f"moe_xb{b}", "sp") for b in range(2)]
    hT = [sb(P, [128, 6 * MACRO], BF16, f"moeh{b}") for b in range(2)]
    delta = [sb(P, [128, 4 * D], F32, f"moed{b}") for b in range(2)]
    dst_ = [P.stream(f"moe_acc{b}", "pool") for b in range(2)]
    silu_ = [sb(P, [128, MACRO], F32, f"moesilu{b}") for b in range(2)]

    chunk_seq = [0]
    schedule = [(e, ci) for e in range(9) for ci in range(4)]

    def load_chunk_next():
        seq = chunk_seq[0]
        if seq >= len(schedule):
            return
        chunk_seq[0] += 1
        e, ci = schedule[seq]
        c0, cn = MOE_CHUNKS[ci]
        sl = seq % 2
        P.dma(wst[0][sl], w1r[sl][:, 0:cn * 8 * 128],
              io["moe_w1"][e][:, c0 * 8 * 128:(c0 + cn) * 8 * 128],
              writes=[w1r[sl].k()])
        P.dma(wst[1][sl], w3r[sl][:, 0:cn * 8 * 128],
              io["moe_w3"][e][:, c0 * 8 * 128:(c0 + cn) * 8 * 128],
              writes=[w3r[sl].k()])
        P.dma(wst[2][sl], w2r[sl][:, 0:cn * 1024],
              io["moe_w2"][e][:, c0 * 1024:(c0 + cn) * 1024],
              writes=[w2r[sl].k()])

    load_chunk_next()
    load_chunk_next()

    iters = [(seq, e, ci, m) for seq, (e, ci) in enumerate(schedule)
             for m in range(NMAC)]

    def ab_h(i):
        seq, e, ci, m = iters[i]
        c0, cn = MOE_CHUNKS[ci]
        sl = seq % 2
        W1, W3 = w1r[sl], w3r[sl]
        xb = xbuf[i % 2]
        P.dma(xbst[i % 2], xb[:], io["xlnT"][m],
              reads=[("hbm_xlnT", m)], writes=[xb.k()])
        HT = hT[i % 2]
        for h in range(cn):
            pa, pb_ = B[h % 2], B[2 + h % 2]
            for k in range(8):
                P.emit("pe", lambda t, h=h, k=k, pa=pa, W1=W1, xb=xb: t.matmul(
                    pa[:], W1[:, (h * 8 + k) * 128:(h * 8 + k + 1) * 128],
                    xb[:, k * MACRO:(k + 1) * MACRO],
                    start=(k == 0), stop=(k == 7)),
                    reads=[W1.k(), xb.k()], writes=[pa.k()])
            for k in range(8):
                P.emit("pe", lambda t, h=h, k=k, pb_=pb_, W3=W3, xb=xb: t.matmul(
                    pb_[:], W3[:, (h * 8 + k) * 128:(h * 8 + k + 1) * 128],
                    xb[:, k * MACRO:(k + 1) * MACRO],
                    start=(k == 0), stop=(k == 7)),
                    reads=[W3.k(), xb.k()], writes=[pb_.k()])
            silu_t = silu_[h % 2]
            P.emit("act", lambda a, pa=pa, silu_t=silu_t: a.activation(
                silu_t[:], pa[:], AF.Silu),
                reads=[pa.k()], writes=[silu_t.k()])
            P.emit("dve", lambda v, h=h, pb_=pb_, HT=HT, silu_t=silu_t:
                   v.tensor_mul(
                HT[:, h * MACRO:(h + 1) * MACRO], silu_t[:], pb_[:]),
                reads=[silu_t.k(), pb_.k()], writes=[HT.k(h)])

    def w2d(i):
        seq, e, ci, m = iters[i]
        c0, cn = MOE_CHUNKS[ci]
        sl = seq % 2
        W2 = w2r[sl]
        HT = hT[i % 2]
        DL = delta[i % 2]
        for tt in range(4):
            for dh in range(2):
                po = B[4 + (tt * 2 + dh) % 2]
                for k in range(cn):
                    P.emit("pe", lambda t, tt=tt, dh=dh, k=k, po=po, W2=W2,
                           HT=HT: t.matmul(
                        po[:], HT[:, k * MACRO + tt * 128: k * MACRO + (tt + 1) * 128],
                        W2[:, k * 1024 + dh * 512:k * 1024 + (dh + 1) * 512],
                        start=(k == 0), stop=(k == cn - 1)),
                        reads=[HT.k(k), W2.k()], writes=[po.k()])
                col = (m * 4 + tt) * 9 + e
                if (tt * 2 + dh) % 2 == 0:
                    P.emit("dve", lambda v, tt=tt, dh=dh, po=po, DL=DL,
                           col=col: v.tensor_scalar_mul(
                        DL[:, tt * D + dh * 512: tt * D + (dh + 1) * 512],
                        po[:], cw[:, col:col + 1]),
                        reads=[po.k(), cw.k(m * 4 + tt)], writes=[DL.k()])
                else:
                    P.emit("act", lambda a, tt=tt, dh=dh, po=po, DL=DL,
                           col=col: a.activation(
                        DL[:, tt * D + dh * 512: tt * D + (dh + 1) * 512],
                        po[:], AF.Identity, scale=cw[:, col:col + 1]),
                        reads=[po.k(), cw.k(m * 4 + tt)], writes=[DL.k()])
        t0 = m * MACRO
        P.dma(dst_[i % 2], io["res2"][t0:t0 + MACRO, :]
              .rearrange("(tt p) f -> p tt f", p=128),
              DL[:].rearrange("p (tt f) -> p tt f", tt=4),
              reads=[DL.k()] + [("hbm_res2", m, "init", t) for t in range(4)],
              writes=[("hbm_res2", m)], accum_op=ALU.add)
        if m == NMAC - 1:
            load_chunk_next()

    ab_h(0)
    for i in range(len(iters)):
        if i + 1 < len(iters):
            ab_h(i + 1)
        w2d(i)
    return ["moe_acc0", "moe_acc1", "moe_r2i0", "moe_r2i1"]


def emit_l1a_pre(P, io):
    identf, B = P.identf, P.banks
    wq = sb(P, [128, 8 * 1024], BF16, "wq")
    wk = sb(P, [128, 8 * 1024], BF16, "wk")
    wv = sb(P, [128, 8 * 1024], BF16, "wv")
    bq = sb(P, [128, D], F32, "bqb")
    bk = sb(P, [128, D], F32, "bkb")
    bv = sb(P, [128, D], F32, "bvb")
    cosb = sb(P, [128, 32 * 32], F32, "cosb")
    sinb = sb(P, [128, 32 * 32], F32, "sinb")
    cosq = sb(P, [128, 32 * 32], F32, "cosq")
    sinq = sb(P, [128, 32 * 32], F32, "sinq")
    scl = sb(P, [128, 1], F32, "scl")
    ld = P.stream("l1p_const", "pool", serial=True)
    P.dma(ld, wq[:], io["wqT_pm"], writes=[wq.k()])
    P.dma(ld, wk[:], io["wkT_pm"], writes=[wk.k()])
    P.dma(ld, wv[:], io["wvT_pm"], writes=[wv.k()])
    P.dma(ld, bq[:], io["bq_b"], writes=[bq.k()])
    P.dma(ld, bk[:], io["bk_b"], writes=[bk.k()])
    P.dma(ld, bv[:], io["bv_b"], writes=[bv.k()])
    P.dma(ld, scl[:], io["l1scale"], writes=[scl.k()])
    P.dma(ld, cosb[:].rearrange("p (tt c) -> p tt c", tt=32),
          io["l1a_cos"].rearrange("(tt p) c -> p tt c", p=128), writes=[cosb.k()])
    P.dma(ld, sinb[:].rearrange("p (tt c) -> p tt c", tt=32),
          io["l1a_sin"].rearrange("(tt p) c -> p tt c", p=128), writes=[sinb.k()])
    P.emit("dve", lambda v: v.tensor_scalar_mul(cosq[:], cosb[:], scl[:, 0:1]),
           reads=[cosb.k(), scl.k()], writes=[cosq.k()])
    P.emit("dve", lambda v: v.tensor_scalar_mul(sinq[:], sinb[:], scl[:, 0:1]),
           reads=[sinb.k(), scl.k()], writes=[sinq.k()])

    xt = [sb(P, [128, D], F32, f"ax{b}") for b in range(2)]
    xn_ = [sb(P, [128, D], F32, f"axn{b}") for b in range(2)]
    sqt_ = [sb(P, [128, D], F32, f"asq{b}") for b in range(2)]
    stats_ = [sb(P, [128, 8], F32, f"ast{b}") for b in range(2)]
    xnT_ = [sb(P, [128, 8 * 128], BF16, f"axnT{b}") for b in range(2)]
    qf_ = [sb(P, [128, D], F32, f"aqf{b}") for b in range(2)]
    qr_ = [sb(P, [128, D], F32, f"aqr{b}") for b in range(2)]
    te_ = [sb(P, [128, D], F32, f"ate{b}") for b in range(2)]
    qTm = sb(P, [128, 8 * MACRO], BF16, "aqTm")
    kTm = sb(P, [128, 8 * MACRO], BF16, "akTm")
    sq2 = sb(P, [128, D], F32, "asq2")
    nrm = sb(P, [128, 16], F32, "anrm")
    qmx = sb(P, [128, 16], F32, "aqmx")
    kmx = sb(P, [128, 16], F32, "akmx")
    P.emit("dve", lambda v: v.memset(qmx[:], 0.0), writes=[qmx.k()])
    P.emit("dve", lambda v: v.memset(kmx[:], 0.0), writes=[kmx.k()])
    xst = [P.stream(f"l1p_x{b}", "sp") for b in range(2)]
    vst = [P.stream(f"l1p_v{b}", "pool") for b in range(2)]
    qst = P.stream("l1p_q", "sp", serial=True)
    kst = P.stream("l1p_k", "sp", serial=True)

    def rope(dst_bf, src_f32, cos_t, sin_t, tt, keys_in, key_out, te):
        cs = cos_t[:, (tt % 32) * 32:(tt % 32) * 32 + 32]
        sn = sin_t[:, (tt % 32) * 32:(tt % 32) * 32 + 32]
        cs3 = cs.rearrange("p (o c) -> p o c", o=1).broadcast_to([128, 16, 32])
        sn3 = sn.rearrange("p (o c) -> p o c", o=1).broadcast_to([128, 16, 32])
        sv = src_f32.rearrange("p (h c) -> p h c", h=16)
        se = sv[:, :, 0:64:2]
        so = sv[:, :, 1:64:2]
        dv = dst_bf.rearrange("p (h c) -> p h c", h=16)
        de = dv[:, :, 0:64:2]
        do = dv[:, :, 1:64:2]
        t3 = te[:].rearrange("p (h c) -> p h c", h=16)
        t_a = t3[:, :, 0:32]
        t_b = t3[:, :, 32:64]
        P.emit("dve", lambda v: v.tensor_mul(t_a, se, cs3),
               reads=keys_in + [cos_t.k()], writes=[te.k("a")])
        P.emit("dve", lambda v: v.tensor_mul(t_b, so, sn3),
               reads=keys_in + [sin_t.k()], writes=[te.k("b")])
        P.emit("dve", lambda v: v.tensor_sub(de, t_a, t_b),
               reads=[te.k("a"), te.k("b")], writes=[key_out])
        P.emit("dve", lambda v: v.tensor_mul(t_a, se, sn3),
               reads=keys_in + [sin_t.k()], writes=[te.k("a")])
        P.emit("dve", lambda v: v.tensor_mul(t_b, so, cs3),
               reads=keys_in + [cos_t.k()], writes=[te.k("b")])
        P.emit("dve", lambda v: v.tensor_add(do, t_a, t_b),
               reads=[te.k("a"), te.k("b")], writes=[key_out])

    for m in range(NMAC):
        for tt in range(4):
            ti = m * 4 + tt
            bb = ti % 2
            XT = xt[bb]
            xn, sqt, stats = xn_[bb], sqt_[bb], stats_[bb]
            xnT, qf, qr, te = xnT_[bb], qf_[bb], qr_[bb], te_[bb]
            t0 = ti * 128
            P.dma(xst[ti % 2], XT[:], io["res2"][t0:t0 + 128, :],
                  writes=[XT.k()])
            layernorm_tile(P, xn[:], XT[:], sqt, stats, [XT.k()], xn.k())
            for d0 in range(0, 8, 4):
                pb = B[(d0 // 4) % 2]
                for r in range(4):
                    dk = d0 + r
                    P.emit("pe", lambda t, dk=dk, r=r, pb=pb, xn=xn: t.transpose(
                        pb[:, r * 128:(r + 1) * 128],
                        xn[:, dk * 128:(dk + 1) * 128], identf[:]),
                        reads=[xn.k(), identf.k()], writes=[pb.k()])
                P.emit("act", lambda a, d0=d0, pb=pb, xnT=xnT: a.copy(
                    xnT[:, d0 * 128:(d0 + 4) * 128], pb[:, 0:512]),
                    reads=[pb.k()],
                    writes=[xnT.k(d0), xnT.k(d0 + 1), xnT.k(d0 + 2),
                            xnT.k(d0 + 3)])
            for proj, (W, bias, dstT) in enumerate(
                    ((wq, bq, qTm), (wk, bk, kTm), (wv, bv, None))):
                for half in range(2):
                    pp = B[2 + half]
                    for dk in range(8):
                        P.emit("pe", lambda t, dk=dk, half=half, pp=pp, W=W,
                               xnT=xnT:
                               t.matmul(pp[:], xnT[:, dk * 128:(dk + 1) * 128],
                                        W[:, dk * 1024 + half * 512:
                                          dk * 1024 + (half + 1) * 512],
                                        start=(dk == 0), stop=(dk == 7)),
                               reads=[xnT.k(dk), W.k()], writes=[pp.k()])
                    P.emit("dve", lambda v, half=half, pp=pp, bias=bias, qf=qf:
                           v.tensor_add(
                        qf[:, half * 512:(half + 1) * 512], pp[:],
                        bias[:, half * 512:(half + 1) * 512]),
                        reads=[pp.k(), bias.k()], writes=[qf.k()])
                if proj < 2:
                    if proj == 0:
                        rope(qr[:], qf[:], cosq, sinq, ti, [qf.k()], qr.k(), te)
                        mxacc = qmx
                    else:
                        rope(qr[:], qf[:], cosb, sinb, ti, [qf.k()], qr.k(), te)
                        mxacc = kmx
                    P.emit("dve", lambda v, qr=qr: v.tensor_mul(sq2[:], qr[:], qr[:]),
                           reads=[qr.k()], writes=[sq2.k()])
                    P.emit("dve", lambda v: v.tensor_reduce(
                        nrm[:, 0:16],
                        sq2[:].rearrange("p (g c) -> p g c", g=16),
                        AX.X, ALU.add),
                        reads=[sq2.k()], writes=[nrm.k(0)])
                    P.emit("dve", lambda v, mxacc=mxacc: v.tensor_max(
                        mxacc[:], mxacc[:], nrm[:, 0:16]),
                        reads=[mxacc.k(), nrm.k(0)], writes=[mxacc.k()])
                else:
                    P.dma(vst[ti % 2], io["vT1"][t0:t0 + 128, :], qf[:],
                          reads=[qf.k()])
                    continue
                for b0 in range(0, 8, 4):
                    pb = B[4 + (b0 // 4) % 2]
                    for r in range(4):
                        blk = b0 + r
                        P.emit("pe", lambda t, blk=blk, r=r, pb=pb, qr=qr:
                               t.transpose(
                            pb[:, r * 128:(r + 1) * 128],
                            qr[:, blk * 128:(blk + 1) * 128], identf[:]),
                            reads=[qr.k(), identf.k()], writes=[pb.k()])
                    P.emit("act", lambda a, b0=b0, tt=tt, pb=pb, dstT=dstT: a.copy(
                        dstT[:].rearrange("p (blk t) -> p blk t", blk=8)
                        [:, b0:b0 + 4, (tt % 4) * 128:(tt % 4 + 1) * 128],
                        pb[:].rearrange("p (blk t) -> p blk t", blk=4)),
                        reads=[pb.k()],
                        writes=[dstT.k(b0), dstT.k(b0 + 1), dstT.k(b0 + 2),
                                dstT.k(b0 + 3)])
        for blk in range(8):
            P.dma(qst, io["qT"][blk][:, m * MACRO:(m + 1) * MACRO],
                  qTm[:, blk * MACRO:(blk + 1) * MACRO], reads=[qTm.k(blk)])
            P.dma(kst, io["kT"][blk][:, m * MACRO:(m + 1) * MACRO],
                  kTm[:, blk * MACRO:(blk + 1) * MACRO], reads=[kTm.k(blk)])
    B2 = P.banks
    for src, col in ((qmx, 0), (kmx, 1)):
        pb = B2[col]
        P.emit("pe", lambda t, src=src, pb=pb: t.transpose(
            pb[0:16, 0:128], src[:], identf[:]),
            reads=[src.k(), identf.k()], writes=[pb.k()])
        P.emit("dve", lambda v, col=col, pb=pb: v.reduce_max(
            nrm[0:16, 2 + col:3 + col], pb[0:16, 0:128], axis=AX.X),
            reads=[pb.k()], writes=[nrm.k(("gm", col))])
    P.emit("act", lambda a: a.activation(nrm[0:16, 4:6], nrm[0:16, 2:4], AF.Sqrt),
           reads=[nrm.k(("gm", 0)), nrm.k(("gm", 1))], writes=[nrm.k("sq")])
    P.emit("dve", lambda v: v.tensor_mul(nrm[0:16, 6:7], nrm[0:16, 4:5],
                                         nrm[0:16, 5:6]),
           reads=[nrm.k("sq")], writes=[nrm.k("M")])
    P.emit("dve", lambda v: v.tensor_scalar(nrm[0:16, 7:8], nrm[0:16, 6:7],
                                            -1.0, 40.0, ALU.mult, ALU.add),
           reads=[nrm.k("M")], writes=[nrm.k("bias")])
    mst = P.stream("l1p_M", "sp", serial=True)
    P.dma(mst, io["attM"][0:16, 0:1], nrm[0:16, 7:8], reads=[nrm.k("bias")])
    return []


DBGH = [False]


def emit_l1a_att(P, io):
    B = P.banks
    NH = 16
    identf = P.identf
    vaug = sb(P, [128, 32 * 1040], BF16, "avaug")
    vtmp = sb(P, [128, 1024], BF16, "avtmp")
    mskl = sb(P, [128, 128], F32, "amaskl")
    biasb = sb(P, [128, NH], F32, "abias")
    ld = P.stream("l1t_const", "sp", serial=True)
    P.dma(ld, mskl[:], io["mask_low"], writes=[mskl.k()])
    for h in range(NH):
        P.dma(ld, biasb[:, h:h + 1],
              io["attM"][h:h + 1, 0:1].broadcast_to([128, 1]),
              writes=[biasb.k()])
    P.emit("dve", lambda v: v.memset(vaug[:], 1.0), writes=[vaug.k("ones")])
    vst = P.stream("l1t_v", "sp", serial=True)
    for jt in range(32):
        P.dma(vst, vtmp[:], io["vT1"][jt * 128:(jt + 1) * 128, :],
              writes=[vtmp.k()])
        P.emit("dve", lambda v, jt=jt: v.tensor_copy(
            vaug[:, jt * 1040:(jt + 1) * 1040]
            .rearrange("p (h c) -> p h c", h=16)[:, :, 0:64],
            vtmp[:].rearrange("p (h c) -> p h c", h=16)),
            reads=[vtmp.k(), vaug.k("ones")], writes=[vaug.k(jt)])
    qh = [sb(P, [64, T], BF16, f"aqh{b}") for b in range(2)]
    kh = [sb(P, [64, T], BF16, f"akh{b}") for b in range(2)]
    qhst = [P.stream(f"l1t_q{b}", "sp") for b in range(2)]
    khst = [P.stream(f"l1t_k{b}", "sp") for b in range(2)]
    pst = [sb(P, [128, T], BF16, f"apst{b}") for b in range(2)]
    ot = [sb(P, [128, 64], F32, f"aot{b}") for b in range(2)]
    rcp = sb(P, [128, 4], F32, "arcp")
    oTf = sb(P, [128, 8 * T], BF16, "aoTf")
    osst = P.stream("l1t_o", "sp", serial=True)

    def load_head(h):
        b = h % 2
        P.dma(qhst[b], qh[b][:], io["qT"][h // 2][64 * (h % 2):64 * (h % 2) + 64, :],
              writes=[qh[b].k()])
        P.dma(khst[b], kh[b][:], io["kT"][h // 2][64 * (h % 2):64 * (h % 2) + 64, :],
              writes=[kh[b].k()])

    load_head(0)
    load_head(1)

    strips = [(h, qt) for h in range(NH) for qt in range(32)]

    def emit_ST(i):
        h, qt = strips[i]
        b = h % 2
        QH, KH = qh[b], kh[b]
        PS = pst[i % 2]
        jn = qt + 1
        for b0 in range(0, jn, 4):
            bn = min(4, jn - b0)
            pb = B[(b0 // 4) % 4]
            for r in range(bn):
                jt = b0 + r
                P.emit("pe", lambda t, jt=jt, r=r, qt=qt, pb=pb, QH=QH, KH=KH:
                       t.matmul(pb[:, r * 128:(r + 1) * 128],
                                KH[:, jt * 128:(jt + 1) * 128],
                                QH[:, qt * 128:(qt + 1) * 128],
                                start=True, stop=True),
                       reads=[QH.k(), KH.k()], writes=[pb.k()])
                if jt == qt:
                    P.emit("dve", lambda v, r=r, pb=pb: v.tensor_add(
                        pb[:, r * 128:(r + 1) * 128],
                        pb[:, r * 128:(r + 1) * 128], mskl[:]),
                        reads=[pb.k(), mskl.k()], writes=[pb.k()])
            P.emit("act", lambda a, b0=b0, bn=bn, h=h, pb=pb, PS=PS:
                   a.activation(PS[:, b0 * 128:(b0 + bn) * 128],
                                pb[:, 0:bn * 128], AF.Exp,
                                bias=biasb[:, h:h + 1]),
                   reads=[pb.k(), biasb.k()], writes=[PS.k(b0 // 4)])

    def emit_AV(i):
        h, qt = strips[i]
        PS = pst[i % 2]
        jn = qt + 1
        pO = B[4 + i % 2]
        for jt in range(jn):
            P.emit("pe", lambda t, jt=jt, h=h, pO=pO, PS=PS: t.matmul(
                pO[:, 0:65], PS[:, jt * 128:(jt + 1) * 128],
                vaug[:, jt * 1040 + h * 65: jt * 1040 + (h + 1) * 65],
                start=(jt == 0), stop=(jt == jn - 1)),
                reads=[PS.k(jt // 4), vaug.k(jt), vaug.k("ones")],
                writes=[pO.k()])
        OT = ot[i % 2]
        P.emit("dve", lambda v, pO=pO: v.reciprocal(rcp[:, 0:1], pO[:, 64:65]),
               reads=[pO.k()], writes=[rcp.k()])
        P.emit("dve", lambda v, pO=pO, OT=OT: v.tensor_scalar_mul(
            OT[:], pO[:, 0:64], rcp[:, 0:1]),
            reads=[pO.k(), rcp.k()], writes=[OT.k()])
        pt_ = B[6 + i % 2]
        P.emit("pe", lambda t, pt_=pt_, OT=OT: t.transpose(
            pt_[0:64, 0:128], OT[:], identf[:]),
            reads=[OT.k(), identf.k()], writes=[pt_.k()])
        po_part = 64 * (h % 2)
        P.emit("act", lambda a, h=h, qt=qt, pt_=pt_, po_part=po_part: a.copy(
            oTf[po_part:po_part + 64, (h // 2) * T + qt * 128:
                (h // 2) * T + (qt + 1) * 128],
            pt_[0:64, 0:128]),
            reads=[pt_.k()], writes=[oTf.k((h, qt))])
        if qt == 31:
            if h + 2 < NH:
                load_head(h + 2)
            if h % 2 == 1:
                blk = h // 2
                P.dma(osst, io["oT2"][blk], oTf[:, blk * T:(blk + 1) * T],
                      reads=[oTf.k((h, g2)) for g2 in range(32)]
                      + [oTf.k((h - 1, g2)) for g2 in range(32)])

    emit_ST(0)
    for i in range(len(strips)):
        if i + 1 < len(strips):
            emit_ST(i + 1)
        emit_AV(i)
    return ["l1t_o"]


def emit_l1a_wo(P, io):
    B = P.banks
    wo = sb(P, [128, 8 * 1024], BF16, "wo")
    bo = sb(P, [128, D], F32, "bob")
    ld = P.stream("l1w_const", "pool", serial=True)
    P.dma(ld, wo[:], io["woT_pm"], writes=[wo.k()])
    P.dma(ld, bo[:], io["bo_b"], writes=[bo.k()])
    oT = [sb(P, [128, 8 * 128], BF16, f"woTt{b}") for b in range(2)]
    xt = [sb(P, [128, D], F32, f"wxt{b}") for b in range(2)]
    rt = [sb(P, [128, D], F32, f"wrt{b}") for b in range(2)]
    ost = [P.stream(f"l1w_o{b}", "sp") for b in range(2)]
    xst = [P.stream(f"l1w_x{b}", "sp") for b in range(2)]
    rst = [P.stream(f"l1w_r{b}", "sp") for b in range(2)]
    for ti in range(32):
        t0 = ti * 128
        b = ti % 2
        OT, XT, RT = oT[b], xt[b], rt[b]
        P.dma(ost[b], OT[:].rearrange("p (blk t) -> p blk t", blk=8),
              io["oT2"][:, :, t0:t0 + 128].rearrange("blk p t -> p blk t"),
              writes=[OT.k()])
        P.dma(xst[b], XT[:], io["res2"][t0:t0 + 128, :], writes=[XT.k()])
        for half in range(2):
            pp = B[2 + half]
            for blk in range(8):
                P.emit("pe", lambda t, blk=blk, half=half, pp=pp, OT=OT: t.matmul(
                    pp[:], OT[:, blk * 128:(blk + 1) * 128],
                    wo[:, blk * 1024 + half * 512: blk * 1024 + (half + 1) * 512],
                    start=(blk == 0), stop=(blk == 7)),
                    reads=[OT.k(), wo.k()], writes=[pp.k()])
            P.emit("dve", lambda v, half=half, pp=pp, RT=RT: v.tensor_add(
                RT[:, half * 512:(half + 1) * 512], pp[:],
                bo[:, half * 512:(half + 1) * 512]),
                reads=[pp.k(), bo.k()], writes=[RT.k()])
        P.emit("dve", lambda v, RT=RT, XT=XT: v.tensor_add(RT[:], RT[:], XT[:]),
               reads=[RT.k(), XT.k()], writes=[RT.k()])
        P.dma(rst[b], io["res3"][t0:t0 + 128, :], RT[:], reads=[RT.k()])
    return ["l1w_r0", "l1w_r1"]


def emit_l1f_pre(P, io):
    xt = [sb(P, [128, D], F32, f"fpx{b}") for b in range(2)]
    zn = [sb(P, [128, D], F32, f"fpz{b}") for b in range(2)]
    sqt = sb(P, [128, D], F32, "fpsq")
    stats = sb(P, [128, 8], F32, "fpst")
    xst = [P.stream(f"f_x{b}", "sp") for b in range(2)]
    zst = [P.stream(f"f_z{b}", "sp") for b in range(2)]
    for ti in range(32):
        b = ti % 2
        XT, ZN = xt[b], zn[b]
        t0 = ti * 128
        P.dma(xst[b], XT[:], io["res3"][t0:t0 + 128, :], writes=[XT.k()])
        layernorm_tile(P, ZN[:], XT[:], sqt, stats, [XT.k()], ZN.k())
        P.dma(zst[b], io["za"][t0:t0 + 128, :], ZN[:], reads=[ZN.k()])
    return []


def shift_load(P, io, zname, dst, ti, streams, key):
    t0 = ti * 128
    P.dma(streams[0], dst[:, 0:512], io[zname][t0:t0 + 128, 0:512],
          writes=[key])
    if ti == 0:
        P.emit("dve", lambda v: v.memset(dst[0:1, 512:1024], 0.0),
               writes=[key])
        P.dma(streams[1], dst[1:128, 512:1024], io[zname][0:127, 512:1024],
              writes=[key])
    else:
        P.dma(streams[1], dst[:, 512:1024], io[zname][t0 - 1:t0 + 127, 512:1024],
              writes=[key])


def emit_l1f_qkv(P, io, l, zname):
    identf, B = P.identf, P.banks
    wqkv = sb(P, [128, 8 * 1536], BF16, "fw")
    qsb = sb(P, [128, 512], F32, "fqs")
    ksb = sb(P, [128, 512], F32, "fks")
    tcq = sb(P, [128, 32 * 64], F32, "ftcq")
    tsq = sb(P, [128, 32 * 64], F32, "ftsq")
    tck = sb(P, [128, 32 * 64], F32, "ftck")
    tsk = sb(P, [128, 32 * 64], F32, "ftsk")
    ld = P.stream("f_const", "pool", serial=True)
    P.dma(ld, wqkv[:], io["f_wqkv"][l], writes=[wqkv.k()])
    P.dma(ld, qsb[:], io["f_qs_b"][l], writes=[qsb.k()])
    P.dma(ld, ksb[:], io["f_ks_b"][l], writes=[ksb.k()])
    for nm_, t_ in (("xpos_cq", tcq), ("xpos_sq", tsq),
                    ("xpos_ck", tck), ("xpos_sk", tsk)):
        P.dma(ld, t_[:].rearrange("p (tt c) -> p tt c", tt=32),
              io[nm_].rearrange("(tt p) c -> p tt c", p=128), writes=[t_.k()])

    ht = [sb(P, [128, D], F32, f"fht{b}") for b in range(2)]
    hT_ = [sb(P, [128, 8 * 128], BF16, f"fhT{b}") for b in range(2)]
    qf_ = [sb(P, [128, 512], F32, f"fqf{b}") for b in range(2)]
    sq2_ = [sb(P, [128, 512], F32, f"fsq2{b}") for b in range(2)]
    red_ = [sb(P, [128, 24], F32, f"fred{b}") for b in range(2)]
    qx_ = [sb(P, [128, 512], F32, f"fqx{b}") for b in range(2)]
    te_ = [sb(P, [128, 512], F32, f"fte{b}") for b in range(2)]
    qTm = sb(P, [128, 4 * MACRO], BF16, "fqTm")
    kTm = sb(P, [128, 4 * MACRO], BF16, "fkTm")
    hst = [[P.stream(f"f_h{a}{b}", "sp") for b in range(2)] for a in range(2)]
    vst = [P.stream(f"f_v{b}", "pool") for b in range(2)]
    qst = P.stream("f_qT", "sp", serial=True)
    kst = P.stream("f_kT", "sp", serial=True)

    def l2n_xpos(src_key, ct, st_, sb_, dstT, tt, qf, sq2, red, qx, te):
        P.emit("dve", lambda v: v.tensor_mul(sq2[:], qf[:], qf[:]),
               reads=[src_key], writes=[sq2.k()])
        P.emit("dve", lambda v: v.tensor_reduce(
            red[:, 0:8], sq2[:].rearrange("p (g c) -> p g c", g=8),
            AX.X, ALU.add),
            reads=[sq2.k()], writes=[red.k(0)])
        P.emit("act", lambda a: a.activation(red[:, 8:16], red[:, 0:8], AF.Sqrt),
               reads=[red.k(0)], writes=[red.k("n")])
        P.emit("dve", lambda v: v.tensor_scalar_max(red[:, 8:16], red[:, 8:16],
                                                    1e-12),
               reads=[red.k("n")], writes=[red.k("n")])
        P.emit("dve", lambda v: v.reciprocal(red[:, 16:24], red[:, 8:16]),
               reads=[red.k("n")], writes=[red.k("rn")])
        rb = red[:, 16:24].rearrange("p (h o) -> p h o", h=8).broadcast_to([128, 8, 64])
        q3 = qf[:].rearrange("p (h c) -> p h c", h=8)
        P.emit("dve", lambda v: v.tensor_mul(q3, q3, rb),
               reads=[src_key, red.k("rn")], writes=[src_key])
        P.emit("dve", lambda v, sb_=sb_: v.tensor_mul(qf[:], qf[:], sb_[:]),
               reads=[src_key, sb_.k()], writes=[src_key])
        cs = ct[:, (tt % 32) * 64:(tt % 32) * 64 + 64]
        sn = st_[:, (tt % 32) * 64:(tt % 32) * 64 + 64]
        cs3 = cs.rearrange("p (o c) -> p o c", o=1).broadcast_to([128, 8, 64])
        qx3 = qx[:].rearrange("p (h c) -> p h c", h=8)
        P.emit("dve", lambda v: v.tensor_mul(qx3, q3, cs3),
               reads=[src_key, ct.k()], writes=[qx.k()])
        sn1 = sn[:, 0:32].rearrange("p (o c) -> p o c", o=1).broadcast_to([128, 8, 32])
        sn2 = sn[:, 32:64].rearrange("p (o c) -> p o c", o=1).broadcast_to([128, 8, 32])
        te3 = te[:].rearrange("p (h c) -> p h c", h=8)
        P.emit("dve", lambda v: v.tensor_mul(te3[:, :, 0:32], q3[:, :, 32:64], sn1),
               reads=[src_key, st_.k()], writes=[te.k()])
        P.emit("dve", lambda v: v.tensor_mul(te3[:, :, 32:64], q3[:, :, 0:32], sn2),
               reads=[src_key, st_.k()], writes=[te.k()])
        P.emit("dve", lambda v: v.tensor_add(qx[:], qx[:], te[:]),
               reads=[qx.k(), te.k()], writes=[qx.k()])
        pb = B[4 + tt % 2]
        for blk in range(4):
            P.emit("pe", lambda t, blk=blk, pb=pb: t.transpose(
                pb[:, blk * 128:(blk + 1) * 128],
                qx[:, blk * 128:(blk + 1) * 128], identf[:]),
                reads=[qx.k(), identf.k()], writes=[pb.k()])
        P.emit("act", lambda a, tt=tt, pb=pb, dstT=dstT: a.copy(
            dstT[:].rearrange("p (blk t) -> p blk t", blk=4)
            [:, :, (tt % 4) * 128:(tt % 4 + 1) * 128],
            pb[:].rearrange("p (blk t) -> p blk t", blk=4)),
            reads=[pb.k()],
            writes=[dstT.k(0), dstT.k(1), dstT.k(2), dstT.k(3)])

    for m in range(NMAC):
        for tt in range(4):
            ti = m * 4 + tt
            bb = ti % 2
            HT = ht[bb]
            hT, qf, sq2, red = hT_[bb], qf_[bb], sq2_[bb], red_[bb]
            qx, te = qx_[bb], te_[bb]
            shift_load(P, io, zname, HT, ti, hst[ti % 2], HT.k())
            for d0 in range(0, 8, 4):
                pb = B[(d0 // 4) % 2]
                for r in range(4):
                    dk = d0 + r
                    P.emit("pe", lambda t, dk=dk, r=r, pb=pb, HT=HT: t.transpose(
                        pb[:, r * 128:(r + 1) * 128],
                        HT[:, dk * 128:(dk + 1) * 128], identf[:]),
                        reads=[HT.k(), identf.k()], writes=[pb.k()])
                P.emit("act", lambda a, d0=d0, pb=pb, hT=hT: a.copy(
                    hT[:, d0 * 128:(d0 + 4) * 128], pb[:, 0:512]),
                    reads=[pb.k()],
                    writes=[hT.k(d0), hT.k(d0 + 1), hT.k(d0 + 2), hT.k(d0 + 3)])
            for proj in range(3):
                pp = B[2 + proj % 2]
                for dk in range(8):
                    P.emit("pe", lambda t, dk=dk, proj=proj, pp=pp, hT=hT: t.matmul(
                        pp[:], hT[:, dk * 128:(dk + 1) * 128],
                        wqkv[:, dk * 1536 + proj * 512: dk * 1536 + (proj + 1) * 512],
                        start=(dk == 0), stop=(dk == 7)),
                        reads=[hT.k(dk), wqkv.k()], writes=[pp.k()])
                if proj == 2:
                    t0 = ti * 128
                    P.emit("act", lambda a, pp=pp, qf=qf: a.copy(qf[:], pp[:]),
                           reads=[pp.k()], writes=[qf.k()])
                    P.dma(vst[ti % 2], io["v2T"][t0:t0 + 128, :], qf[:],
                          reads=[qf.k()])
                else:
                    P.emit("act", lambda a, pp=pp, qf=qf: a.copy(qf[:], pp[:]),
                           reads=[pp.k()], writes=[qf.k()])
                    if proj == 0:
                        l2n_xpos(qf.k(), tcq, tsq, qsb, qTm, ti,
                                 qf, sq2, red, qx, te)
                    else:
                        l2n_xpos(qf.k(), tck, tsk, ksb, kTm, ti,
                                 qf, sq2, red, qx, te)
        for blk in range(4):
            P.dma(qst, io["q2T"][blk][:, m * MACRO:(m + 1) * MACRO],
                  qTm[:, blk * MACRO:(blk + 1) * MACRO], reads=[qTm.k(blk)])
            P.dma(kst, io["k2T"][blk][:, m * MACRO:(m + 1) * MACRO],
                  kTm[:, blk * MACRO:(blk + 1) * MACRO], reads=[kTm.k(blk)])
    return []


def emit_l1f_att(P, io, l):
    B = P.banks
    vsb = sb(P, [128, 32 * 512], BF16, "fvsb")
    msku = sb(P, [128, 128], F32, "fmsku")
    mskl = sb(P, [128, 128], F32, "fmskl")
    ld = P.stream("f_tconst", "sp", serial=True)
    P.dma(ld, msku[:], io["mask_up"], writes=[msku.k()])
    P.dma(ld, mskl[:], io["mask_low"], writes=[mskl.k()])
    for jt in range(32):
        P.dma(ld, vsb[:, jt * 512:(jt + 1) * 512],
              io["v2T"][jt * 128:(jt + 1) * 128, :], writes=[vsb.k()])
    qh = [sb(P, [64, T], BF16, f"fqh{b}") for b in range(2)]
    kh = [sb(P, [64, T], BF16, f"fkh{b}") for b in range(2)]
    qhst = [P.stream(f"f_tq{b}", "sp") for b in range(2)]
    khst = [P.stream(f"f_tk{b}", "sp") for b in range(2)]
    Sst = [sb(P, [128, 1024], F32, f"fS{b}") for b in range(2)]
    Pst = [sb(P, [128, 1024], F32, f"fP{b}") for b in range(2)]
    PT = [sb(P, [128, 8 * 128], BF16, f"fPT{b}") for b in range(2)]
    red = [sb(P, [128, 16], F32, f"fres{b}") for b in range(2)]
    ot = [sb(P, [128, 64], F32, f"fot{b}") for b in range(2)]
    oTf = sb(P, [128, 4 * T], BF16, "foTf")
    osst = P.stream("f_to", "sp", serial=True)

    def load_head(h):
        b = h % 2
        P.dma(qhst[b], qh[b][:], io["q2T"][h // 2][64 * (h % 2):64 * (h % 2) + 64, :],
              writes=[qh[b].k()])
        P.dma(khst[b], kh[b][:], io["k2T"][h // 2][64 * (h % 2):64 * (h % 2) + 64, :],
              writes=[kh[b].k()])

    load_head(0)
    load_head(1)
    strips = [(h, g) for h in range(8) for g in range(32)]

    def geom(g):
        w = g // 4
        j0 = max(0, (w - 1) * 4)
        jn = g + 1 - j0
        return w, j0, jn

    def p_S(i):
        h, g = strips[i]
        b = h % 2
        QH, KH = qh[b], kh[b]
        SS = Sst[i % 2]
        w, j0, jn = geom(g)
        ext = jn * 128
        nch = (ext + 511) // 512
        for jc in range(nch):
            wd = min(512, ext - jc * 512)
            pS = B[jc % 2]
            P.emit("pe", lambda t, g=g, j0=j0, jc=jc, wd=wd, pS=pS, QH=QH,
                   KH=KH: t.matmul(
                pS[:, 0:wd], QH[:, g * 128:(g + 1) * 128],
                KH[:, j0 * 128 + jc * 512: j0 * 128 + jc * 512 + wd],
                start=True, stop=True),
                reads=[QH.k(), KH.k()], writes=[pS.k()])
            P.emit("act", lambda a, jc=jc, wd=wd, pS=pS, SS=SS: a.copy(
                SS[:, jc * 512:jc * 512 + wd], pS[:, 0:wd]),
                reads=[pS.k()], writes=[SS.k(jc)])
        dg = (g - j0) * 128
        P.emit("dve", lambda v, dg=dg, SS=SS: v.tensor_add(
            SS[:, dg:dg + 128], SS[:, dg:dg + 128], msku[:]),
            reads=[SS.k(dg // 512), msku.k()], writes=[SS.k(dg // 512)])
        if w > 0:
            P.emit("dve", lambda v, SS=SS: v.tensor_add(
                SS[:, 0:128], SS[:, 0:128], mskl[:]),
                reads=[SS.k(0), mskl.k()], writes=[SS.k(0)])

    def p_soft(i):
        h, g = strips[i]
        SS, PS, RD = Sst[i % 2], Pst[i % 2], red[i % 2]
        w, j0, jn = geom(g)
        ext = jn * 128
        nch = (ext + 511) // 512
        for jc in range(nch):
            wd = min(512, ext - jc * 512)
            P.emit("dve", lambda v, jc=jc, wd=wd, SS=SS, RD=RD: v.reduce_max(
                RD[:, jc:jc + 1], SS[:, jc * 512:jc * 512 + wd], axis=AX.X),
                reads=[SS.k(jc)], writes=[RD.k(jc)])
        P.emit("dve", lambda v, nch=nch, RD=RD: v.tensor_reduce(
            RD[:, 8:9], RD[:, 0:nch], AX.X, ALU.max),
            reads=[RD.k(jc) for jc in range(nch)], writes=[RD.k("nm")])
        P.emit("dve", lambda v, RD=RD: v.tensor_scalar_mul(RD[:, 8:9],
                                                           RD[:, 8:9], -1.0),
               reads=[RD.k("nm")], writes=[RD.k("nm")])
        P.emit("act", lambda a, ext=ext, SS=SS, PS=PS, RD=RD: a.activation(
            PS[:, 0:ext], SS[:, 0:ext], AF.Exp, bias=RD[:, 8:9],
            accum_out=RD[:, 9:10]),
            reads=[RD.k("nm")] + [SS.k(jc) for jc in range(nch)],
            writes=[PS.k(), RD.k("rs")])
        P.emit("dve", lambda v, RD=RD: v.reciprocal(RD[:, 10:11], RD[:, 9:10]),
               reads=[RD.k("rs")], writes=[RD.k("rrs")])

    def p_AV(i):
        h, g = strips[i]
        PS, RD, OT = Pst[i % 2], red[i % 2], ot[i % 2]
        PTb = PT[i % 2]
        w, j0, jn = geom(g)
        for b0 in range(0, jn, 4):
            bn = min(4, jn - b0)
            pb = B[2 + (b0 // 4) % 2]
            for r in range(bn):
                jt = b0 + r
                P.emit("pe", lambda t, jt=jt, r=r, pb=pb, PS=PS: t.transpose(
                    pb[:, r * 128:(r + 1) * 128], PS[:, jt * 128:(jt + 1) * 128],
                    P.identf[:]),
                    reads=[PS.k(), P.identf.k()], writes=[pb.k()])
            P.emit("act", lambda a, b0=b0, bn=bn, pb=pb, PTb=PTb: a.copy(
                PTb[:, b0 * 128:(b0 + bn) * 128], pb[:, 0:bn * 128]),
                reads=[pb.k()], writes=[PTb.k(b0 // 4)])
        pO = B[4 + i % 2]
        for jt in range(jn):
            P.emit("pe", lambda t, jt=jt, j0=j0, h=h, pO=pO, PTb=PTb: t.matmul(
                pO[:, 0:64], PTb[:, jt * 128:(jt + 1) * 128],
                vsb[:, (j0 + jt) * 512 + h * 64: (j0 + jt) * 512 + (h + 1) * 64],
                start=(jt == 0), stop=(jt == jn - 1)),
                reads=[PTb.k(jt // 4), vsb.k()], writes=[pO.k()])
        P.emit("dve", lambda v, pO=pO, RD=RD, OT=OT: v.tensor_scalar_mul(
            OT[:], pO[:, 0:64], RD[:, 10:11]),
            reads=[pO.k(), RD.k("rrs")], writes=[OT.k()])
        pt_ = B[6 + i % 2]
        P.emit("pe", lambda t, pt_=pt_, OT=OT: t.transpose(
            pt_[0:64, 0:128], OT[:], P.identf[:]),
            reads=[OT.k(), P.identf.k()], writes=[pt_.k()])
        po_part = 64 * (h % 2)
        P.emit("act", lambda a, h=h, g=g, pt_=pt_, po_part=po_part: a.copy(
            oTf[po_part:po_part + 64, (h // 2) * T + g * 128:
                (h // 2) * T + (g + 1) * 128],
            pt_[0:64, 0:128]),
            reads=[pt_.k()], writes=[oTf.k((h, g))])
        if g == 31:
            if h + 2 < 8:
                load_head(h + 2)
            if h % 2 == 1:
                blk = h // 2
                P.dma(osst, io["o2T"][blk], oTf[:, blk * T:(blk + 1) * T],
                      reads=[oTf.k((h, g2)) for g2 in range(32)]
                      + [oTf.k((h - 1, g2)) for g2 in range(32)])

    p_S(0)
    for i in range(len(strips)):
        p_soft(i)
        if i + 1 < len(strips):
            p_S(i + 1)
        p_AV(i)
    return []


def emit_l1f_wo(P, io, l, zname, outname):
    B = P.banks
    wo = sb(P, [128, 4 * 1024], BF16, "fwo")
    ld = P.stream("f_const", "pool", serial=True)
    P.dma(ld, wo[:], io["f_wo"][l], writes=[wo.k()])
    oT = [sb(P, [128, 4 * 128], BF16, f"fwoT{b}") for b in range(2)]
    xt = [sb(P, [128, D], F32, f"fwx{b}") for b in range(2)]
    rt = [sb(P, [128, D], F32, f"fwr{b}") for b in range(2)]
    ost = [P.stream(f"f_wo{b}", "sp") for b in range(2)]
    xst = [P.stream(f"f_x{b}", "sp") for b in range(2)]
    rst = [P.stream(f"f_z{b}", "sp") for b in range(2)]
    for ti in range(32):
        t0 = ti * 128
        b = ti % 2
        OT, XT, RT = oT[b], xt[b], rt[b]
        P.dma(ost[b], OT[:].rearrange("p (blk t) -> p blk t", blk=4),
              io["o2T"][:, :, t0:t0 + 128].rearrange("blk p t -> p blk t"),
              writes=[OT.k()])
        P.dma(xst[b], XT[:], io[zname][t0:t0 + 128, :], writes=[XT.k()])
        for half in range(2):
            pp = B[2 + half]
            for blk in range(4):
                P.emit("pe", lambda t, blk=blk, half=half, pp=pp, OT=OT: t.matmul(
                    pp[:], OT[:, blk * 128:(blk + 1) * 128],
                    wo[:, blk * 1024 + half * 512: blk * 1024 + (half + 1) * 512],
                    start=(blk == 0), stop=(blk == 3)),
                    reads=[OT.k(), wo.k()], writes=[pp.k()])
            P.emit("dve", lambda v, half=half, pp=pp, RT=RT, XT=XT: v.tensor_add(
                RT[:, half * 512:(half + 1) * 512], pp[:],
                XT[:, half * 512:(half + 1) * 512]),
                reads=[pp.k(), XT.k()], writes=[RT.k()])
        P.dma(rst[b], io[outname][t0:t0 + 128, :], RT[:], reads=[RT.k()])
    return []


def emit_l1f_ffn(P, io, l, zname, outname, resname=None):
    identf, B = P.identf, P.banks
    lng = sb(P, [128, D], F32, "flng")
    lnb = sb(P, [128, D], F32, "flnb")
    b2b = sb(P, [128, D], F32, "fb2")
    b1a = sb(P, [128, 22], F32, "fb1a")
    b1g = sb(P, [128, 22], F32, "fb1g")
    ld = P.stream("f_const", "pool", serial=True)
    P.dma(ld, lng[:], io["f_lng_b"][l], writes=[lng.k()])
    P.dma(ld, lnb[:], io["f_lnb_b"][l], writes=[lnb.k()])
    P.dma(ld, b2b[:], io["f_b2_b"][l], writes=[b2b.k()])
    P.dma(ld, b1a[:], io["f_b1a"][l], writes=[b1a.k()])
    P.dma(ld, b1g[:], io["f_b1g"][l], writes=[b1g.k()])

    ht = [sb(P, [128, D], F32, f"ffh{b}") for b in range(2)]
    zt = [sb(P, [128, D], F32, f"ffz{b}") for b in range(2)]
    hn = sb(P, [128, D], F32, "ffhn")
    sqt = sb(P, [128, D], F32, "ffsq")
    stats = sb(P, [128, 8], F32, "ffst")
    hTm = sb(P, [128, 8 * MACRO], BF16, "ffhTm")
    hst = [[P.stream(f"f_h{a}{b}", "sp") for b in range(2)] for a in range(2)]
    zst = [P.stream(f"f_x{b}", "sp") for b in range(2)]
    r3st = [P.stream(f"f_r3{b}", "sp") for b in range(2)]
    ist = [P.stream(f"f_oi{b}", "sp") for b in range(2)]
    hTst = P.stream("f_hT", "sp", serial=True)
    for m in range(NMAC):
        for tt in range(4):
            ti = m * 4 + tt
            t0 = ti * 128
            b = ti % 2
            HT, ZT = ht[b], zt[b]
            shift_load(P, io, zname, HT, ti, hst[b], HT.k())
            P.dma(zst[b], ZT[:], io[zname][t0:t0 + 128, :], writes=[ZT.k()])
            layernorm_tile(P, hn[:], HT[:], sqt, stats, [HT.k()], hn.k(),
                           gamma=lng, beta=lnb)
            for d0 in range(0, 8, 4):
                pb = B[(d0 // 4) % 2]
                for r in range(4):
                    dk = d0 + r
                    P.emit("pe", lambda t, dk=dk, r=r, pb=pb: t.transpose(
                        pb[:, r * 128:(r + 1) * 128],
                        hn[:, dk * 128:(dk + 1) * 128], identf[:]),
                        reads=[hn.k(), identf.k()], writes=[pb.k()])
                P.emit("act", lambda a, d0=d0, tt=tt, pb=pb: a.copy(
                    hTm[:].rearrange("p (dk t) -> p dk t", dk=8)
                    [:, d0:d0 + 4, tt * 128:(tt + 1) * 128],
                    pb[:].rearrange("p (dk t) -> p dk t", dk=4)),
                    reads=[pb.k()],
                    writes=[hTm.k(d0), hTm.k(d0 + 1), hTm.k(d0 + 2),
                            hTm.k(d0 + 3)])
            P.emit("dve", lambda v, ZT=ZT: v.tensor_add(ZT[:], ZT[:], b2b[:]),
                   reads=[ZT.k(), b2b.k()], writes=[ZT.k()])
            if resname is not None:
                RT = sqt
                P.dma(r3st[b], RT[:], io[resname][t0:t0 + 128, :],
                      writes=[RT.k()])
                P.emit("dve", lambda v, ZT=ZT, RT=RT: v.tensor_add(
                    ZT[:], ZT[:], RT[:]),
                    reads=[ZT.k(), RT.k()], writes=[ZT.k()])
            P.dma(ist[b], io[outname][t0:t0 + 128, :], ZT[:],
                  reads=[ZT.k()], writes=[("hbm_out", ti // 4, "init", tt)])
        P.dma(hTst, io["h2T"][m], hTm[:],
              reads=[hTm.k(dk) for dk in range(8)],
              writes=[("hbm_h2T", m)])

    w1r = [sb(P, [128, 6 * 8 * 128], BF16, f"fw1r{b}") for b in range(2)]
    w3r = [sb(P, [128, 6 * 8 * 128], BF16, f"fw3r{b}") for b in range(2)]
    w2r = [sb(P, [128, 6 * 1024], BF16, f"fw2r{b}") for b in range(2)]
    wst = [[P.stream(f"f_w{w}{b}", "pool") for b in range(2)] for w in range(3)]
    xbuf = [sb(P, [128, 8 * MACRO], BF16, f"ffxb{b}") for b in range(2)]
    xbst = [P.stream(f"f_xb{b}", "sp") for b in range(2)]
    hTc = [sb(P, [128, 6 * MACRO], BF16, f"ffhc{b}") for b in range(2)]
    delta = [sb(P, [128, 4 * D], F32, f"ffd{b}") for b in range(2)]
    dst_ = [P.stream(f"f_acc{b}", "pool") for b in range(2)]
    gel_ = [sb(P, [128, MACRO], F32, f"ffgel{b}") for b in range(2)]

    chunk_seq = [0]

    def load_chunk_next():
        seq = chunk_seq[0]
        if seq >= 4:
            return
        chunk_seq[0] += 1
        c0, cn = MOE_CHUNKS[seq]
        sl = seq % 2
        P.dma(wst[0][sl], w1r[sl][:, 0:cn * 8 * 128],
              io["f_w1a"][l][:, c0 * 8 * 128:(c0 + cn) * 8 * 128],
              writes=[w1r[sl].k()])
        P.dma(wst[1][sl], w3r[sl][:, 0:cn * 8 * 128],
              io["f_w1g"][l][:, c0 * 8 * 128:(c0 + cn) * 8 * 128],
              writes=[w3r[sl].k()])
        P.dma(wst[2][sl], w2r[sl][:, 0:cn * 1024],
              io["f_w2"][l][:, c0 * 1024:(c0 + cn) * 1024],
              writes=[w2r[sl].k()])

    load_chunk_next()
    load_chunk_next()
    iters = [(ci, m) for ci in range(4) for m in range(NMAC)]

    def ab_h(i):
        ci, m = iters[i]
        c0, cn = MOE_CHUNKS[ci]
        sl = ci % 2
        W1, W3 = w1r[sl], w3r[sl]
        xb = xbuf[i % 2]
        P.dma(xbst[i % 2], xb[:], io["h2T"][m],
              reads=[("hbm_h2T", m)], writes=[xb.k()])
        HT = hTc[i % 2]
        for hh in range(cn):
            pa, pg = B[hh % 2], B[2 + hh % 2]
            for k in range(8):
                P.emit("pe", lambda t, hh=hh, k=k, pa=pa, W1=W1, xb=xb: t.matmul(
                    pa[:], W1[:, (hh * 8 + k) * 128:(hh * 8 + k + 1) * 128],
                    xb[:, k * MACRO:(k + 1) * MACRO],
                    start=(k == 0), stop=(k == 7)),
                    reads=[W1.k(), xb.k()], writes=[pa.k()])
            for k in range(8):
                P.emit("pe", lambda t, hh=hh, k=k, pg=pg, W3=W3, xb=xb: t.matmul(
                    pg[:], W3[:, (hh * 8 + k) * 128:(hh * 8 + k + 1) * 128],
                    xb[:, k * MACRO:(k + 1) * MACRO],
                    start=(k == 0), stop=(k == 7)),
                    reads=[W3.k(), xb.k()], writes=[pg.k()])
            gel = gel_[hh % 2]
            P.emit("act", lambda a, hh=hh, ci=ci, c0=c0, pg=pg, gel=gel:
                   a.activation(
                gel[:], pg[:], AF.Gelu, bias=b1g[:, c0 + hh:c0 + hh + 1]),
                reads=[pg.k(), b1g.k()], writes=[gel.k()])
            P.emit("dve", lambda v, hh=hh, ci=ci, c0=c0, pa=pa, HT=HT, gel=gel:
                   v.scalar_tensor_tensor(
                HT[:, hh * MACRO:(hh + 1) * MACRO], pa[:],
                b1a[:, c0 + hh:c0 + hh + 1], gel[:], ALU.add, ALU.mult),
                reads=[pa.k(), b1a.k(), gel.k()], writes=[HT.k(hh)])

    def w2d(i):
        ci, m = iters[i]
        c0, cn = MOE_CHUNKS[ci]
        sl = ci % 2
        W2 = w2r[sl]
        HT = hTc[i % 2]
        DL = delta[i % 2]
        for tt in range(4):
            for dh in range(2):
                po = B[4 + (tt * 2 + dh) % 2]
                for k in range(cn):
                    P.emit("pe", lambda t, tt=tt, dh=dh, k=k, po=po, W2=W2,
                           HT=HT: t.matmul(
                        po[:], HT[:, k * MACRO + tt * 128: k * MACRO + (tt + 1) * 128],
                        W2[:, k * 1024 + dh * 512:k * 1024 + (dh + 1) * 512],
                        start=(k == 0), stop=(k == cn - 1)),
                        reads=[HT.k(k), W2.k()], writes=[po.k()])
                if (tt * 2 + dh) % 2 == 0:
                    P.emit("dve", lambda v, tt=tt, dh=dh, po=po, DL=DL:
                           v.tensor_copy(
                        DL[:, tt * D + dh * 512: tt * D + (dh + 1) * 512], po[:]),
                        reads=[po.k()], writes=[DL.k()])
                else:
                    P.emit("act", lambda a, tt=tt, dh=dh, po=po, DL=DL: a.copy(
                        DL[:, tt * D + dh * 512: tt * D + (dh + 1) * 512], po[:]),
                        reads=[po.k()], writes=[DL.k()])
        t0 = m * MACRO
        P.dma(dst_[i % 2], io[outname][t0:t0 + MACRO, :]
              .rearrange("(tt p) f -> p tt f", p=128),
              DL[:].rearrange("p (tt f) -> p tt f", tt=4),
              reads=[DL.k()] + [("hbm_out", m, "init", t) for t in range(4)],
              writes=[("hbm_out", m)], accum_op=ALU.add)
        if m == NMAC - 1:
            load_chunk_next()

    ab_h(0)
    for i in range(len(iters)):
        if i + 1 < len(iters):
            ab_h(i + 1)
        w2d(i)
    return ["f_acc0", "f_acc1", "f_oi0", "f_oi1"]


STAGES = [
    ("l0a", emit_l0a,
     ["xin", "ident", "inw_pm", "cmw_pm", "outw_pm", "inb_col", "dwb_col",
      "mixb_col", "ld_col", "outb_b", "l0g_b", "l0b_b"],
     ["res1"]),
    ("moe", emit_moe,
     ["res1", "ident", "gateT_pm", "gateb_b", "moe_w1", "moe_w3", "moe_w2"],
     ["res2", "xlnT"]),
    ("l1a_pre", emit_l1a_pre,
     ["res2", "ident", "wqT_pm", "wkT_pm", "wvT_pm", "bq_b", "bk_b", "bv_b",
      "l1scale", "l1a_cos", "l1a_sin"],
     ["qT", "kT", "vT1", "attM"]),
    ("l1a_att", emit_l1a_att,
     ["qT", "kT", "vT1", "attM", "ident", "mask_low"],
     ["oT2"]),
    ("l1a_wo", emit_l1a_wo,
     ["oT2", "res2", "ident", "woT_pm", "bo_b"],
     ["res3"]),
    ("l1f_pre", emit_l1f_pre, ["res3"], ["za"]),
    ("l1f_qkv0", lambda P, io: emit_l1f_qkv(P, io, 0, "za"),
     ["za", "ident", "f_wqkv", "f_qs_b", "f_ks_b",
      "xpos_cq", "xpos_sq", "xpos_ck", "xpos_sk"],
     ["q2T", "k2T", "v2T"]),
    ("l1f_att0", lambda P, io: emit_l1f_att(P, io, 0),
     ["q2T", "k2T", "v2T", "ident", "mask_up", "mask_low"], ["o2T"]),
    ("l1f_wo0", lambda P, io: emit_l1f_wo(P, io, 0, "za", "zb"),
     ["o2T", "za", "ident", "f_wo"], ["zb"]),
    ("l1f_ffn0", lambda P, io: emit_l1f_ffn(P, io, 0, "zb", "zc"),
     ["zb", "ident", "f_lng_b", "f_lnb_b", "f_b2_b", "f_b1a", "f_b1g",
      "f_w1a", "f_w1g", "f_w2"],
     ["zc", "h2T"]),
    ("l1f_qkv1", lambda P, io: emit_l1f_qkv(P, io, 1, "zc"),
     ["zc", "ident", "f_wqkv", "f_qs_b", "f_ks_b",
      "xpos_cq", "xpos_sq", "xpos_ck", "xpos_sk"],
     ["q2T", "k2T", "v2T"]),
    ("l1f_att1", lambda P, io: emit_l1f_att(P, io, 1),
     ["q2T", "k2T", "v2T", "ident", "mask_up", "mask_low"], ["o2T"]),
    ("l1f_wo1", lambda P, io: emit_l1f_wo(P, io, 1, "zc", "zb"),
     ["o2T", "zc", "ident", "f_wo"], ["zb"]),
    ("l1f_ffn1", lambda P, io: emit_l1f_ffn(P, io, 1, "zb", "out",
                                            resname="res3"),
     ["zb", "res3", "ident", "f_lng_b", "f_lnb_b", "f_b2_b", "f_b1a", "f_b1g",
      "f_w1a", "f_w1g", "f_w2"],
     ["out", "h2T"]),
]

FINAL_OUT = "out"


def build_group(stages, ext_in, ext_out):
    nc = bass.Bass("TRN2", target_bir_lowering=False)
    with contextlib.ExitStack() as stack:
        io = {}
        names = {"ident"}
        for (_, _, rd, wr) in stages:
            names |= set(rd) | set(wr)
        for n in sorted(names):
            if n == "xin":
                shape, dt = [T, D], F32
            elif n in HOST_SPECS:
                shape, dt = HOST_SPECS[n], F32
            else:
                shp, dts = SCRATCH_SPECS[n]
                shape, dt = shp, DT[dts]
            kind = "Internal"
            if n in ext_in:
                kind = "ExternalInput"
            elif n in ext_out:
                kind = "ExternalOutput"
            io[n] = nc.dram_tensor(n, list(shape), dt, kind=kind).ap()
        with nc.Block() as block:
            P = Prog(nc, stack)

            def body(_):
                nc.gpsimd.dma_reset(range(153, 256))
                nc.gpsimd.sem_clear(range(153, 256))
                nc.all_engine_barrier()
                prelude(P, io)
                out_streams = []
                for i, (nm, fn, rd, wr) in enumerate(stages):
                    if i:
                        P.barrier()
                    with contextlib.ExitStack() as sc:
                        P.scope = sc
                        out_streams += fn(P, io)
                    P.scope = stack
                P.finish(out_streams)

            block.sync(body)
    return nc


def run_groups(groups, inputs, n_cores=8):
    host = prep(inputs)
    xin = np.ascontiguousarray(np.asarray(inputs["L0_attn_x"], dtype=np.float32))
    state = [dict() for _ in range(n_cores)]
    final = None
    for gi, group in enumerate(groups):
        produced, consumed = set(), set()
        for (_, _, rd, wr) in group:
            consumed |= set(rd) - produced
            produced |= set(wr)
        later_needed = set()
        for g2 in groups[gi + 1:]:
            for (_, _, rd, _w) in g2:
                later_needed |= set(rd)
        is_last = gi == len(groups) - 1
        ext_out = produced & later_needed
        if is_last:
            ext_out |= {FINAL_OUT}
        ext_in = consumed - produced
        nc = build_group(group, ext_in, ext_out)
        in_maps = []
        for c in range(n_cores):
            m = {}
            for n in ext_in:
                if n == "xin":
                    m[n] = xin[c]
                elif n in HOST_SPECS:
                    m[n] = host[n]
                else:
                    m[n] = state[c][n]
            in_maps.append(m)
        res = run_bass_kernel_spmd(nc, in_maps, core_ids=list(range(n_cores)))
        for c in range(n_cores):
            for n, arr in res.results[c].items():
                state[c][n] = np.asarray(arr)
        if is_last:
            final = [state[c][FINAL_OUT] for c in range(n_cores)]
    return np.stack(final)


FUSED = True


def kernel(**inputs):
    groups = [list(STAGES)] if FUSED else [[s] for s in STAGES]
    return run_groups(groups, inputs)


# version 11 (final)
# speedup vs baseline: 1.1838x; 1.0415x over previous
import contextlib
import numpy as np
import concourse.bass as bass
import concourse.mybir as mybir
from concourse.bass_utils import run_bass_kernel_spmd

F32 = mybir.dt.float32
BF16 = mybir.dt.bfloat16
AF = mybir.ActivationFunctionType
ALU = mybir.AluOpType
AX = mybir.AxisListType

T = 4096
D = 1024
MACRO = 512
NMAC = T // MACRO
EPS = 1e-5

ENG = ("pe", "dve", "act", "sp", "pool")
DEBUG = False
BYPASS_LN = False


class Prog:

    def __init__(self, nc, stack):
        self.nc = nc
        self.base = stack
        self.scope = stack
        self.eng = dict(pe=nc.tensor, dve=nc.vector, act=nc.scalar,
                        sp=nc.sync, pool=nc.gpsimd)
        self.sem = {e: stack.enter_context(nc.semaphore(f"sem_{e}"))
                    for e in ("pe", "dve", "act")}
        self.instrs = {e: [] for e in ENG}
        self.marked = {e: [] for e in ENG}
        self.last_write = {}
        self.readers = {}
        self.waited = {e: {} for e in ENG}
        self.streams = {}

    def _resolve(self, tok):
        if tok[0] == "d":
            s = self.streams[tok[1]]
            return s[0], 16 * tok[2]
        _, e, rec = tok
        if rec[2] is None:
            for m in self.marked[e]:
                if m[1] >= rec[1]:
                    return self.sem[e], m[2]
            rec[2] = len(self.marked[e]) + 1
            self.marked[e].append(rec)
            rec[0].then_inc(self.sem[e], 1)
        return self.sem[e], rec[2]

    def _wait(self, e, sem_obj, val):
        key = id(sem_obj)
        if self.waited[e].get(key, 0) >= val:
            return
        self.eng[e].wait_ge(sem_obj, val)
        self.waited[e][key] = val

    def _deps(self, e, reads, writes):
        def skip(tok):
            if tok[0] != "e" or tok[1] != e:
                return False
            return e in ("pe", "sp", "pool")
        toks = []
        for k in reads:
            lw = self.last_write.get(k)
            if lw is not None and not skip(lw):
                toks.append(lw)
        for k in writes:
            lw = self.last_write.get(k)
            if lw is not None and not skip(lw):
                toks.append(lw)
            for rt in self.readers.get(k, ()):
                if not skip(rt):
                    toks.append(rt)
        return toks

    def _update(self, tok, reads, writes):
        for k in reads:
            self.readers.setdefault(k, []).append(tok)
        for k in writes:
            self.last_write[k] = tok
            self.readers[k] = []

    def emit(self, e, fn, reads=(), writes=()):
        for tok in self._deps(e, reads, writes):
            sem_obj, val = self._resolve(tok)
            self._wait(e, sem_obj, val)
        instr = fn(self.eng[e])
        rec = [instr, len(self.instrs[e]), None]
        self.instrs[e].append(rec)
        self._update(("e", e, rec), reads, writes)
        return rec

    def stream(self, name, q="sp", serial=False):
        if name not in self.streams:
            sem = self.base.enter_context(self.nc.semaphore(f"dma_{name}"))
            self.streams[name] = [sem, 0, q, serial]
        return name

    def dma(self, sname, dst, src, reads=(), writes=(), **kw):
        s = self.streams[sname]
        q = s[2]
        if s[3] and s[1] > 0:
            self._wait(q, s[0], 16 * s[1])
        for tok in self._deps(q, reads, writes):
            sem_obj, val = self._resolve(tok)
            self._wait(q, sem_obj, val)
        instr = self.eng[q].dma_start(dst, src, **kw)
        s[1] += 1
        instr.then_inc(s[0], 16)
        rec = [instr, len(self.instrs[q]), None]
        self.instrs[q].append(rec)
        self._update(("d", sname, s[1]), reads, writes)
        return rec

    def barrier(self):
        toks = []
        for e in ("pe", "dve", "act"):
            if self.instrs[e]:
                rec = self.instrs[e][-1]
                toks.append(self._resolve(("e", e, rec)))
        for name, s in self.streams.items():
            if s[1] > 0:
                toks.append((s[0], 16 * s[1]))
        for e in ENG:
            for sem_obj, val in toks:
                self._wait(e, sem_obj, val)

    def finish(self, out_streams):
        for name in set(out_streams):
            s = self.streams[name]
            self._wait("pool", s[0], 16 * s[1])


class Tens:
    def __init__(self, h, name):
        self.h = h
        self.name = name

    def __getitem__(self, idx):
        return self.h[idx]

    def k(self, tag=None):
        return (self.name, tag)


_uid = [0]


def sb(P, shape, dtype, name=None, base=False):
    _uid[0] += 1
    nm = f"{name or 'sb'}_{_uid[0]}"
    st = P.base if base else P.scope
    h = st.enter_context(P.nc.sbuf_tensor(nm, list(shape), dtype))
    return Tens(h, nm)


def psbank(P, name=None):
    _uid[0] += 1
    nm = f"{name or 'ps'}_{_uid[0]}"
    h = P.base.enter_context(P.nc.psum_tensor(nm, [128, 512], F32))
    return Tens(h, nm)


def prelude(P, io):
    P.banks = [psbank(P, f"B{i}") for i in range(8)]
    P.identf = sb(P, [128, 128], F32, "identf", base=True)
    P.identb = sb(P, [128, 128], BF16, "identb", base=True)
    ld = P.stream("init_const", "pool", serial=True)
    P.dma(ld, P.identf[:], io["ident"], writes=[P.identf.k()])
    P.dma(ld, P.identb[:], io["ident"], writes=[P.identb.k()])


def layernorm_tile(P, out_ap, x_ap, tmp, st, xkeys, okey, n=D,
                   gamma=None, beta=None):
    xkeys = list(xkeys)
    P.emit("act", lambda a: a.activation(out_ap, x_ap, AF.Square,
                                         accum_out=st[:, 0:1]),
           reads=xkeys, writes=[(okey, "trash"), st.k("ssq")])
    P.emit("dve", lambda v: v.reduce_sum(st[:, 1:2], x_ap, axis=AX.X),
           reads=xkeys, writes=[st.k("s")])
    P.emit("dve", lambda v: v.tensor_scalar_mul(st[:, 2:3], st[:, 1:2], 1.0 / n),
           reads=[st.k("s")], writes=[st.k("mu")])
    P.emit("dve", lambda v: v.tensor_mul(st[:, 3:4], st[:, 2:3], st[:, 2:3]),
           reads=[st.k("mu")], writes=[st.k("mu2")])
    P.emit("dve", lambda v: v.tensor_scalar(st[:, 4:5], st[:, 0:1], 1.0 / n,
                                            None, ALU.mult),
           reads=[st.k("ssq")], writes=[st.k("var")])
    P.emit("dve", lambda v: v.tensor_sub(st[:, 4:5], st[:, 4:5], st[:, 3:4]),
           reads=[st.k("var"), st.k("mu2")], writes=[st.k("var")])
    P.emit("dve", lambda v: v.tensor_scalar_add(st[:, 4:5], st[:, 4:5], EPS),
           reads=[st.k("var")], writes=[st.k("var")])
    P.emit("act", lambda a: a.activation(st[:, 5:6], st[:, 4:5], AF.Sqrt),
           reads=[st.k("var")], writes=[st.k("std")])
    P.emit("dve", lambda v: v.reciprocal(st[:, 6:7], st[:, 5:6]),
           reads=[st.k("std")], writes=[st.k("rstd")])
    P.emit("dve", lambda v: v.tensor_mul(st[:, 7:8], st[:, 2:3], st[:, 6:7]),
           reads=[st.k("mu"), st.k("rstd")], writes=[st.k("nb")])
    P.emit("dve", lambda v: v.tensor_scalar_mul(st[:, 7:8], st[:, 7:8], -1.0),
           reads=[st.k("nb")], writes=[st.k("nb")])
    P.emit("act", lambda a: a.activation(out_ap, x_ap, AF.Identity,
                                         bias=st[:, 7:8], scale=st[:, 6:7]),
           reads=xkeys + [st.k("rstd"), st.k("nb")], writes=[okey])
    if gamma is not None:
        P.emit("dve", lambda v: v.tensor_mul(out_ap, out_ap, gamma[:, 0:n]),
               reads=[okey, gamma.k()], writes=[okey])
        P.emit("dve", lambda v: v.tensor_add(out_ap, out_ap, beta[:, 0:n]),
               reads=[okey, beta.k()], writes=[okey])


def _pm(tiles):
    t = np.asarray(tiles)
    P_ = t.shape[-2]
    lead = int(np.prod(t.shape[:-2], dtype=np.int64))
    t = t.reshape(lead, P_, t.shape[-1])
    return np.ascontiguousarray(t.transpose(1, 0, 2).reshape(P_, -1))


def _lhs_tiles(A):
    K, M = A.shape
    return A.reshape(K // 128, 128, M // 128, 128).transpose(2, 0, 1, 3)


def _pad_rows(A, to):
    out = np.zeros((to,) + A.shape[1:], dtype=A.dtype)
    out[: A.shape[0]] = A
    return out


def _col128(v):
    return np.ascontiguousarray(v.reshape(-1, 128).T)


def prep(inputs):
    g = {}
    f32 = np.float32

    g["ident"] = np.eye(128, dtype=f32)
    hd = 64
    freqs = (10000.0 ** (-np.arange(0, hd, 2, dtype=np.float64) / hd))
    ang = np.arange(T, dtype=np.float64)[:, None] * freqs[None, :]
    g["l1a_cos"] = np.cos(ang).astype(f32)
    g["l1a_sin"] = np.sin(ang).astype(f32)
    half = np.arange(0, hd, 2, dtype=np.float64)
    inv_freq = 1.0 / 10000.0 ** (half / hd)
    t_ = np.arange(T, dtype=np.float64)
    fr = t_[:, None] * inv_freq[None, :]
    fr64 = np.concatenate([fr, fr], 1)
    base = (half + 0.4 * hd) / (1.4 * hd)
    power = (t_ - T // 2) / 256.0
    sc = base[None, :] ** power[:, None]
    sc64 = np.concatenate([sc, sc], 1)
    cos64, sin64 = np.cos(fr64), np.sin(fr64)
    sgn = np.concatenate([-np.ones(32), np.ones(32)])[None, :]
    g["xpos_cq"] = (cos64 * sc64).astype(f32)
    g["xpos_sq"] = (sin64 * sc64 * sgn).astype(f32)
    g["xpos_ck"] = (cos64 / sc64).astype(f32)
    g["xpos_sk"] = (sin64 / sc64 * sgn).astype(f32)
    g["mask_up"] = np.triu(np.full((128, 128), -1e9, dtype=f32), 1)
    g["mask_low"] = np.tril(np.full((128, 128), -1e9, dtype=f32), -1)

    g["inw_pm"] = _pm(_lhs_tiles(inputs["L0_attn_in_w"]))
    g["outw_pm"] = _pm(inputs["L0_attn_out_w"].reshape(16, 128, 1024))
    g["inb_col"] = _col128(inputs["L0_attn_in_b"])
    g["dwb_col"] = _col128(inputs["L0_attn_dw_b"])
    g["mixb_col"] = _col128(inputs["L0_attn_mix_b"])
    g["ld_col"] = _col128(inputs["L0_attn_log_decay"])
    g["outb_b"] = np.broadcast_to(inputs["L0_attn_out_b"], (128, D)).copy()
    g["l0g_b"] = np.broadcast_to(inputs["L0_attn_norm_gamma"], (128, D)).copy()
    g["l0b_b"] = np.broadcast_to(inputs["L0_attn_norm_beta"], (128, D)).copy()
    dw = inputs["L0_attn_dw_w"]
    dwdiag = np.zeros((16, 7, 128, 128), dtype=f32)
    idx = np.arange(128)
    for gi in range(16):
        for k in range(7):
            dwdiag[gi, k, idx, idx] = dw[gi * 128 + idx, k]
    mixt = _lhs_tiles(inputs["L0_attn_mix_w"])
    cmw = np.concatenate([dwdiag, mixt], axis=1)
    g["cmw_pm"] = _pm(cmw)

    w1 = np.concatenate([inputs["L0_ffn_w1"], inputs["L0_ffn_new_w1"]], 0)
    w2 = np.concatenate([inputs["L0_ffn_w2"], inputs["L0_ffn_new_w2"]], 0)
    w3 = np.concatenate([inputs["L0_ffn_w3"], inputs["L0_ffn_new_w3"]], 0)
    g["moe_w1"] = np.stack([_pm(_lhs_tiles(np.ascontiguousarray(w1[e].T)))
                            for e in range(9)])
    g["moe_w3"] = np.stack([_pm(_lhs_tiles(np.ascontiguousarray(w3[e].T)))
                            for e in range(9)])
    g["moe_w2"] = np.stack([_pm(np.ascontiguousarray(w2[e].T).reshape(22, 128, 1024))
                            for e in range(9)])
    gate = np.concatenate([inputs["L0_ffn_gate_weight"],
                           inputs["L0_ffn_new_gate_weight"]], 0)
    g["gateT_pm"] = _pm(np.ascontiguousarray(gate.T).reshape(8, 128, 9))
    gb = np.zeros((128, 9), dtype=f32)
    gb[:, 8] = np.asarray(inputs["L0_ffn_gate_bias"])[0]
    g["gateb_b"] = gb

    for nm, W in (("wq", "L1_attn_Wq"), ("wk", "L1_attn_Wk"),
                  ("wv", "L1_attn_Wv"), ("wo", "L1_attn_Wo")):
        g[nm + "T_pm"] = _pm(np.ascontiguousarray(inputs[W].T).reshape(8, 128, 1024))
    for nm, B in (("bq", "L1_attn_bq"), ("bk", "L1_attn_bk"),
                  ("bv", "L1_attn_bv"), ("bo", "L1_attn_bo")):
        g[nm + "_b"] = np.broadcast_to(inputs[B], (128, D)).copy()
    g["l1scale"] = np.broadcast_to(np.asarray(inputs["L1_attn_scale"]), (128, 1)).copy()

    g["f_wqkv"] = np.stack([_pm(np.ascontiguousarray(inputs["L1_ffn_wqkv"][l])
                                .reshape(8, 128, 1536)) for l in range(2)])
    g["f_wo"] = np.stack([_pm(np.ascontiguousarray(inputs["L1_ffn_wo"][l])
                              .reshape(4, 128, 1024)) for l in range(2)])
    w1f = inputs["L1_ffn_w1"]
    g["f_w1a"] = np.stack([_pm(_lhs_tiles(_pad_rows(
        np.ascontiguousarray(w1f[l][:, :2730].T), 22 * 128).T.copy()))
        for l in range(2)])
    g["f_w1g"] = np.stack([_pm(_lhs_tiles(_pad_rows(
        np.ascontiguousarray(w1f[l][:, 2730:].T), 22 * 128).T.copy()))
        for l in range(2)])
    g["f_w2"] = np.stack([_pm(_pad_rows(inputs["L1_ffn_w2"][l], 22 * 128)
                              .reshape(22, 128, 1024)) for l in range(2)])
    b1 = inputs["L1_ffn_b1"]
    g["f_b1a"] = np.stack([_col128(_pad_rows(b1[l][:2730], 22 * 128))
                           for l in range(2)])
    g["f_b1g"] = np.stack([_col128(_pad_rows(b1[l][2730:], 22 * 128))
                           for l in range(2)])
    g["f_b2_b"] = np.stack([np.broadcast_to(inputs["L1_ffn_b2"][l], (128, D)).copy()
                            for l in range(2)])
    g["f_lng_b"] = np.stack([np.broadcast_to(inputs["L1_ffn_ln_g"][l], (128, D)).copy()
                             for l in range(2)])
    g["f_lnb_b"] = np.stack([np.broadcast_to(inputs["L1_ffn_ln_b"][l], (128, D)).copy()
                             for l in range(2)])
    g["f_qs_b"] = np.stack([np.broadcast_to(np.tile(np.asarray(inputs["L1_ffn_q_scale"][l]), 8),
                                            (128, 512)).copy() for l in range(2)])
    g["f_ks_b"] = np.stack([np.broadcast_to(np.tile(np.asarray(inputs["L1_ffn_k_scale"][l]), 8),
                                            (128, 512)).copy() for l in range(2)])
    for k in g:
        g[k] = np.ascontiguousarray(g[k], dtype=f32)
    return g


HOST_SPECS = {
    "ident": [128, 128],
    "l1a_cos": [T, 32], "l1a_sin": [T, 32],
    "xpos_cq": [T, 64], "xpos_sq": [T, 64],
    "xpos_ck": [T, 64], "xpos_sk": [T, 64],
    "mask_up": [128, 128], "mask_low": [128, 128],
    "inw_pm": [128, 32 * 8 * 128],
    "cmw_pm": [128, 16 * 23 * 128],
    "outw_pm": [128, 16 * 1024],
    "inb_col": [128, 32], "dwb_col": [128, 16],
    "mixb_col": [128, 16], "ld_col": [128, 16],
    "outb_b": [128, D], "l0g_b": [128, D], "l0b_b": [128, D],
    "moe_w1": [9, 128, 22 * 8 * 128],
    "moe_w3": [9, 128, 22 * 8 * 128],
    "moe_w2": [9, 128, 22 * 1024],
    "gateT_pm": [128, 8 * 9], "gateb_b": [128, 9],
    "wqT_pm": [128, 8 * 1024], "wkT_pm": [128, 8 * 1024],
    "wvT_pm": [128, 8 * 1024], "woT_pm": [128, 8 * 1024],
    "bq_b": [128, D], "bk_b": [128, D], "bv_b": [128, D], "bo_b": [128, D],
    "l1scale": [128, 1],
    "f_wqkv": [2, 128, 8 * 1536],
    "f_wo": [2, 128, 4 * 1024],
    "f_w1a": [2, 128, 22 * 8 * 128],
    "f_w1g": [2, 128, 22 * 8 * 128],
    "f_w2": [2, 128, 22 * 1024],
    "f_b1a": [2, 128, 22], "f_b1g": [2, 128, 22],
    "f_b2_b": [2, 128, D], "f_lng_b": [2, 128, D], "f_lnb_b": [2, 128, D],
    "f_qs_b": [2, 128, 512], "f_ks_b": [2, 128, 512],
}

SCRATCH_SPECS = {
    "dbg_xnT": ([NMAC, 128, 8 * MACRO], "bf16"),
    "dbg_g": ([NMAC, 128, 16 * MACRO], "bf16"),
    "dbg_v": ([NMAC, 128, 16 * (MACRO + 6)], "bf16"),
    "dbg_y": ([NMAC, 128, 16 * MACRO], "bf16"),
    "dbg_h": ([NMAC, 128, 16 * MACRO], "bf16"),
    "dbg_ln": ([NMAC, 128, 2 * D], "bf16"),
    "dbg_att": ([128, 1232], "f32"),
    "dbg_st": ([NMAC, 128, 32], "f32"),
    "res1": ([T, D], "f32"),
    "res2": ([T, D], "f32"),
    "res3": ([T, D], "f32"),
    "xlnT": ([NMAC, 128, 8 * MACRO], "bf16"),
    "qT": ([8, 128, T], "bf16"),
    "kT": ([8, 128, T], "bf16"),
    "vT1": ([T, D], "bf16"),
    "attM": ([16, 1], "f32"),
    "oT2": ([8, 128, T], "bf16"),
    "za": ([T, D], "f32"),
    "zb": ([T, D], "f32"),
    "zc": ([T, D], "f32"),
    "q2T": ([4, 128, T], "bf16"),
    "k2T": ([4, 128, T], "bf16"),
    "v2T": ([T, 512], "bf16"),
    "o2T": ([4, 128, T], "bf16"),
    "h2T": ([NMAC, 128, 8 * MACRO], "bf16"),
    "out": ([T, D], "f32"),
}

DT = {"f32": F32, "bf16": BF16}


def emit_l0a(P, io):
    identf = P.identf
    outw = sb(P, [128, 16 * 1024], BF16, "outw")
    inb = sb(P, [128, 32], F32, "inb")
    bcomb = sb(P, [128, 16], F32, "bcomb")
    dec = sb(P, [128, 16], F32, "dec")
    outb = sb(P, [128, D], F32, "outb")
    gam = sb(P, [128, D], F32, "gam")
    bet = sb(P, [128, D], F32, "bet")
    tmpc = sb(P, [128, 16], F32, "tmpc")
    carry = sb(P, [128, 16], F32, "carry")

    ld = P.stream("l0_const", "pool", serial=True)
    P.dma(ld, outw[:], io["outw_pm"], writes=[outw.k()])
    P.dma(ld, inb[:], io["inb_col"], writes=[inb.k()])
    P.dma(ld, tmpc[:], io["dwb_col"], writes=[tmpc.k()])
    P.dma(ld, bcomb[:], io["mixb_col"], writes=[bcomb.k()])
    P.dma(ld, dec[:], io["ld_col"], writes=[dec.k()])
    P.dma(ld, outb[:], io["outb_b"], writes=[outb.k()])
    P.dma(ld, gam[:], io["l0g_b"], writes=[gam.k()])
    P.dma(ld, bet[:], io["l0b_b"], writes=[bet.k()])
    P.emit("dve", lambda v: v.tensor_add(bcomb[:], bcomb[:], tmpc[:]),
           reads=[bcomb.k(), tmpc.k()], writes=[bcomb.k()])
    P.emit("act", lambda a: a.activation(dec[:], dec[:], AF.Sigmoid),
           reads=[dec.k()], writes=[dec.k()])

    CT = 1
    NCH = 32 // CT
    NSLOT = 3
    inw_sl = [sb(P, [128, CT * 8 * 128], BF16, f"inw{s}") for s in range(NSLOT)]
    inw_st = [P.stream(f"l0_inw{s}", "pool") for s in range(NSLOT)]

    inw_seq = [0]

    def load_inw_next():
        seq = inw_seq[0]
        if seq >= NMAC * NCH:
            return
        inw_seq[0] += 1
        j = seq % NCH
        sl = seq % NSLOT
        P.dma(inw_st[sl], inw_sl[sl][:],
              io["inw_pm"][:, j * CT * 1024:(j + 1) * CT * 1024],
              writes=[inw_sl[sl].k()])

    MT = 1
    NMH = 16 // MT
    MSLOT = 3
    mix_sl = [sb(P, [128, 23 * 128], BF16, f"mix{s}") for s in range(MSLOT)]
    mix_st = [P.stream(f"l0_mix{s}", "pool") for s in range(MSLOT)]

    mix_seq = [0]

    def load_mix_next():
        seq = mix_seq[0]
        if seq >= NMAC * NMH:
            return
        mix_seq[0] += 1
        j = seq % NMH
        sl = seq % MSLOT
        P.dma(mix_st[sl], mix_sl[sl][:],
              io["cmw_pm"][:, j * 23 * 128:(j + 1) * 23 * 128],
              writes=[mix_sl[sl].k()])

    xt = [sb(P, [128, 4 * D], F32, f"xt{b}") for b in range(2)]
    ln1t_ = [sb(P, [128, D], F32, f"ln1t{b}") for b in range(2)]
    xn_ = [sb(P, [128, D], F32, f"xn{b}") for b in range(2)]
    sqt_ = [None, None]
    stats_ = [sb(P, [128, 8], F32, f"stats{b}") for b in range(2)]
    xnT_ = [sb(P, [128, 8 * MACRO], BF16, f"xnTa"),
            sb(P, [128, 8 * MACRO], BF16, f"xnTb")]
    gmac = sb(P, [128, 16 * MACRO], BF16, "gmac")
    vmac = sb(P, [128, 16 * (MACRO + 6)], BF16, "vmac")
    tail = sb(P, [128, 16 * 6], BF16, "tail")
    ymac = sb(P, [128, 16 * MACRO], BF16, "ymac")
    hmac = sb(P, [128, 16 * MACRO], BF16, "hmac")
    vtmp = sb(P, [128, MACRO], F32, "vtmp")
    res_t = [sb(P, [128, D], F32, f"res_t{b}") for b in range(2)]
    if DEBUG:
        dbgln = sb(P, [128, 2 * D], BF16, "dbgln")
        dbgst = sb(P, [128, 32], F32, "dbgst")

    xst = [[P.stream(f"l0_x{b}_{tt}", "sp") for tt in range(4)] for b in range(2)]
    rst = [P.stream(f"l0_res{b}", "sp") for b in range(2)]
    B = P.banks

    P.emit("dve", lambda v: v.memset(tail[:], 0.0), writes=[tail.k()])
    for _ in range(NSLOT - 1):
        load_inw_next()
        load_mix_next()

    def ln_phase(m):
        XT = xt[m % 2]
        xnT = xnT_[m % 2]
        for tt in range(4):
            t0 = m * MACRO + tt * 128
            P.dma(xst[m % 2][tt], XT[:, tt * D:(tt + 1) * D],
                  io["xin"][t0:t0 + 128, :], writes=[XT.k(tt)])
        for tt in range(4):
            xa = XT[:, tt * D:(tt + 1) * D]
            ln1t, xn = ln1t_[tt % 2], xn_[tt % 2]
            sqt, stats = sqt_[tt % 2], stats_[tt % 2]
            if BYPASS_LN:
                P.emit("dve", lambda v, xa=xa, ln1t=ln1t: v.tensor_copy(ln1t[:], xa),
                       reads=[XT.k(tt)], writes=[ln1t.k()])
                P.emit("dve", lambda v, xn=xn, ln1t=ln1t: v.tensor_copy(xn[:], ln1t[:]),
                       reads=[ln1t.k()], writes=[xn.k()])
            else:
                layernorm_tile(P, ln1t[:], xa, sqt, stats, [XT.k(tt)], ln1t.k())
                if DEBUG:
                    if tt < 2:
                        P.emit("dve", lambda v, tt=tt: v.tensor_copy(
                            dbgln[:, tt * D:(tt + 1) * D], ln1t[:]),
                            reads=[ln1t.k()], writes=[dbgln.k(tt)])
                    P.emit("dve", lambda v, tt=tt: v.tensor_copy(
                        dbgst[:, tt * 8:(tt + 1) * 8], stats[:]),
                        reads=[stats.k(kk) for kk in
                               ("ssq", "s", "mu", "mu2", "var", "std", "rstd", "nb")],
                        writes=[dbgst.k(tt)])
                layernorm_tile(P, xn[:], ln1t[:], sqt, stats, [ln1t.k()], xn.k(),
                               gamma=gam, beta=bet)
            for d0 in range(0, 8, 4):
                pb = B[6 + (d0 // 4) % 2]
                for r in range(4):
                    dk = d0 + r
                    P.emit("pe", lambda t, dk=dk, r=r, pb=pb, xn=xn: t.transpose(
                        pb[:, r * 128:(r + 1) * 128],
                        xn[:, dk * 128:(dk + 1) * 128], identf[:]),
                        reads=[xn.k(), identf.k()], writes=[pb.k()])
                P.emit("act", lambda a, tt=tt, d0=d0, pb=pb, xnT=xnT: a.copy(
                    xnT[:].rearrange("p (dk t) -> p dk t", dk=8)
                    [:, d0:d0 + 4, tt * 128:(tt + 1) * 128],
                    pb[:].rearrange("p (dk t) -> p dk t", dk=4)),
                    reads=[pb.k()],
                    writes=[xnT.k(d0), xnT.k(d0 + 1), xnT.k(d0 + 2), xnT.k(d0 + 3)])
            P.emit("dve", lambda v, xa=xa, ln1t=ln1t: v.tensor_add(xa, xa, ln1t[:]),
                   reads=[XT.k(tt), ln1t.k()], writes=[XT.k(tt)])

    def compute_phase(m):
        XT = xt[m % 2]
        xnT = xnT_[m % 2]
        for c in range(32):
            j = c // CT
            buf = inw_sl[(m * NCH + j) % NSLOT]
            coff = (c % CT) * 8 * 128
            pb = B[c % 2]
            for k in range(8):
                P.emit("pe", lambda t, k=k, buf=buf, coff=coff, pb=pb:
                       t.matmul(pb[:], buf[:, coff + k * 128:coff + (k + 1) * 128],
                                xnT[:, k * MACRO:(k + 1) * MACRO],
                                start=(k == 0), stop=(k == 7)),
                       reads=[buf.k(), xnT.k(k)], writes=[pb.k()])
            if c < 16:
                gi = c
                P.emit("act", lambda a, gi=gi, pb=pb: a.activation(
                    gmac[:, gi * MACRO:(gi + 1) * MACRO], pb[:], AF.Sigmoid,
                    bias=inb[:, gi:gi + 1]),
                    reads=[pb.k(), inb.k()], writes=[gmac.k(gi)])
            else:
                gi = c - 16
                P.emit("act", lambda a, gi=gi, pb=pb: a.add(
                    vmac[:, gi * (MACRO + 6) + 6: gi * (MACRO + 6) + 6 + MACRO],
                    pb[:], inb[:, 16 + gi:16 + gi + 1]),
                    reads=[pb.k(), inb.k()], writes=[vmac.k(gi)])
                init = 0.0 if m == 0 else carry[:, gi:gi + 1]
                P.emit("dve", lambda v, gi=gi, init=init: v.tensor_tensor_scan(
                    ymac[:, gi * MACRO:(gi + 1) * MACRO],
                    dec[:, gi:gi + 1].broadcast_to([128, MACRO]),
                    vmac[:, gi * (MACRO + 6) + 6: gi * (MACRO + 6) + 6 + MACRO],
                    init, ALU.mult, ALU.add),
                    reads=[vmac.k(gi), dec.k()] + ([carry.k()] if m else []),
                    writes=[ymac.k(gi)])
            if c % CT == CT - 1:
                load_inw_next()
        P.emit("dve", lambda v: v.tensor_copy(
            vmac[:].rearrange("p (g c) -> p g c", g=16)[:, :, 0:6],
            tail[:].rearrange("p (g c) -> p g c", g=16)),
            reads=[tail.k()] + [vmac.k(gi) for gi in range(16)],
            writes=[vmac.k("halo")])
        P.emit("dve", lambda v: v.tensor_copy(
            carry[:],
            ymac[:].rearrange("p (g c) -> p g c", g=16)[:, :, MACRO - 1:MACRO]
            .rearrange("p g c -> p (g c)")),
            reads=[ymac.k(gi) for gi in range(16)], writes=[carry.k()])
        for gi in range(16):
            jm = gi // MT
            mbuf = mix_sl[(m * NMH + jm) % MSLOT]
            pc, pm_ = B[2 + gi % 2], B[4 + gi % 2]
            for k in range(7):
                P.emit("pe", lambda t, gi=gi, k=k, pc=pc, mbuf=mbuf: t.matmul(
                    pc[:], mbuf[:, k * 128:(k + 1) * 128],
                    vmac[:, gi * (MACRO + 6) + k: gi * (MACRO + 6) + k + MACRO],
                    start=(k == 0), stop=(k == 6)),
                    reads=[mbuf.k(), vmac.k(gi), vmac.k("halo")], writes=[pc.k()])
            for k in range(16):
                P.emit("pe", lambda t, k=k, mbuf=mbuf, pm_=pm_: t.matmul(
                    pm_[:], mbuf[:, (7 + k) * 128:(8 + k) * 128],
                    ymac[:, k * MACRO:(k + 1) * MACRO],
                    start=(k == 0), stop=(k == 15)),
                    reads=[mbuf.k(), ymac.k(k)], writes=[pm_.k()])
            P.emit("act", lambda a, gi=gi, pc=pc: a.add(
                vtmp[:], pc[:], bcomb[:, gi:gi + 1]),
                reads=[pc.k(), bcomb.k()], writes=[vtmp.k()])
            P.emit("dve", lambda v, pm_=pm_: v.tensor_add(
                vtmp[:], vtmp[:], pm_[:]),
                reads=[vtmp.k(), pm_.k()], writes=[vtmp.k()])
            P.emit("dve", lambda v, gi=gi: v.tensor_mul(
                hmac[:, gi * MACRO:(gi + 1) * MACRO], vtmp[:],
                gmac[:, gi * MACRO:(gi + 1) * MACRO]),
                reads=[vtmp.k(), gmac.k(gi)], writes=[hmac.k(gi)])
            if gi % MT == MT - 1:
                load_mix_next()
        P.emit("dve", lambda v: v.tensor_copy(
            tail[:].rearrange("p (g c) -> p g c", g=16),
            vmac[:].rearrange("p (g c) -> p g c", g=16)[:, :, MACRO:MACRO + 6]),
            reads=[vmac.k(gi) for gi in range(16)], writes=[tail.k()])
        if DEBUG:
            dst = P.stream("l0_dbg", "sp", serial=True)
            for nm_, buf_ in (("dbg_xnT", xnT), ("dbg_g", gmac), ("dbg_v", vmac),
                              ("dbg_y", ymac), ("dbg_h", hmac)):
                pass
            for nm_, buf_ in (("dbg_xnT", xnT), ("dbg_g", gmac), ("dbg_v", vmac),
                              ("dbg_y", ymac), ("dbg_h", hmac),
                              ("dbg_ln", dbgln), ("dbg_st", dbgst)):
                if nm_ not in io:
                    continue
                rd = [buf_.k(kk) for kk in range(16 if nm_ != "dbg_xnT" else 8)]
                if nm_ == "dbg_v":
                    rd.append(buf_.k("halo"))
                if nm_ in ("dbg_ln", "dbg_st"):
                    rd = [buf_.k(kk) for kk in range(4)]
                P.dma(dst, io[nm_][m], buf_[:], reads=rd)
        for tt in range(4):
            RT = res_t[tt % 2]
            for half in range(2):
                po = B[6 + half]
                for k in range(16):
                    P.emit("pe", lambda t, tt=tt, half=half, k=k, po=po: t.matmul(
                        po[:], hmac[:, k * MACRO + tt * 128: k * MACRO + (tt + 1) * 128],
                        outw[:, k * 1024 + half * 512: k * 1024 + (half + 1) * 512],
                        start=(k == 0), stop=(k == 15)),
                        reads=[hmac.k(k), outw.k()], writes=[po.k()])
                P.emit("dve", lambda v, half=half, po=po, RT=RT: v.tensor_add(
                    RT[:, half * 512:(half + 1) * 512], po[:],
                    outb[:, half * 512:(half + 1) * 512]),
                    reads=[po.k(), outb.k()], writes=[RT.k()])
            P.emit("dve", lambda v, tt=tt, RT=RT: v.tensor_add(
                RT[:], RT[:], XT[:, tt * D:(tt + 1) * D]),
                reads=[RT.k(), XT.k(tt)], writes=[RT.k()])
            t0 = m * MACRO + tt * 128
            P.dma(rst[tt % 2], io["res1"][t0:t0 + 128, :], RT[:], reads=[RT.k()])

    ln_phase(0)
    for m in range(NMAC):
        if m + 1 < NMAC:
            ln_phase(m + 1)
        compute_phase(m)
    return ["l0_res0", "l0_res1"]


MOE_CHUNKS = [(0, 6), (6, 6), (12, 6), (18, 4)]


def emit_moe(P, io):
    identf = P.identf
    B = P.banks
    gateT = sb(P, [128, 8 * 9], F32, "gateT")
    gateb = sb(P, [128, 9], F32, "gateb")
    cw = sb(P, [128, 32 * 9], F32, "cw")
    ld = P.stream("moe_const", "pool", serial=True)
    P.dma(ld, gateT[:], io["gateT_pm"], writes=[gateT.k()])
    P.dma(ld, gateb[:], io["gateb_b"], writes=[gateb.k()])

    xt = [sb(P, [128, D], F32, f"mx{b}") for b in range(2)]
    xn = sb(P, [128, D], F32, "mxn")
    sqt = sb(P, [128, D], F32, "msq")
    stats = sb(P, [128, 8], F32, "mst")
    xnTf = sb(P, [128, D], F32, "mxnTf")
    xnTb = sb(P, [128, 8 * MACRO], BF16, "mxnTb")
    sc9 = sb(P, [128, 32], F32, "msc9")
    xst = [P.stream(f"moe_x{b}", "sp") for b in range(2)]
    r2st = [P.stream(f"moe_r2i{b}", "sp") for b in range(2)]
    xlst = P.stream("moe_xlT", "sp")

    for m in range(NMAC):
        for tt in range(4):
            ti = m * 4 + tt
            XT = xt[ti % 2]
            t0 = ti * 128
            P.dma(xst[ti % 2], XT[:], io["res1"][t0:t0 + 128, :],
                  reads=[("hbm_res1", m)], writes=[XT.k()])
            layernorm_tile(P, xn[:], XT[:], sqt, stats, [XT.k()], xn.k())
            P.dma(r2st[ti % 2], io["res2"][t0:t0 + 128, :], XT[:],
                  reads=[XT.k()], writes=[("hbm_res2", m, "init", tt)])
            for d0 in range(0, 8, 4):
                pb = B[(d0 // 4) % 2]
                for r in range(4):
                    dk = d0 + r
                    P.emit("pe", lambda t, dk=dk, r=r, pb=pb: t.transpose(
                        pb[:, r * 128:(r + 1) * 128],
                        xn[:, dk * 128:(dk + 1) * 128], identf[:]),
                        reads=[xn.k(), identf.k()], writes=[pb.k()])
                P.emit("act", lambda a, d0=d0, pb=pb: a.copy(
                    xnTf[:, d0 * 128:(d0 + 4) * 128], pb[:, 0:512]),
                    reads=[pb.k()],
                    writes=[xnTf.k(d0), xnTf.k(d0 + 1), xnTf.k(d0 + 2),
                            xnTf.k(d0 + 3)])
                P.emit("dve", lambda v, tt=tt, d0=d0: v.tensor_copy(
                    xnTb[:].rearrange("p (dk t) -> p dk t", dk=8)
                    [:, d0:d0 + 4, tt * 128:(tt + 1) * 128],
                    xnTf[:, d0 * 128:(d0 + 4) * 128]
                    .rearrange("p (dk t) -> p dk t", dk=4)),
                    reads=[xnTf.k(d0), xnTf.k(d0 + 1), xnTf.k(d0 + 2),
                           xnTf.k(d0 + 3)],
                    writes=[xnTb.k(d0), xnTb.k(d0 + 1), xnTb.k(d0 + 2),
                            xnTb.k(d0 + 3)])
            pg = B[2]
            for dk in range(8):
                P.emit("pe", lambda t, dk=dk, pg=pg: t.matmul(
                    pg[:, 0:9], xnTf[:, dk * 128:(dk + 1) * 128],
                    gateT[:, dk * 9:(dk + 1) * 9],
                    start=(dk == 0), stop=(dk == 7)),
                    reads=[xnTf.k(dk), gateT.k()], writes=[pg.k()])
            P.emit("dve", lambda v, pg=pg: v.tensor_add(
                sc9[:, 0:9], pg[:, 0:9], gateb[:]),
                reads=[pg.k(), gateb.k()], writes=[sc9.k("s")])
            P.emit("act", lambda a: a.activation(sc9[:, 0:9], sc9[:, 0:9],
                                                 AF.Sigmoid),
                   reads=[sc9.k("s")], writes=[sc9.k("s")])
            P.emit("dve", lambda v: v.max(sc9[:, 16:24], sc9[:, 0:9]),
                   reads=[sc9.k("s")], writes=[sc9.k("m8")])
            P.emit("dve", lambda v: v.tensor_add(sc9[:, 24:25], sc9[:, 16:17],
                                                 sc9[:, 17:18]),
                   reads=[sc9.k("m8")], writes=[sc9.k("d")])
            P.emit("dve", lambda v: v.tensor_scalar_add(sc9[:, 24:25],
                                                        sc9[:, 24:25], 1e-20),
                   reads=[sc9.k("d")], writes=[sc9.k("d")])
            P.emit("dve", lambda v: v.reciprocal(sc9[:, 25:26], sc9[:, 24:25]),
                   reads=[sc9.k("d")], writes=[sc9.k("rd")])
            P.emit("dve", lambda v: v.tensor_mul(sc9[:, 26:27], sc9[:, 16:17],
                                                 sc9[:, 25:26]),
                   reads=[sc9.k("m8"), sc9.k("rd")], writes=[sc9.k("w1")])
            P.emit("dve", lambda v: v.tensor_mul(sc9[:, 27:28], sc9[:, 17:18],
                                                 sc9[:, 25:26]),
                   reads=[sc9.k("m8"), sc9.k("rd")], writes=[sc9.k("w2")])
            P.emit("dve", lambda v: v.tensor_scalar(
                sc9[:, 0:9] if False else cw[:, ti * 9:(ti + 1) * 9],
                sc9[:, 0:9], sc9[:, 16:17], sc9[:, 26:27],
                ALU.is_equal, ALU.mult),
                reads=[sc9.k("s"), sc9.k("m8"), sc9.k("w1")],
                writes=[cw.k(ti)])
            P.emit("dve", lambda v: v.tensor_scalar(
                sc9[:, 0:9], sc9[:, 0:9], sc9[:, 17:18], sc9[:, 27:28],
                ALU.is_equal, ALU.mult),
                reads=[sc9.k("s"), sc9.k("m8"), sc9.k("w2")],
                writes=[sc9.k("eq2")])
            P.emit("dve", lambda v, ti=ti: v.tensor_add(
                cw[:, ti * 9:(ti + 1) * 9], cw[:, ti * 9:(ti + 1) * 9],
                sc9[:, 0:9]),
                reads=[cw.k(ti), sc9.k("eq2")], writes=[cw.k(ti)])
        P.dma(xlst, io["xlnT"][m], xnTb[:],
              reads=[xnTb.k(dk) for dk in range(8)],
              writes=[("hbm_xlnT", m)])

    w1r = [sb(P, [128, 6 * 8 * 128], BF16, f"w1r{b}") for b in range(2)]
    w3r = [sb(P, [128, 6 * 8 * 128], BF16, f"w3r{b}") for b in range(2)]
    w2r = [sb(P, [128, 6 * 1024], BF16, f"w2r{b}") for b in range(2)]
    wst = [[P.stream(f"moe_w{w}{b}", "pool") for b in range(2)]
           for w in range(3)]
    xbuf = [sb(P, [128, 8 * MACRO], BF16, f"moexb{b}") for b in range(2)]
    xbst = [P.stream(f"moe_xb{b}", "sp") for b in range(2)]
    hT = [sb(P, [128, 6 * MACRO], BF16, f"moeh{b}") for b in range(2)]
    delta = [sb(P, [128, 4 * D], F32, f"moed{b}") for b in range(2)]
    dst_ = [P.stream(f"moe_acc{b}", "pool") for b in range(2)]
    silu_ = [sb(P, [128, MACRO], F32, f"moesilu{b}") for b in range(2)]

    chunk_seq = [0]
    schedule = [(e, ci) for e in range(9) for ci in range(4)]

    def load_chunk_next():
        seq = chunk_seq[0]
        if seq >= len(schedule):
            return
        chunk_seq[0] += 1
        e, ci = schedule[seq]
        c0, cn = MOE_CHUNKS[ci]
        sl = seq % 2
        P.dma(wst[0][sl], w1r[sl][:, 0:cn * 8 * 128],
              io["moe_w1"][e][:, c0 * 8 * 128:(c0 + cn) * 8 * 128],
              writes=[w1r[sl].k()])
        P.dma(wst[1][sl], w3r[sl][:, 0:cn * 8 * 128],
              io["moe_w3"][e][:, c0 * 8 * 128:(c0 + cn) * 8 * 128],
              writes=[w3r[sl].k()])
        P.dma(wst[2][sl], w2r[sl][:, 0:cn * 1024],
              io["moe_w2"][e][:, c0 * 1024:(c0 + cn) * 1024],
              writes=[w2r[sl].k()])

    load_chunk_next()
    load_chunk_next()

    iters = [(seq, e, ci, m) for seq, (e, ci) in enumerate(schedule)
             for m in range(NMAC)]

    def ab_h(i):
        seq, e, ci, m = iters[i]
        c0, cn = MOE_CHUNKS[ci]
        sl = seq % 2
        W1, W3 = w1r[sl], w3r[sl]
        xb = xbuf[i % 2]
        P.dma(xbst[i % 2], xb[:], io["xlnT"][m],
              reads=[("hbm_xlnT", m)], writes=[xb.k()])
        HT = hT[i % 2]
        for h in range(cn):
            pa, pb_ = B[h % 2], B[2 + h % 2]
            for k in range(8):
                P.emit("pe", lambda t, h=h, k=k, pa=pa, W1=W1, xb=xb: t.matmul(
                    pa[:], W1[:, (h * 8 + k) * 128:(h * 8 + k + 1) * 128],
                    xb[:, k * MACRO:(k + 1) * MACRO],
                    start=(k == 0), stop=(k == 7)),
                    reads=[W1.k(), xb.k()], writes=[pa.k()])
            for k in range(8):
                P.emit("pe", lambda t, h=h, k=k, pb_=pb_, W3=W3, xb=xb: t.matmul(
                    pb_[:], W3[:, (h * 8 + k) * 128:(h * 8 + k + 1) * 128],
                    xb[:, k * MACRO:(k + 1) * MACRO],
                    start=(k == 0), stop=(k == 7)),
                    reads=[W3.k(), xb.k()], writes=[pb_.k()])
            silu_t = silu_[h % 2]
            P.emit("act", lambda a, pa=pa, silu_t=silu_t: a.activation(
                silu_t[:], pa[:], AF.Silu),
                reads=[pa.k()], writes=[silu_t.k()])
            P.emit("dve", lambda v, h=h, pb_=pb_, HT=HT, silu_t=silu_t:
                   v.tensor_mul(
                HT[:, h * MACRO:(h + 1) * MACRO], silu_t[:], pb_[:]),
                reads=[silu_t.k(), pb_.k()], writes=[HT.k(h)])

    def w2d(i):
        seq, e, ci, m = iters[i]
        c0, cn = MOE_CHUNKS[ci]
        sl = seq % 2
        W2 = w2r[sl]
        HT = hT[i % 2]
        DL = delta[i % 2]
        for tt in range(4):
            for dh in range(2):
                po = B[4 + (tt * 2 + dh) % 2]
                for k in range(cn):
                    P.emit("pe", lambda t, tt=tt, dh=dh, k=k, po=po, W2=W2,
                           HT=HT: t.matmul(
                        po[:], HT[:, k * MACRO + tt * 128: k * MACRO + (tt + 1) * 128],
                        W2[:, k * 1024 + dh * 512:k * 1024 + (dh + 1) * 512],
                        start=(k == 0), stop=(k == cn - 1)),
                        reads=[HT.k(k), W2.k()], writes=[po.k()])
                col = (m * 4 + tt) * 9 + e
                if (tt * 2 + dh) % 2 == 0:
                    P.emit("dve", lambda v, tt=tt, dh=dh, po=po, DL=DL,
                           col=col: v.tensor_scalar_mul(
                        DL[:, tt * D + dh * 512: tt * D + (dh + 1) * 512],
                        po[:], cw[:, col:col + 1]),
                        reads=[po.k(), cw.k(m * 4 + tt)], writes=[DL.k()])
                else:
                    P.emit("act", lambda a, tt=tt, dh=dh, po=po, DL=DL,
                           col=col: a.activation(
                        DL[:, tt * D + dh * 512: tt * D + (dh + 1) * 512],
                        po[:], AF.Identity, scale=cw[:, col:col + 1]),
                        reads=[po.k(), cw.k(m * 4 + tt)], writes=[DL.k()])
        t0 = m * MACRO
        P.dma(dst_[i % 2], io["res2"][t0:t0 + MACRO, :]
              .rearrange("(tt p) f -> p tt f", p=128),
              DL[:].rearrange("p (tt f) -> p tt f", tt=4),
              reads=[DL.k()] + [("hbm_res2", m, "init", t) for t in range(4)],
              writes=[("hbm_res2", m)], accum_op=ALU.add)
        if m == NMAC - 1:
            load_chunk_next()

    ab_h(0)
    for i in range(len(iters)):
        if i + 1 < len(iters):
            ab_h(i + 1)
        w2d(i)
    return ["moe_acc0", "moe_acc1", "moe_r2i0", "moe_r2i1"]


def emit_l1a_pre(P, io):
    identf, B = P.identf, P.banks
    wq = sb(P, [128, 8 * 1024], BF16, "wq")
    wk = sb(P, [128, 8 * 1024], BF16, "wk")
    wv = sb(P, [128, 8 * 1024], BF16, "wv")
    bq = sb(P, [128, D], F32, "bqb")
    bk = sb(P, [128, D], F32, "bkb")
    bv = sb(P, [128, D], F32, "bvb")
    cosb = sb(P, [128, 32 * 32], F32, "cosb")
    sinb = sb(P, [128, 32 * 32], F32, "sinb")
    cosq = sb(P, [128, 32 * 32], F32, "cosq")
    sinq = sb(P, [128, 32 * 32], F32, "sinq")
    scl = sb(P, [128, 1], F32, "scl")
    ld = P.stream("l1p_const", "pool", serial=True)
    P.dma(ld, wq[:], io["wqT_pm"], writes=[wq.k()])
    P.dma(ld, wk[:], io["wkT_pm"], writes=[wk.k()])
    P.dma(ld, wv[:], io["wvT_pm"], writes=[wv.k()])
    P.dma(ld, bq[:], io["bq_b"], writes=[bq.k()])
    P.dma(ld, bk[:], io["bk_b"], writes=[bk.k()])
    P.dma(ld, bv[:], io["bv_b"], writes=[bv.k()])
    P.dma(ld, scl[:], io["l1scale"], writes=[scl.k()])
    P.dma(ld, cosb[:].rearrange("p (tt c) -> p tt c", tt=32),
          io["l1a_cos"].rearrange("(tt p) c -> p tt c", p=128), writes=[cosb.k()])
    P.dma(ld, sinb[:].rearrange("p (tt c) -> p tt c", tt=32),
          io["l1a_sin"].rearrange("(tt p) c -> p tt c", p=128), writes=[sinb.k()])
    P.emit("dve", lambda v: v.tensor_scalar_mul(cosq[:], cosb[:], scl[:, 0:1]),
           reads=[cosb.k(), scl.k()], writes=[cosq.k()])
    P.emit("dve", lambda v: v.tensor_scalar_mul(sinq[:], sinb[:], scl[:, 0:1]),
           reads=[sinb.k(), scl.k()], writes=[sinq.k()])

    xt = [sb(P, [128, D], F32, f"ax{b}") for b in range(2)]
    xn_ = [sb(P, [128, D], F32, f"axn{b}") for b in range(2)]
    sqt_ = [sb(P, [128, D], F32, f"asq{b}") for b in range(2)]
    stats_ = [sb(P, [128, 8], F32, f"ast{b}") for b in range(2)]
    xnT_ = [sb(P, [128, 8 * 128], BF16, f"axnT{b}") for b in range(2)]
    qf_ = [sb(P, [128, D], F32, f"aqf{b}") for b in range(2)]
    qr_ = [sb(P, [128, D], F32, f"aqr{b}") for b in range(2)]
    te_ = [sb(P, [128, D], F32, f"ate{b}") for b in range(2)]
    qTm = sb(P, [128, 8 * MACRO], BF16, "aqTm")
    kTm = sb(P, [128, 8 * MACRO], BF16, "akTm")
    sq2 = sb(P, [128, D], F32, "asq2")
    nrm = sb(P, [128, 16], F32, "anrm")
    qmx = sb(P, [128, 16], F32, "aqmx")
    kmx = sb(P, [128, 16], F32, "akmx")
    P.emit("dve", lambda v: v.memset(qmx[:], 0.0), writes=[qmx.k()])
    P.emit("dve", lambda v: v.memset(kmx[:], 0.0), writes=[kmx.k()])
    xst = [P.stream(f"l1p_x{b}", "sp") for b in range(2)]
    vst = [P.stream(f"l1p_v{b}", "pool") for b in range(2)]
    qst = P.stream("l1p_q", "sp", serial=True)
    kst = P.stream("l1p_k", "sp", serial=True)

    def rope(dst_bf, src_f32, cos_t, sin_t, tt, keys_in, key_out, te):
        cs = cos_t[:, (tt % 32) * 32:(tt % 32) * 32 + 32]
        sn = sin_t[:, (tt % 32) * 32:(tt % 32) * 32 + 32]
        cs3 = cs.rearrange("p (o c) -> p o c", o=1).broadcast_to([128, 16, 32])
        sn3 = sn.rearrange("p (o c) -> p o c", o=1).broadcast_to([128, 16, 32])
        sv = src_f32.rearrange("p (h c) -> p h c", h=16)
        se = sv[:, :, 0:64:2]
        so = sv[:, :, 1:64:2]
        dv = dst_bf.rearrange("p (h c) -> p h c", h=16)
        de = dv[:, :, 0:64:2]
        do = dv[:, :, 1:64:2]
        t3 = te[:].rearrange("p (h c) -> p h c", h=16)
        t_a = t3[:, :, 0:32]
        t_b = t3[:, :, 32:64]
        P.emit("dve", lambda v: v.tensor_mul(t_a, se, cs3),
               reads=keys_in + [cos_t.k()], writes=[te.k("a")])
        P.emit("dve", lambda v: v.tensor_mul(t_b, so, sn3),
               reads=keys_in + [sin_t.k()], writes=[te.k("b")])
        P.emit("dve", lambda v: v.tensor_sub(de, t_a, t_b),
               reads=[te.k("a"), te.k("b")], writes=[key_out])
        P.emit("dve", lambda v: v.tensor_mul(t_a, se, sn3),
               reads=keys_in + [sin_t.k()], writes=[te.k("a")])
        P.emit("dve", lambda v: v.tensor_mul(t_b, so, cs3),
               reads=keys_in + [cos_t.k()], writes=[te.k("b")])
        P.emit("dve", lambda v: v.tensor_add(do, t_a, t_b),
               reads=[te.k("a"), te.k("b")], writes=[key_out])

    for m in range(NMAC):
        for tt in range(4):
            ti = m * 4 + tt
            bb = ti % 2
            XT = xt[bb]
            xn, sqt, stats = xn_[bb], sqt_[bb], stats_[bb]
            xnT, qf, qr, te = xnT_[bb], qf_[bb], qr_[bb], te_[bb]
            t0 = ti * 128
            P.dma(xst[ti % 2], XT[:], io["res2"][t0:t0 + 128, :],
                  writes=[XT.k()])
            layernorm_tile(P, xn[:], XT[:], sqt, stats, [XT.k()], xn.k())
            for d0 in range(0, 8, 4):
                pb = B[(d0 // 4) % 2]
                for r in range(4):
                    dk = d0 + r
                    P.emit("pe", lambda t, dk=dk, r=r, pb=pb, xn=xn: t.transpose(
                        pb[:, r * 128:(r + 1) * 128],
                        xn[:, dk * 128:(dk + 1) * 128], identf[:]),
                        reads=[xn.k(), identf.k()], writes=[pb.k()])
                P.emit("act", lambda a, d0=d0, pb=pb, xnT=xnT: a.copy(
                    xnT[:, d0 * 128:(d0 + 4) * 128], pb[:, 0:512]),
                    reads=[pb.k()],
                    writes=[xnT.k(d0), xnT.k(d0 + 1), xnT.k(d0 + 2),
                            xnT.k(d0 + 3)])
            for proj, (W, bias, dstT) in enumerate(
                    ((wq, bq, qTm), (wk, bk, kTm), (wv, bv, None))):
                for half in range(2):
                    pp = B[2 + half]
                    for dk in range(8):
                        P.emit("pe", lambda t, dk=dk, half=half, pp=pp, W=W,
                               xnT=xnT:
                               t.matmul(pp[:], xnT[:, dk * 128:(dk + 1) * 128],
                                        W[:, dk * 1024 + half * 512:
                                          dk * 1024 + (half + 1) * 512],
                                        start=(dk == 0), stop=(dk == 7)),
                               reads=[xnT.k(dk), W.k()], writes=[pp.k()])
                    P.emit("dve", lambda v, half=half, pp=pp, bias=bias, qf=qf:
                           v.tensor_add(
                        qf[:, half * 512:(half + 1) * 512], pp[:],
                        bias[:, half * 512:(half + 1) * 512]),
                        reads=[pp.k(), bias.k()], writes=[qf.k()])
                if proj < 2:
                    if proj == 0:
                        rope(qr[:], qf[:], cosq, sinq, ti, [qf.k()], qr.k(), te)
                        mxacc = qmx
                    else:
                        rope(qr[:], qf[:], cosb, sinb, ti, [qf.k()], qr.k(), te)
                        mxacc = kmx
                    P.emit("dve", lambda v, qr=qr: v.tensor_mul(sq2[:], qr[:], qr[:]),
                           reads=[qr.k()], writes=[sq2.k()])
                    P.emit("dve", lambda v: v.tensor_reduce(
                        nrm[:, 0:16],
                        sq2[:].rearrange("p (g c) -> p g c", g=16),
                        AX.X, ALU.add),
                        reads=[sq2.k()], writes=[nrm.k(0)])
                    P.emit("dve", lambda v, mxacc=mxacc: v.tensor_max(
                        mxacc[:], mxacc[:], nrm[:, 0:16]),
                        reads=[mxacc.k(), nrm.k(0)], writes=[mxacc.k()])
                else:
                    P.dma(vst[ti % 2], io["vT1"][t0:t0 + 128, :], qf[:],
                          reads=[qf.k()])
                    continue
                for b0 in range(0, 8, 4):
                    pb = B[4 + (b0 // 4) % 2]
                    for r in range(4):
                        blk = b0 + r
                        P.emit("pe", lambda t, blk=blk, r=r, pb=pb, qr=qr:
                               t.transpose(
                            pb[:, r * 128:(r + 1) * 128],
                            qr[:, blk * 128:(blk + 1) * 128], identf[:]),
                            reads=[qr.k(), identf.k()], writes=[pb.k()])
                    P.emit("act", lambda a, b0=b0, tt=tt, pb=pb, dstT=dstT: a.copy(
                        dstT[:].rearrange("p (blk t) -> p blk t", blk=8)
                        [:, b0:b0 + 4, (tt % 4) * 128:(tt % 4 + 1) * 128],
                        pb[:].rearrange("p (blk t) -> p blk t", blk=4)),
                        reads=[pb.k()],
                        writes=[dstT.k(b0), dstT.k(b0 + 1), dstT.k(b0 + 2),
                                dstT.k(b0 + 3)])
        for blk in range(8):
            P.dma(qst, io["qT"][blk][:, m * MACRO:(m + 1) * MACRO],
                  qTm[:, blk * MACRO:(blk + 1) * MACRO], reads=[qTm.k(blk)])
            P.dma(kst, io["kT"][blk][:, m * MACRO:(m + 1) * MACRO],
                  kTm[:, blk * MACRO:(blk + 1) * MACRO], reads=[kTm.k(blk)])
    B2 = P.banks
    for src, col in ((qmx, 0), (kmx, 1)):
        pb = B2[col]
        P.emit("pe", lambda t, src=src, pb=pb: t.transpose(
            pb[0:16, 0:128], src[:], identf[:]),
            reads=[src.k(), identf.k()], writes=[pb.k()])
        P.emit("dve", lambda v, col=col, pb=pb: v.reduce_max(
            nrm[0:16, 2 + col:3 + col], pb[0:16, 0:128], axis=AX.X),
            reads=[pb.k()], writes=[nrm.k(("gm", col))])
    P.emit("act", lambda a: a.activation(nrm[0:16, 4:6], nrm[0:16, 2:4], AF.Sqrt),
           reads=[nrm.k(("gm", 0)), nrm.k(("gm", 1))], writes=[nrm.k("sq")])
    P.emit("dve", lambda v: v.tensor_mul(nrm[0:16, 6:7], nrm[0:16, 4:5],
                                         nrm[0:16, 5:6]),
           reads=[nrm.k("sq")], writes=[nrm.k("M")])
    P.emit("dve", lambda v: v.tensor_scalar(nrm[0:16, 7:8], nrm[0:16, 6:7],
                                            -1.0, 40.0, ALU.mult, ALU.add),
           reads=[nrm.k("M")], writes=[nrm.k("bias")])
    mst = P.stream("l1p_M", "sp", serial=True)
    P.dma(mst, io["attM"][0:16, 0:1], nrm[0:16, 7:8], reads=[nrm.k("bias")])
    return []


DBGH = [False]


def emit_l1a_att(P, io):
    B = P.banks
    NH = 16
    identf = P.identf
    vaug = sb(P, [128, 32 * 1040], BF16, "avaug")
    vtmp = sb(P, [128, 1024], BF16, "avtmp")
    mskl = sb(P, [128, 128], F32, "amaskl")
    biasb = sb(P, [128, NH], F32, "abias")
    ld = P.stream("l1t_const", "sp", serial=True)
    P.dma(ld, mskl[:], io["mask_low"], writes=[mskl.k()])
    for h in range(NH):
        P.dma(ld, biasb[:, h:h + 1],
              io["attM"][h:h + 1, 0:1].broadcast_to([128, 1]),
              writes=[biasb.k()])
    P.emit("dve", lambda v: v.memset(vaug[:], 1.0), writes=[vaug.k("ones")])
    vst = P.stream("l1t_v", "sp", serial=True)
    for jt in range(32):
        P.dma(vst, vtmp[:], io["vT1"][jt * 128:(jt + 1) * 128, :],
              writes=[vtmp.k()])
        P.emit("dve", lambda v, jt=jt: v.tensor_copy(
            vaug[:, jt * 1040:(jt + 1) * 1040]
            .rearrange("p (h c) -> p h c", h=16)[:, :, 0:64],
            vtmp[:].rearrange("p (h c) -> p h c", h=16)),
            reads=[vtmp.k(), vaug.k("ones")], writes=[vaug.k(jt)])
    qh = [sb(P, [64, T], BF16, f"aqh{b}") for b in range(2)]
    kh = [sb(P, [64, T], BF16, f"akh{b}") for b in range(2)]
    qhst = [P.stream(f"l1t_q{b}", "sp") for b in range(2)]
    khst = [P.stream(f"l1t_k{b}", "sp") for b in range(2)]
    pst = [sb(P, [128, T], BF16, f"apst{b}") for b in range(3)]
    ot = [sb(P, [128, 64], F32, f"aot{b}") for b in range(2)]
    rcp = sb(P, [128, 4], F32, "arcp")
    oTf = sb(P, [128, 8 * T], BF16, "aoTf")
    osst = P.stream("l1t_o", "sp", serial=True)

    def load_head(h):
        b = h % 2
        P.dma(qhst[b], qh[b][:], io["qT"][h // 2][64 * (h % 2):64 * (h % 2) + 64, :],
              writes=[qh[b].k()])
        P.dma(khst[b], kh[b][:], io["kT"][h // 2][64 * (h % 2):64 * (h % 2) + 64, :],
              writes=[kh[b].k()])

    load_head(0)
    load_head(1)

    strips = [(h, qt) for h in range(NH) for qt in range(32)]

    def emit_ST(i):
        h, qt = strips[i]
        b = h % 2
        QH, KH = qh[b], kh[b]
        PS = pst[i % 3]
        jn = qt + 1
        for b0 in range(0, jn, 4):
            bn = min(4, jn - b0)
            pb = B[(b0 // 4) % 4]
            for r in range(bn):
                jt = b0 + r
                P.emit("pe", lambda t, jt=jt, r=r, qt=qt, pb=pb, QH=QH, KH=KH:
                       t.matmul(pb[:, r * 128:(r + 1) * 128],
                                KH[:, jt * 128:(jt + 1) * 128],
                                QH[:, qt * 128:(qt + 1) * 128],
                                start=True, stop=True),
                       reads=[QH.k(), KH.k()], writes=[pb.k()])
                if jt == qt:
                    P.emit("dve", lambda v, r=r, pb=pb: v.tensor_add(
                        pb[:, r * 128:(r + 1) * 128],
                        pb[:, r * 128:(r + 1) * 128], mskl[:]),
                        reads=[pb.k(), mskl.k()], writes=[pb.k()])
            P.emit("act", lambda a, b0=b0, bn=bn, h=h, pb=pb, PS=PS:
                   a.activation(PS[:, b0 * 128:(b0 + bn) * 128],
                                pb[:, 0:bn * 128], AF.Exp,
                                bias=biasb[:, h:h + 1]),
                   reads=[pb.k(), biasb.k()], writes=[PS.k(b0 // 4)])

    def emit_AV(i):
        h, qt = strips[i]
        PS = pst[i % 3]
        jn = qt + 1
        pO = B[4 + i % 2]
        for jt in range(jn):
            P.emit("pe", lambda t, jt=jt, h=h, pO=pO, PS=PS: t.matmul(
                pO[:, 0:65], PS[:, jt * 128:(jt + 1) * 128],
                vaug[:, jt * 1040 + h * 65: jt * 1040 + (h + 1) * 65],
                start=(jt == 0), stop=(jt == jn - 1)),
                reads=[PS.k(jt // 4), vaug.k(jt), vaug.k("ones")],
                writes=[pO.k()])
        OT = ot[i % 2]
        P.emit("dve", lambda v, pO=pO: v.reciprocal(rcp[:, 0:1], pO[:, 64:65]),
               reads=[pO.k()], writes=[rcp.k()])
        P.emit("dve", lambda v, pO=pO, OT=OT: v.tensor_scalar_mul(
            OT[:], pO[:, 0:64], rcp[:, 0:1]),
            reads=[pO.k(), rcp.k()], writes=[OT.k()])
        pt_ = B[6 + i % 2]
        P.emit("pe", lambda t, pt_=pt_, OT=OT: t.transpose(
            pt_[0:64, 0:128], OT[:], identf[:]),
            reads=[OT.k(), identf.k()], writes=[pt_.k()])
        po_part = 64 * (h % 2)
        P.emit("act", lambda a, h=h, qt=qt, pt_=pt_, po_part=po_part: a.copy(
            oTf[po_part:po_part + 64, (h // 2) * T + qt * 128:
                (h // 2) * T + (qt + 1) * 128],
            pt_[0:64, 0:128]),
            reads=[pt_.k()], writes=[oTf.k((h, qt))])
        if qt == 31:
            if h + 2 < NH:
                load_head(h + 2)
            if h % 2 == 1:
                blk = h // 2
                P.dma(osst, io["oT2"][blk], oTf[:, blk * T:(blk + 1) * T],
                      reads=[oTf.k((h, g2)) for g2 in range(32)]
                      + [oTf.k((h - 1, g2)) for g2 in range(32)])

    emit_ST(0)
    emit_ST(1)
    for i in range(len(strips)):
        if i + 2 < len(strips):
            emit_ST(i + 2)
        emit_AV(i)
    return ["l1t_o"]


def emit_l1a_wo(P, io):
    B = P.banks
    wo = sb(P, [128, 8 * 1024], BF16, "wo")
    bo = sb(P, [128, D], F32, "bob")
    ld = P.stream("l1w_const", "pool", serial=True)
    P.dma(ld, wo[:], io["woT_pm"], writes=[wo.k()])
    P.dma(ld, bo[:], io["bo_b"], writes=[bo.k()])
    oT = [sb(P, [128, 8 * 128], BF16, f"woTt{b}") for b in range(2)]
    xt = [sb(P, [128, D], F32, f"wxt{b}") for b in range(2)]
    rt = [sb(P, [128, D], F32, f"wrt{b}") for b in range(2)]
    ost = [P.stream(f"l1w_o{b}", "sp") for b in range(2)]
    xst = [P.stream(f"l1w_x{b}", "sp") for b in range(2)]
    rst = [P.stream(f"l1w_r{b}", "sp") for b in range(2)]
    for ti in range(32):
        t0 = ti * 128
        b = ti % 2
        OT, XT, RT = oT[b], xt[b], rt[b]
        P.dma(ost[b], OT[:].rearrange("p (blk t) -> p blk t", blk=8),
              io["oT2"][:, :, t0:t0 + 128].rearrange("blk p t -> p blk t"),
              writes=[OT.k()])
        P.dma(xst[b], XT[:], io["res2"][t0:t0 + 128, :], writes=[XT.k()])
        for half in range(2):
            pp = B[2 + half]
            for blk in range(8):
                P.emit("pe", lambda t, blk=blk, half=half, pp=pp, OT=OT: t.matmul(
                    pp[:], OT[:, blk * 128:(blk + 1) * 128],
                    wo[:, blk * 1024 + half * 512: blk * 1024 + (half + 1) * 512],
                    start=(blk == 0), stop=(blk == 7)),
                    reads=[OT.k(), wo.k()], writes=[pp.k()])
            P.emit("dve", lambda v, half=half, pp=pp, RT=RT: v.tensor_add(
                RT[:, half * 512:(half + 1) * 512], pp[:],
                bo[:, half * 512:(half + 1) * 512]),
                reads=[pp.k(), bo.k()], writes=[RT.k()])
        P.emit("dve", lambda v, RT=RT, XT=XT: v.tensor_add(RT[:], RT[:], XT[:]),
               reads=[RT.k(), XT.k()], writes=[RT.k()])
        P.dma(rst[b], io["res3"][t0:t0 + 128, :], RT[:], reads=[RT.k()])
    return ["l1w_r0", "l1w_r1"]


def emit_l1f_pre(P, io):
    xt = [sb(P, [128, D], F32, f"fpx{b}") for b in range(2)]
    zn = [sb(P, [128, D], F32, f"fpz{b}") for b in range(2)]
    sqt = sb(P, [128, D], F32, "fpsq")
    stats = sb(P, [128, 8], F32, "fpst")
    xst = [P.stream(f"f_x{b}", "sp") for b in range(2)]
    zst = [P.stream(f"f_z{b}", "sp") for b in range(2)]
    for ti in range(32):
        b = ti % 2
        XT, ZN = xt[b], zn[b]
        t0 = ti * 128
        P.dma(xst[b], XT[:], io["res3"][t0:t0 + 128, :], writes=[XT.k()])
        layernorm_tile(P, ZN[:], XT[:], sqt, stats, [XT.k()], ZN.k())
        P.dma(zst[b], io["za"][t0:t0 + 128, :], ZN[:], reads=[ZN.k()])
    return []


def shift_load(P, io, zname, dst, ti, streams, key):
    t0 = ti * 128
    P.dma(streams[0], dst[:, 0:512], io[zname][t0:t0 + 128, 0:512],
          writes=[key])
    if ti == 0:
        P.emit("dve", lambda v: v.memset(dst[0:1, 512:1024], 0.0),
               writes=[key])
        P.dma(streams[1], dst[1:128, 512:1024], io[zname][0:127, 512:1024],
              writes=[key])
    else:
        P.dma(streams[1], dst[:, 512:1024], io[zname][t0 - 1:t0 + 127, 512:1024],
              writes=[key])


def emit_l1f_qkv(P, io, l, zname):
    identf, B = P.identf, P.banks
    wqkv = sb(P, [128, 8 * 1536], BF16, "fw")
    qsb = sb(P, [128, 512], F32, "fqs")
    ksb = sb(P, [128, 512], F32, "fks")
    tcq = sb(P, [128, 32 * 64], F32, "ftcq")
    tsq = sb(P, [128, 32 * 64], F32, "ftsq")
    tck = sb(P, [128, 32 * 64], F32, "ftck")
    tsk = sb(P, [128, 32 * 64], F32, "ftsk")
    ld = P.stream("f_const", "pool", serial=True)
    P.dma(ld, wqkv[:], io["f_wqkv"][l], writes=[wqkv.k()])
    P.dma(ld, qsb[:], io["f_qs_b"][l], writes=[qsb.k()])
    P.dma(ld, ksb[:], io["f_ks_b"][l], writes=[ksb.k()])
    for nm_, t_ in (("xpos_cq", tcq), ("xpos_sq", tsq),
                    ("xpos_ck", tck), ("xpos_sk", tsk)):
        P.dma(ld, t_[:].rearrange("p (tt c) -> p tt c", tt=32),
              io[nm_].rearrange("(tt p) c -> p tt c", p=128), writes=[t_.k()])

    ht = [sb(P, [128, D], F32, f"fht{b}") for b in range(2)]
    hT_ = [sb(P, [128, 8 * 128], BF16, f"fhT{b}") for b in range(2)]
    qf_ = [sb(P, [128, 512], F32, f"fqf{b}") for b in range(2)]
    sq2_ = [sb(P, [128, 512], F32, f"fsq2{b}") for b in range(2)]
    red_ = [sb(P, [128, 24], F32, f"fred{b}") for b in range(2)]
    qx_ = [sb(P, [128, 512], F32, f"fqx{b}") for b in range(2)]
    te_ = [sb(P, [128, 512], F32, f"fte{b}") for b in range(2)]
    qTm = sb(P, [128, 4 * MACRO], BF16, "fqTm")
    kTm = sb(P, [128, 4 * MACRO], BF16, "fkTm")
    hst = [[P.stream(f"f_h{a}{b}", "sp") for b in range(2)] for a in range(2)]
    vst = [P.stream(f"f_v{b}", "pool") for b in range(2)]
    qst = P.stream("f_qT", "sp", serial=True)
    kst = P.stream("f_kT", "sp", serial=True)

    def l2n_xpos(src_key, ct, st_, sb_, dstT, tt, qf, sq2, red, qx, te):
        P.emit("dve", lambda v: v.tensor_mul(sq2[:], qf[:], qf[:]),
               reads=[src_key], writes=[sq2.k()])
        P.emit("dve", lambda v: v.tensor_reduce(
            red[:, 0:8], sq2[:].rearrange("p (g c) -> p g c", g=8),
            AX.X, ALU.add),
            reads=[sq2.k()], writes=[red.k(0)])
        P.emit("act", lambda a: a.activation(red[:, 8:16], red[:, 0:8], AF.Sqrt),
               reads=[red.k(0)], writes=[red.k("n")])
        P.emit("dve", lambda v: v.tensor_scalar_max(red[:, 8:16], red[:, 8:16],
                                                    1e-12),
               reads=[red.k("n")], writes=[red.k("n")])
        P.emit("dve", lambda v: v.reciprocal(red[:, 16:24], red[:, 8:16]),
               reads=[red.k("n")], writes=[red.k("rn")])
        rb = red[:, 16:24].rearrange("p (h o) -> p h o", h=8).broadcast_to([128, 8, 64])
        q3 = qf[:].rearrange("p (h c) -> p h c", h=8)
        P.emit("dve", lambda v: v.tensor_mul(q3, q3, rb),
               reads=[src_key, red.k("rn")], writes=[src_key])
        P.emit("dve", lambda v, sb_=sb_: v.tensor_mul(qf[:], qf[:], sb_[:]),
               reads=[src_key, sb_.k()], writes=[src_key])
        cs = ct[:, (tt % 32) * 64:(tt % 32) * 64 + 64]
        sn = st_[:, (tt % 32) * 64:(tt % 32) * 64 + 64]
        cs3 = cs.rearrange("p (o c) -> p o c", o=1).broadcast_to([128, 8, 64])
        qx3 = qx[:].rearrange("p (h c) -> p h c", h=8)
        P.emit("dve", lambda v: v.tensor_mul(qx3, q3, cs3),
               reads=[src_key, ct.k()], writes=[qx.k()])
        sn1 = sn[:, 0:32].rearrange("p (o c) -> p o c", o=1).broadcast_to([128, 8, 32])
        sn2 = sn[:, 32:64].rearrange("p (o c) -> p o c", o=1).broadcast_to([128, 8, 32])
        te3 = te[:].rearrange("p (h c) -> p h c", h=8)
        P.emit("dve", lambda v: v.tensor_mul(te3[:, :, 0:32], q3[:, :, 32:64], sn1),
               reads=[src_key, st_.k()], writes=[te.k()])
        P.emit("dve", lambda v: v.tensor_mul(te3[:, :, 32:64], q3[:, :, 0:32], sn2),
               reads=[src_key, st_.k()], writes=[te.k()])
        P.emit("dve", lambda v: v.tensor_add(qx[:], qx[:], te[:]),
               reads=[qx.k(), te.k()], writes=[qx.k()])
        pb = B[4 + tt % 2]
        for blk in range(4):
            P.emit("pe", lambda t, blk=blk, pb=pb: t.transpose(
                pb[:, blk * 128:(blk + 1) * 128],
                qx[:, blk * 128:(blk + 1) * 128], identf[:]),
                reads=[qx.k(), identf.k()], writes=[pb.k()])
        P.emit("act", lambda a, tt=tt, pb=pb, dstT=dstT: a.copy(
            dstT[:].rearrange("p (blk t) -> p blk t", blk=4)
            [:, :, (tt % 4) * 128:(tt % 4 + 1) * 128],
            pb[:].rearrange("p (blk t) -> p blk t", blk=4)),
            reads=[pb.k()],
            writes=[dstT.k(0), dstT.k(1), dstT.k(2), dstT.k(3)])

    for m in range(NMAC):
        for tt in range(4):
            ti = m * 4 + tt
            bb = ti % 2
            HT = ht[bb]
            hT, qf, sq2, red = hT_[bb], qf_[bb], sq2_[bb], red_[bb]
            qx, te = qx_[bb], te_[bb]
            shift_load(P, io, zname, HT, ti, hst[ti % 2], HT.k())
            for d0 in range(0, 8, 4):
                pb = B[(d0 // 4) % 2]
                for r in range(4):
                    dk = d0 + r
                    P.emit("pe", lambda t, dk=dk, r=r, pb=pb, HT=HT: t.transpose(
                        pb[:, r * 128:(r + 1) * 128],
                        HT[:, dk * 128:(dk + 1) * 128], identf[:]),
                        reads=[HT.k(), identf.k()], writes=[pb.k()])
                P.emit("act", lambda a, d0=d0, pb=pb, hT=hT: a.copy(
                    hT[:, d0 * 128:(d0 + 4) * 128], pb[:, 0:512]),
                    reads=[pb.k()],
                    writes=[hT.k(d0), hT.k(d0 + 1), hT.k(d0 + 2), hT.k(d0 + 3)])
            for proj in range(3):
                pp = B[2 + proj % 2]
                for dk in range(8):
                    P.emit("pe", lambda t, dk=dk, proj=proj, pp=pp, hT=hT: t.matmul(
                        pp[:], hT[:, dk * 128:(dk + 1) * 128],
                        wqkv[:, dk * 1536 + proj * 512: dk * 1536 + (proj + 1) * 512],
                        start=(dk == 0), stop=(dk == 7)),
                        reads=[hT.k(dk), wqkv.k()], writes=[pp.k()])
                if proj == 2:
                    t0 = ti * 128
                    P.emit("act", lambda a, pp=pp, qf=qf: a.copy(qf[:], pp[:]),
                           reads=[pp.k()], writes=[qf.k()])
                    P.dma(vst[ti % 2], io["v2T"][t0:t0 + 128, :], qf[:],
                          reads=[qf.k()])
                else:
                    P.emit("act", lambda a, pp=pp, qf=qf: a.copy(qf[:], pp[:]),
                           reads=[pp.k()], writes=[qf.k()])
                    if proj == 0:
                        l2n_xpos(qf.k(), tcq, tsq, qsb, qTm, ti,
                                 qf, sq2, red, qx, te)
                    else:
                        l2n_xpos(qf.k(), tck, tsk, ksb, kTm, ti,
                                 qf, sq2, red, qx, te)
        for blk in range(4):
            P.dma(qst, io["q2T"][blk][:, m * MACRO:(m + 1) * MACRO],
                  qTm[:, blk * MACRO:(blk + 1) * MACRO], reads=[qTm.k(blk)])
            P.dma(kst, io["k2T"][blk][:, m * MACRO:(m + 1) * MACRO],
                  kTm[:, blk * MACRO:(blk + 1) * MACRO], reads=[kTm.k(blk)])
    return []


def emit_l1f_att(P, io, l):
    B = P.banks
    vsb = sb(P, [128, 32 * 512], BF16, "fvsb")
    msku = sb(P, [128, 128], F32, "fmsku")
    mskl = sb(P, [128, 128], F32, "fmskl")
    ld = P.stream("f_tconst", "sp", serial=True)
    P.dma(ld, msku[:], io["mask_up"], writes=[msku.k()])
    P.dma(ld, mskl[:], io["mask_low"], writes=[mskl.k()])
    for jt in range(32):
        P.dma(ld, vsb[:, jt * 512:(jt + 1) * 512],
              io["v2T"][jt * 128:(jt + 1) * 128, :], writes=[vsb.k()])
    qh = [sb(P, [64, T], BF16, f"fqh{b}") for b in range(2)]
    kh = [sb(P, [64, T], BF16, f"fkh{b}") for b in range(2)]
    qhst = [P.stream(f"f_tq{b}", "sp") for b in range(2)]
    khst = [P.stream(f"f_tk{b}", "sp") for b in range(2)]
    Sst = [sb(P, [128, 1024], F32, f"fS{b}") for b in range(2)]
    Pst = [sb(P, [128, 1024], F32, f"fP{b}") for b in range(2)]
    PT = [sb(P, [128, 8 * 128], BF16, f"fPT{b}") for b in range(2)]
    red = [sb(P, [128, 16], F32, f"fres{b}") for b in range(2)]
    ot = [sb(P, [128, 64], F32, f"fot{b}") for b in range(2)]
    oTf = sb(P, [128, 4 * T], BF16, "foTf")
    osst = P.stream("f_to", "sp", serial=True)

    def load_head(h):
        b = h % 2
        P.dma(qhst[b], qh[b][:], io["q2T"][h // 2][64 * (h % 2):64 * (h % 2) + 64, :],
              writes=[qh[b].k()])
        P.dma(khst[b], kh[b][:], io["k2T"][h // 2][64 * (h % 2):64 * (h % 2) + 64, :],
              writes=[kh[b].k()])

    load_head(0)
    load_head(1)
    strips = [(h, g) for h in range(8) for g in range(32)]

    def geom(g):
        w = g // 4
        j0 = max(0, (w - 1) * 4)
        jn = g + 1 - j0
        return w, j0, jn

    def p_S(i):
        h, g = strips[i]
        b = h % 2
        QH, KH = qh[b], kh[b]
        SS = Sst[i % 2]
        w, j0, jn = geom(g)
        ext = jn * 128
        nch = (ext + 511) // 512
        for jc in range(nch):
            wd = min(512, ext - jc * 512)
            pS = B[jc % 2]
            P.emit("pe", lambda t, g=g, j0=j0, jc=jc, wd=wd, pS=pS, QH=QH,
                   KH=KH: t.matmul(
                pS[:, 0:wd], QH[:, g * 128:(g + 1) * 128],
                KH[:, j0 * 128 + jc * 512: j0 * 128 + jc * 512 + wd],
                start=True, stop=True),
                reads=[QH.k(), KH.k()], writes=[pS.k()])
            P.emit("act", lambda a, jc=jc, wd=wd, pS=pS, SS=SS: a.copy(
                SS[:, jc * 512:jc * 512 + wd], pS[:, 0:wd]),
                reads=[pS.k()], writes=[SS.k(jc)])
        dg = (g - j0) * 128
        P.emit("dve", lambda v, dg=dg, SS=SS: v.tensor_add(
            SS[:, dg:dg + 128], SS[:, dg:dg + 128], msku[:]),
            reads=[SS.k(dg // 512), msku.k()], writes=[SS.k(dg // 512)])
        if w > 0:
            P.emit("dve", lambda v, SS=SS: v.tensor_add(
                SS[:, 0:128], SS[:, 0:128], mskl[:]),
                reads=[SS.k(0), mskl.k()], writes=[SS.k(0)])

    def p_soft(i):
        h, g = strips[i]
        SS, PS, RD = Sst[i % 2], Pst[i % 2], red[i % 2]
        w, j0, jn = geom(g)
        ext = jn * 128
        nch = (ext + 511) // 512
        for jc in range(nch):
            wd = min(512, ext - jc * 512)
            P.emit("dve", lambda v, jc=jc, wd=wd, SS=SS, RD=RD: v.reduce_max(
                RD[:, jc:jc + 1], SS[:, jc * 512:jc * 512 + wd], axis=AX.X),
                reads=[SS.k(jc)], writes=[RD.k(jc)])
        P.emit("dve", lambda v, nch=nch, RD=RD: v.tensor_reduce(
            RD[:, 8:9], RD[:, 0:nch], AX.X, ALU.max),
            reads=[RD.k(jc) for jc in range(nch)], writes=[RD.k("nm")])
        P.emit("dve", lambda v, RD=RD: v.tensor_scalar_mul(RD[:, 8:9],
                                                           RD[:, 8:9], -1.0),
               reads=[RD.k("nm")], writes=[RD.k("nm")])
        P.emit("act", lambda a, ext=ext, SS=SS, PS=PS, RD=RD: a.activation(
            PS[:, 0:ext], SS[:, 0:ext], AF.Exp, bias=RD[:, 8:9],
            accum_out=RD[:, 9:10]),
            reads=[RD.k("nm")] + [SS.k(jc) for jc in range(nch)],
            writes=[PS.k(), RD.k("rs")])
        P.emit("dve", lambda v, RD=RD: v.reciprocal(RD[:, 10:11], RD[:, 9:10]),
               reads=[RD.k("rs")], writes=[RD.k("rrs")])

    def p_AV(i):
        h, g = strips[i]
        PS, RD, OT = Pst[i % 2], red[i % 2], ot[i % 2]
        PTb = PT[i % 2]
        w, j0, jn = geom(g)
        for b0 in range(0, jn, 4):
            bn = min(4, jn - b0)
            pb = B[2 + (b0 // 4) % 2]
            for r in range(bn):
                jt = b0 + r
                P.emit("pe", lambda t, jt=jt, r=r, pb=pb, PS=PS: t.transpose(
                    pb[:, r * 128:(r + 1) * 128], PS[:, jt * 128:(jt + 1) * 128],
                    P.identf[:]),
                    reads=[PS.k(), P.identf.k()], writes=[pb.k()])
            P.emit("act", lambda a, b0=b0, bn=bn, pb=pb, PTb=PTb: a.copy(
                PTb[:, b0 * 128:(b0 + bn) * 128], pb[:, 0:bn * 128]),
                reads=[pb.k()], writes=[PTb.k(b0 // 4)])
        pO = B[4 + i % 2]
        for jt in range(jn):
            P.emit("pe", lambda t, jt=jt, j0=j0, h=h, pO=pO, PTb=PTb: t.matmul(
                pO[:, 0:64], PTb[:, jt * 128:(jt + 1) * 128],
                vsb[:, (j0 + jt) * 512 + h * 64: (j0 + jt) * 512 + (h + 1) * 64],
                start=(jt == 0), stop=(jt == jn - 1)),
                reads=[PTb.k(jt // 4), vsb.k()], writes=[pO.k()])
        P.emit("dve", lambda v, pO=pO, RD=RD, OT=OT: v.tensor_scalar_mul(
            OT[:], pO[:, 0:64], RD[:, 10:11]),
            reads=[pO.k(), RD.k("rrs")], writes=[OT.k()])
        pt_ = B[6 + i % 2]
        P.emit("pe", lambda t, pt_=pt_, OT=OT: t.transpose(
            pt_[0:64, 0:128], OT[:], P.identf[:]),
            reads=[OT.k(), P.identf.k()], writes=[pt_.k()])
        po_part = 64 * (h % 2)
        P.emit("act", lambda a, h=h, g=g, pt_=pt_, po_part=po_part: a.copy(
            oTf[po_part:po_part + 64, (h // 2) * T + g * 128:
                (h // 2) * T + (g + 1) * 128],
            pt_[0:64, 0:128]),
            reads=[pt_.k()], writes=[oTf.k((h, g))])
        if g == 31:
            if h + 2 < 8:
                load_head(h + 2)
            if h % 2 == 1:
                blk = h // 2
                P.dma(osst, io["o2T"][blk], oTf[:, blk * T:(blk + 1) * T],
                      reads=[oTf.k((h, g2)) for g2 in range(32)]
                      + [oTf.k((h - 1, g2)) for g2 in range(32)])

    p_S(0)
    for i in range(len(strips)):
        p_soft(i)
        if i + 1 < len(strips):
            p_S(i + 1)
        p_AV(i)
    return []


def emit_l1f_wo(P, io, l, zname, outname):
    B = P.banks
    wo = sb(P, [128, 4 * 1024], BF16, "fwo")
    ld = P.stream("f_const", "pool", serial=True)
    P.dma(ld, wo[:], io["f_wo"][l], writes=[wo.k()])
    oT = [sb(P, [128, 4 * 128], BF16, f"fwoT{b}") for b in range(2)]
    xt = [sb(P, [128, D], F32, f"fwx{b}") for b in range(2)]
    rt = [sb(P, [128, D], F32, f"fwr{b}") for b in range(2)]
    ost = [P.stream(f"f_wo{b}", "sp") for b in range(2)]
    xst = [P.stream(f"f_x{b}", "sp") for b in range(2)]
    rst = [P.stream(f"f_z{b}", "sp") for b in range(2)]
    for ti in range(32):
        t0 = ti * 128
        b = ti % 2
        OT, XT, RT = oT[b], xt[b], rt[b]
        P.dma(ost[b], OT[:].rearrange("p (blk t) -> p blk t", blk=4),
              io["o2T"][:, :, t0:t0 + 128].rearrange("blk p t -> p blk t"),
              writes=[OT.k()])
        P.dma(xst[b], XT[:], io[zname][t0:t0 + 128, :], writes=[XT.k()])
        for half in range(2):
            pp = B[2 + half]
            for blk in range(4):
                P.emit("pe", lambda t, blk=blk, half=half, pp=pp, OT=OT: t.matmul(
                    pp[:], OT[:, blk * 128:(blk + 1) * 128],
                    wo[:, blk * 1024 + half * 512: blk * 1024 + (half + 1) * 512],
                    start=(blk == 0), stop=(blk == 3)),
                    reads=[OT.k(), wo.k()], writes=[pp.k()])
            P.emit("dve", lambda v, half=half, pp=pp, RT=RT, XT=XT: v.tensor_add(
                RT[:, half * 512:(half + 1) * 512], pp[:],
                XT[:, half * 512:(half + 1) * 512]),
                reads=[pp.k(), XT.k()], writes=[RT.k()])
        P.dma(rst[b], io[outname][t0:t0 + 128, :], RT[:], reads=[RT.k()])
    return []


def emit_l1f_ffn(P, io, l, zname, outname, resname=None):
    identf, B = P.identf, P.banks
    lng = sb(P, [128, D], F32, "flng")
    lnb = sb(P, [128, D], F32, "flnb")
    b2b = sb(P, [128, D], F32, "fb2")
    b1a = sb(P, [128, 22], F32, "fb1a")
    b1g = sb(P, [128, 22], F32, "fb1g")
    ld = P.stream("f_const", "pool", serial=True)
    P.dma(ld, lng[:], io["f_lng_b"][l], writes=[lng.k()])
    P.dma(ld, lnb[:], io["f_lnb_b"][l], writes=[lnb.k()])
    P.dma(ld, b2b[:], io["f_b2_b"][l], writes=[b2b.k()])
    P.dma(ld, b1a[:], io["f_b1a"][l], writes=[b1a.k()])
    P.dma(ld, b1g[:], io["f_b1g"][l], writes=[b1g.k()])

    ht = [sb(P, [128, D], F32, f"ffh{b}") for b in range(2)]
    zt = [sb(P, [128, D], F32, f"ffz{b}") for b in range(2)]
    hn = sb(P, [128, D], F32, "ffhn")
    sqt = sb(P, [128, D], F32, "ffsq")
    stats = sb(P, [128, 8], F32, "ffst")
    hTm = sb(P, [128, 8 * MACRO], BF16, "ffhTm")
    hst = [[P.stream(f"f_h{a}{b}", "sp") for b in range(2)] for a in range(2)]
    zst = [P.stream(f"f_x{b}", "sp") for b in range(2)]
    r3st = [P.stream(f"f_r3{b}", "sp") for b in range(2)]
    ist = [P.stream(f"f_oi{b}", "sp") for b in range(2)]
    hTst = P.stream("f_hT", "sp", serial=True)
    for m in range(NMAC):
        for tt in range(4):
            ti = m * 4 + tt
            t0 = ti * 128
            b = ti % 2
            HT, ZT = ht[b], zt[b]
            shift_load(P, io, zname, HT, ti, hst[b], HT.k())
            P.dma(zst[b], ZT[:], io[zname][t0:t0 + 128, :], writes=[ZT.k()])
            layernorm_tile(P, hn[:], HT[:], sqt, stats, [HT.k()], hn.k(),
                           gamma=lng, beta=lnb)
            for d0 in range(0, 8, 4):
                pb = B[(d0 // 4) % 2]
                for r in range(4):
                    dk = d0 + r
                    P.emit("pe", lambda t, dk=dk, r=r, pb=pb: t.transpose(
                        pb[:, r * 128:(r + 1) * 128],
                        hn[:, dk * 128:(dk + 1) * 128], identf[:]),
                        reads=[hn.k(), identf.k()], writes=[pb.k()])
                P.emit("act", lambda a, d0=d0, tt=tt, pb=pb: a.copy(
                    hTm[:].rearrange("p (dk t) -> p dk t", dk=8)
                    [:, d0:d0 + 4, tt * 128:(tt + 1) * 128],
                    pb[:].rearrange("p (dk t) -> p dk t", dk=4)),
                    reads=[pb.k()],
                    writes=[hTm.k(d0), hTm.k(d0 + 1), hTm.k(d0 + 2),
                            hTm.k(d0 + 3)])
            P.emit("dve", lambda v, ZT=ZT: v.tensor_add(ZT[:], ZT[:], b2b[:]),
                   reads=[ZT.k(), b2b.k()], writes=[ZT.k()])
            if resname is not None:
                RT = sqt
                P.dma(r3st[b], RT[:], io[resname][t0:t0 + 128, :],
                      writes=[RT.k()])
                P.emit("dve", lambda v, ZT=ZT, RT=RT: v.tensor_add(
                    ZT[:], ZT[:], RT[:]),
                    reads=[ZT.k(), RT.k()], writes=[ZT.k()])
            P.dma(ist[b], io[outname][t0:t0 + 128, :], ZT[:],
                  reads=[ZT.k()], writes=[("hbm_out", ti // 4, "init", tt)])
        P.dma(hTst, io["h2T"][m], hTm[:],
              reads=[hTm.k(dk) for dk in range(8)],
              writes=[("hbm_h2T", m)])

    w1r = [sb(P, [128, 6 * 8 * 128], BF16, f"fw1r{b}") for b in range(2)]
    w3r = [sb(P, [128, 6 * 8 * 128], BF16, f"fw3r{b}") for b in range(2)]
    w2r = [sb(P, [128, 6 * 1024], BF16, f"fw2r{b}") for b in range(2)]
    wst = [[P.stream(f"f_w{w}{b}", "pool") for b in range(2)] for w in range(3)]
    xbuf = [sb(P, [128, 8 * MACRO], BF16, f"ffxb{b}") for b in range(2)]
    xbst = [P.stream(f"f_xb{b}", "sp") for b in range(2)]
    hTc = [sb(P, [128, 6 * MACRO], BF16, f"ffhc{b}") for b in range(2)]
    delta = [sb(P, [128, 4 * D], F32, f"ffd{b}") for b in range(2)]
    dst_ = [P.stream(f"f_acc{b}", "pool") for b in range(2)]
    gel_ = [sb(P, [128, MACRO], F32, f"ffgel{b}") for b in range(2)]

    chunk_seq = [0]

    def load_chunk_next():
        seq = chunk_seq[0]
        if seq >= 4:
            return
        chunk_seq[0] += 1
        c0, cn = MOE_CHUNKS[seq]
        sl = seq % 2
        P.dma(wst[0][sl], w1r[sl][:, 0:cn * 8 * 128],
              io["f_w1a"][l][:, c0 * 8 * 128:(c0 + cn) * 8 * 128],
              writes=[w1r[sl].k()])
        P.dma(wst[1][sl], w3r[sl][:, 0:cn * 8 * 128],
              io["f_w1g"][l][:, c0 * 8 * 128:(c0 + cn) * 8 * 128],
              writes=[w3r[sl].k()])
        P.dma(wst[2][sl], w2r[sl][:, 0:cn * 1024],
              io["f_w2"][l][:, c0 * 1024:(c0 + cn) * 1024],
              writes=[w2r[sl].k()])

    load_chunk_next()
    load_chunk_next()
    iters = [(ci, m) for ci in range(4) for m in range(NMAC)]

    def ab_h(i):
        ci, m = iters[i]
        c0, cn = MOE_CHUNKS[ci]
        sl = ci % 2
        W1, W3 = w1r[sl], w3r[sl]
        xb = xbuf[i % 2]
        P.dma(xbst[i % 2], xb[:], io["h2T"][m],
              reads=[("hbm_h2T", m)], writes=[xb.k()])
        HT = hTc[i % 2]
        for hh in range(cn):
            pa, pg = B[hh % 2], B[2 + hh % 2]
            for k in range(8):
                P.emit("pe", lambda t, hh=hh, k=k, pa=pa, W1=W1, xb=xb: t.matmul(
                    pa[:], W1[:, (hh * 8 + k) * 128:(hh * 8 + k + 1) * 128],
                    xb[:, k * MACRO:(k + 1) * MACRO],
                    start=(k == 0), stop=(k == 7)),
                    reads=[W1.k(), xb.k()], writes=[pa.k()])
            for k in range(8):
                P.emit("pe", lambda t, hh=hh, k=k, pg=pg, W3=W3, xb=xb: t.matmul(
                    pg[:], W3[:, (hh * 8 + k) * 128:(hh * 8 + k + 1) * 128],
                    xb[:, k * MACRO:(k + 1) * MACRO],
                    start=(k == 0), stop=(k == 7)),
                    reads=[W3.k(), xb.k()], writes=[pg.k()])
            gel = gel_[hh % 2]
            P.emit("act", lambda a, hh=hh, ci=ci, c0=c0, pg=pg, gel=gel:
                   a.activation(
                gel[:], pg[:], AF.Gelu, bias=b1g[:, c0 + hh:c0 + hh + 1]),
                reads=[pg.k(), b1g.k()], writes=[gel.k()])
            P.emit("dve", lambda v, hh=hh, ci=ci, c0=c0, pa=pa, HT=HT, gel=gel:
                   v.scalar_tensor_tensor(
                HT[:, hh * MACRO:(hh + 1) * MACRO], pa[:],
                b1a[:, c0 + hh:c0 + hh + 1], gel[:], ALU.add, ALU.mult),
                reads=[pa.k(), b1a.k(), gel.k()], writes=[HT.k(hh)])

    def w2d(i):
        ci, m = iters[i]
        c0, cn = MOE_CHUNKS[ci]
        sl = ci % 2
        W2 = w2r[sl]
        HT = hTc[i % 2]
        DL = delta[i % 2]
        for tt in range(4):
            for dh in range(2):
                po = B[4 + (tt * 2 + dh) % 2]
                for k in range(cn):
                    P.emit("pe", lambda t, tt=tt, dh=dh, k=k, po=po, W2=W2,
                           HT=HT: t.matmul(
                        po[:], HT[:, k * MACRO + tt * 128: k * MACRO + (tt + 1) * 128],
                        W2[:, k * 1024 + dh * 512:k * 1024 + (dh + 1) * 512],
                        start=(k == 0), stop=(k == cn - 1)),
                        reads=[HT.k(k), W2.k()], writes=[po.k()])
                if (tt * 2 + dh) % 2 == 0:
                    P.emit("dve", lambda v, tt=tt, dh=dh, po=po, DL=DL:
                           v.tensor_copy(
                        DL[:, tt * D + dh * 512: tt * D + (dh + 1) * 512], po[:]),
                        reads=[po.k()], writes=[DL.k()])
                else:
                    P.emit("act", lambda a, tt=tt, dh=dh, po=po, DL=DL: a.copy(
                        DL[:, tt * D + dh * 512: tt * D + (dh + 1) * 512], po[:]),
                        reads=[po.k()], writes=[DL.k()])
        t0 = m * MACRO
        P.dma(dst_[i % 2], io[outname][t0:t0 + MACRO, :]
              .rearrange("(tt p) f -> p tt f", p=128),
              DL[:].rearrange("p (tt f) -> p tt f", tt=4),
              reads=[DL.k()] + [("hbm_out", m, "init", t) for t in range(4)],
              writes=[("hbm_out", m)], accum_op=ALU.add)
        if m == NMAC - 1:
            load_chunk_next()

    ab_h(0)
    for i in range(len(iters)):
        if i + 1 < len(iters):
            ab_h(i + 1)
        w2d(i)
    return ["f_acc0", "f_acc1", "f_oi0", "f_oi1"]


STAGES = [
    ("l0a", emit_l0a,
     ["xin", "ident", "inw_pm", "cmw_pm", "outw_pm", "inb_col", "dwb_col",
      "mixb_col", "ld_col", "outb_b", "l0g_b", "l0b_b"],
     ["res1"]),
    ("moe", emit_moe,
     ["res1", "ident", "gateT_pm", "gateb_b", "moe_w1", "moe_w3", "moe_w2"],
     ["res2", "xlnT"]),
    ("l1a_pre", emit_l1a_pre,
     ["res2", "ident", "wqT_pm", "wkT_pm", "wvT_pm", "bq_b", "bk_b", "bv_b",
      "l1scale", "l1a_cos", "l1a_sin"],
     ["qT", "kT", "vT1", "attM"]),
    ("l1a_att", emit_l1a_att,
     ["qT", "kT", "vT1", "attM", "ident", "mask_low"],
     ["oT2"]),
    ("l1a_wo", emit_l1a_wo,
     ["oT2", "res2", "ident", "woT_pm", "bo_b"],
     ["res3"]),
    ("l1f_pre", emit_l1f_pre, ["res3"], ["za"]),
    ("l1f_qkv0", lambda P, io: emit_l1f_qkv(P, io, 0, "za"),
     ["za", "ident", "f_wqkv", "f_qs_b", "f_ks_b",
      "xpos_cq", "xpos_sq", "xpos_ck", "xpos_sk"],
     ["q2T", "k2T", "v2T"]),
    ("l1f_att0", lambda P, io: emit_l1f_att(P, io, 0),
     ["q2T", "k2T", "v2T", "ident", "mask_up", "mask_low"], ["o2T"]),
    ("l1f_wo0", lambda P, io: emit_l1f_wo(P, io, 0, "za", "zb"),
     ["o2T", "za", "ident", "f_wo"], ["zb"]),
    ("l1f_ffn0", lambda P, io: emit_l1f_ffn(P, io, 0, "zb", "zc"),
     ["zb", "ident", "f_lng_b", "f_lnb_b", "f_b2_b", "f_b1a", "f_b1g",
      "f_w1a", "f_w1g", "f_w2"],
     ["zc", "h2T"]),
    ("l1f_qkv1", lambda P, io: emit_l1f_qkv(P, io, 1, "zc"),
     ["zc", "ident", "f_wqkv", "f_qs_b", "f_ks_b",
      "xpos_cq", "xpos_sq", "xpos_ck", "xpos_sk"],
     ["q2T", "k2T", "v2T"]),
    ("l1f_att1", lambda P, io: emit_l1f_att(P, io, 1),
     ["q2T", "k2T", "v2T", "ident", "mask_up", "mask_low"], ["o2T"]),
    ("l1f_wo1", lambda P, io: emit_l1f_wo(P, io, 1, "zc", "zb"),
     ["o2T", "zc", "ident", "f_wo"], ["zb"]),
    ("l1f_ffn1", lambda P, io: emit_l1f_ffn(P, io, 1, "zb", "out",
                                            resname="res3"),
     ["zb", "res3", "ident", "f_lng_b", "f_lnb_b", "f_b2_b", "f_b1a", "f_b1g",
      "f_w1a", "f_w1g", "f_w2"],
     ["out", "h2T"]),
]

FINAL_OUT = "out"


def build_group(stages, ext_in, ext_out):
    nc = bass.Bass("TRN2", target_bir_lowering=False)
    with contextlib.ExitStack() as stack:
        io = {}
        names = {"ident"}
        for (_, _, rd, wr) in stages:
            names |= set(rd) | set(wr)
        for n in sorted(names):
            if n == "xin":
                shape, dt = [T, D], F32
            elif n in HOST_SPECS:
                shape, dt = HOST_SPECS[n], F32
            else:
                shp, dts = SCRATCH_SPECS[n]
                shape, dt = shp, DT[dts]
            kind = "Internal"
            if n in ext_in:
                kind = "ExternalInput"
            elif n in ext_out:
                kind = "ExternalOutput"
            io[n] = nc.dram_tensor(n, list(shape), dt, kind=kind).ap()
        with nc.Block() as block:
            P = Prog(nc, stack)

            def body(_):
                nc.gpsimd.dma_reset(range(153, 256))
                nc.gpsimd.sem_clear(range(153, 256))
                nc.all_engine_barrier()
                prelude(P, io)
                out_streams = []
                for i, (nm, fn, rd, wr) in enumerate(stages):
                    if i:
                        P.barrier()
                    with contextlib.ExitStack() as sc:
                        P.scope = sc
                        out_streams += fn(P, io)
                    P.scope = stack
                P.finish(out_streams)

            block.sync(body)
    return nc


def run_groups(groups, inputs, n_cores=8):
    host = prep(inputs)
    xin = np.ascontiguousarray(np.asarray(inputs["L0_attn_x"], dtype=np.float32))
    state = [dict() for _ in range(n_cores)]
    final = None
    for gi, group in enumerate(groups):
        produced, consumed = set(), set()
        for (_, _, rd, wr) in group:
            consumed |= set(rd) - produced
            produced |= set(wr)
        later_needed = set()
        for g2 in groups[gi + 1:]:
            for (_, _, rd, _w) in g2:
                later_needed |= set(rd)
        is_last = gi == len(groups) - 1
        ext_out = produced & later_needed
        if is_last:
            ext_out |= {FINAL_OUT}
        ext_in = consumed - produced
        nc = build_group(group, ext_in, ext_out)
        in_maps = []
        for c in range(n_cores):
            m = {}
            for n in ext_in:
                if n == "xin":
                    m[n] = xin[c]
                elif n in HOST_SPECS:
                    m[n] = host[n]
                else:
                    m[n] = state[c][n]
            in_maps.append(m)
        res = run_bass_kernel_spmd(nc, in_maps, core_ids=list(range(n_cores)))
        for c in range(n_cores):
            for n, arr in res.results[c].items():
                state[c][n] = np.asarray(arr)
        if is_last:
            final = [state[c][FINAL_OUT] for c in range(n_cores)]
    return np.stack(final)


FUSED = True


def kernel(**inputs):
    groups = [list(STAGES)] if FUSED else [[s] for s in STAGES]
    return run_groups(groups, inputs)
